# version 18 (final)
# baseline (speedup 1.0000x reference)
import numpy as np
from contextlib import ExitStack

import concourse.bass as bass
import concourse.mybir as mybir
from concourse.bass_utils import run_bass_kernel_spmd

F32 = mybir.dt.float32
BF16 = mybir.dt.bfloat16
AF = mybir.ActivationFunctionType
ALU = mybir.AluOpType
AX = mybir.AxisListType

D = 1024
DC = D // 128
LN_EPS = 1e-5

ENG2SEM = {"tensor": "T", "vector": "V", "scalar": "S", "gpsimd": "P"}


class Ctx:
    ENGINES = ("tensor", "vector", "scalar", "gpsimd", "sync")

    def __init__(self, nc, es):
        self.nc = nc
        self.es = es
        self.q = {e: [] for e in self.ENGINES}
        self._sems = {}
        self.cnt = {}

    def sem(self, name):
        if name not in self._sems:
            self._sems[name] = self.es.enter_context(self.nc.semaphore(name))
            self.cnt[name] = 0
        return self._sems[name]

    def emit(self, engine, fn, inc=None, amt=1):
        pend = getattr(self, "_pend", None)
        if pend is None:
            pend = self._pend = {e: [] for e in self.ENGINES}
        waits = pend[engine]
        pend[engine] = []
        sems = self._sems
        for s, v in waits[:-1]:
            self.q[engine].append(lambda eng, s=s, v=v: eng.wait_ge(sems[s], v))
        att = waits[-1] if waits else None
        if inc is not None:
            self.sem(inc)
            self.cnt[inc] += amt

            def wrapped(eng, fn=fn, s=inc, a=amt, att=att):
                inst = fn(eng)
                if att is not None:
                    inst._wait_ge(sems[att[0]], att[1])
                inst.then_inc(sems[s], a)
            self.q[engine].append(wrapped)
        else:
            def wrapped2(eng, fn=fn, att=att):
                inst = fn(eng)
                if att is not None and inst is not None:
                    inst._wait_ge(sems[att[0]], att[1])
            self.q[engine].append(wrapped2)

    def wait(self, engine, semname, val=None):
        self.sem(semname)
        v = self.cnt[semname] if val is None else val
        if v <= 0:
            return
        hwm = getattr(self, "_hwm", None)
        if hwm is None:
            hwm = self._hwm = {e: {} for e in self.ENGINES}
        if hwm[engine].get(semname, -1) >= v:
            return
        hwm[engine][semname] = v
        pend = getattr(self, "_pend", None)
        if pend is None:
            pend = self._pend = {e: [] for e in self.ENGINES}
        pend[engine].append((semname, v))

    def wait_marks(self, engine, marks):
        for s, v in marks.items():
            if engine == "tensor" and s == "T":
                continue
            self.wait(engine, s, v)

    def val(self, semname):
        self.sem(semname)
        return self.cnt[semname]

    def op(self, engine, fn, inc=True):
        self.emit(engine, fn, inc=(ENG2SEM[engine] if inc else None))

    def mark(self, engine):
        return {ENG2SEM[engine]: self.val(ENG2SEM[engine])}

    def sb(self, name, shape, dtype):
        return self.es.enter_context(self.nc.sbuf_tensor(name, shape, dtype))

    def ps(self, name, shape, dtype=F32):
        return self.es.enter_context(self.nc.psum_tensor(name, shape, dtype))

    def push_scope(self):
        self._scopes = getattr(self, "_scopes", [])
        self._scopes.append(self.es)
        self.es = ExitStack()

    def pop_scope(self):
        self.es.close()
        self.es = self._scopes.pop()

    def barrier(self):
        marks = dict(self.cnt)
        for eng in ("tensor", "vector", "scalar", "sync", "gpsimd"):
            if not self.q[eng] and eng != "sync":
                continue
            self.wait_marks(eng, marks)

    def finish(self, block):
        pend = getattr(self, "_pend", {})
        sems = self._sems
        for e, waits in pend.items():
            for s, v in waits:
                self.q[e].append(lambda eng, s=s, v=v: eng.wait_ge(sems[s], v))
            pend[e] = []
        for e in self.ENGINES:
            if not self.q[e]:
                continue
            ops = self.q[e]

            def body(eng, ops=ops):
                for f in ops:
                    f(eng)
            getattr(block, e)(body)


class RBuf:

    def __init__(self, n=2):
        self.n = n
        self.marks = [dict() for _ in range(n)]

    def record(self, idx, mark):
        d = self.marks[idx % self.n]
        for s, v in mark.items():
            d[s] = max(d.get(s, 0), v)

    def gate(self, cx, engine, idx):
        cx.wait_marks(engine, self.marks[idx % self.n])
        self.marks[idx % self.n] = {}


class Stream:

    def __init__(self, cx, name, nbuf=2, alt=False):
        self.cx = cx
        self.nbuf = nbuf
        self.alt = alt
        self.names = [f"{name}{i}" for i in range(nbuf)]
        for n in self.names:
            cx.sem(n)
        self.counts = [0] * nbuf

    def queue(self, idx):
        par = idx % self.nbuf
        return ("sync" if par % 2 == 0 else "gpsimd") if self.alt else "sync"

    def dma(self, idx, out, in_, engine=None):
        par = idx % self.nbuf
        if engine is None:
            engine = self.queue(idx)
        self.counts[par] += 1
        self.cx.emit(engine,
                     lambda eng, o=out, i=in_: eng.dma_start(out=o, in_=i),
                     inc=self.names[par], amt=16)

    def wait(self, engine, idx):
        par = idx % self.nbuf
        self.cx.wait(engine, self.names[par], 16 * self.counts[par])

    def mark(self, idx):
        par = idx % self.nbuf
        return {self.names[par]: 16 * self.counts[par]}

    def drain(self, engine="sync"):
        for i, n in enumerate(self.names):
            self.cx.wait(engine, n, 16 * self.counts[i])


def ln_stats_combine(cx, bank_sum, bank_sq, mu_out, rstd_out, tmp1, tmp2, inv_n,
                     eps_col=None):
    if eps_col is None:
        eps_col = cx.eps_col
    cx.op("vector", lambda e: e.tensor_scalar_mul(out=mu_out, in0=bank_sum, scalar1=inv_n))
    cx.wait("vector", "V")
    cx.op("vector", lambda e: e.tensor_mul(out=tmp1, in0=mu_out, in1=mu_out))
    cx.wait("vector", "V")
    cx.op("vector", lambda e: e.scalar_tensor_tensor(
        out=tmp2, in0=bank_sq, scalar=inv_n, in1=tmp1,
        op0=ALU.mult, op1=ALU.subtract))
    cx.wait("scalar", "V")
    cx.op("scalar", lambda e: e.activation(out=tmp1, in_=tmp2, func=AF.Ln, bias=eps_col))
    cx.wait("scalar", "S")
    cx.op("scalar", lambda e: e.activation(out=rstd_out, in_=tmp1, func=AF.Exp, scale=-0.5))


def emit_l1_ffn(cx, T, xT, w1, w2, b1c, b2c, yT, ones, pb):
    NT = T // 1024
    F = 4096
    NF = F // 128

    xf = cx.sb("ff_xf", [128, 2 * 512], F32)
    xh = cx.sb("ff_xh", [128, DC * 1024], BF16)
    sq = cx.sb("ff_sq", [128, 8 * 512], BF16)
    xn = cx.sb("ff_xn", [128, DC * 1024], BF16)
    h = cx.sb("ff_h", [128, NF * 1024], BF16)
    zt = cx.sb("ff_zt", [128, DC * 1024], BF16)
    w1b = cx.sb("ff_w1b", [128, 4 * 2048], BF16)
    w2b = cx.sb("ff_w2b", [128, 4 * 2048], BF16)
    mu_s = cx.sb("ff_mu", [128, 2 * 512], F32)
    rstd_s = cx.sb("ff_rstd", [128, 2 * 512], F32)
    tmp1 = cx.sb("ff_tmp1", [128, 512], F32)
    tmp2 = cx.sb("ff_tmp2", [128, 512], F32)
    ftmp = cx.sb("ff_ftmp", [128, 512], F32)
    ftmp2 = cx.sb("ff_ftmp2", [128, 512], F32)
    outf = cx.sb("ff_outf", [128, 2 * 512], F32)

    sx = Stream(cx, "ffx")
    sw1 = Stream(cx, "ffw1", nbuf=4, alt=True)
    sw2 = Stream(cx, "ffw2", nbuf=4, alt=True)
    so = Stream(cx, "ffo")

    r_xf = RBuf(2)
    r_sq = RBuf(2)
    r_w1f = RBuf(2)
    r_w1b = RBuf(2)
    r_w2f = RBuf(2)
    r_w2b = RBuf(2)
    r_outf = RBuf(2)
    r_banks = [RBuf(1) for _ in range(8)]
    r_xh = RBuf(1)
    r_xn = RBuf(1)
    r_h = RBuf(1)
    r_zt = RBuf(1)
    r_stat = RBuf(1)

    xload = w1load = w2load = eload = ostore = 0

    for j in range(NT):
        t0 = j * 1024
        r_xh.gate(cx, "vector", 0)
        for sub in range(2):
            ts = t0 + sub * 512
            for c in range(DC):
                par = xload % 2
                r_xf.gate(cx, "sync", xload)
                sx.dma(xload, xf[:, par * 512:(par + 1) * 512],
                       xT[c * 128:(c + 1) * 128, ts:ts + 512])
                sx.wait("vector", xload)
                col = c * 1024 + sub * 512
                cx.op("vector", lambda e, col=col, par=par:
                      e.tensor_copy(out=xh[:, col:col + 512],
                                    in_=xf[:, par * 512:(par + 1) * 512]))
                r_xf.record(xload, cx.mark("vector"))
                vcast = cx.val("V")
                cx.wait("scalar", "V", vcast)
                if c == 0:
                    r_sq.gate(cx, "scalar", 0)
                cx.op("scalar", lambda e, col=col, c=c:
                      e.activation(out=sq[:, c * 512:(c + 1) * 512],
                                   in_=xh[:, col:col + 512], func=AF.Square))
                scast = cx.val("S")
                cx.wait("tensor", "V", vcast)
                if c == 0:
                    r_banks[0].gate(cx, "tensor", 0)
                    r_banks[1].gate(cx, "tensor", 0)
                st, sp = (c == 0), (c == DC - 1)
                cx.op("tensor", lambda e, col=col, st=st, sp=sp: e.matmul(
                    out=pb[0][:], lhsT=ones[:], rhs=xh[:, col:col + 512],
                    start=st, stop=sp), inc=sp)
                cx.wait("tensor", "S", scast)
                cx.op("tensor", lambda e, c=c, st=st, sp=sp: e.matmul(
                    out=pb[1][:], lhsT=ones[:], rhs=sq[:, c * 512:(c + 1) * 512],
                    start=st, stop=sp), inc=sp)
                if sp:
                    r_sq.record(0, cx.mark("tensor"))
                xload += 1
            r_stat.gate(cx, "vector", 0)
            cx.wait("vector", "T")
            ln_stats_combine(cx, pb[0][:], pb[1][:],
                             mu_s[:, sub * 512:(sub + 1) * 512],
                             rstd_s[:, sub * 512:(sub + 1) * 512],
                             tmp1[:], tmp2[:], 1.0 / D)
            r_banks[0].record(0, cx.mark("vector"))
            r_banks[1].record(0, cx.mark("vector"))
            rstd_done = cx.val("S")
            cx.wait("vector", "S", rstd_done)
            for c in range(DC):
                col = c * 1024 + sub * 512
                cx.op("vector", lambda e, col=col, sub=sub: e.tensor_tensor(
                    out=xn[:, col:col + 512], in0=xh[:, col:col + 512],
                    in1=mu_s[:, sub * 512:(sub + 1) * 512], op=ALU.subtract))
                cx.wait("vector", "V")
                cx.op("vector", lambda e, col=col, sub=sub: e.tensor_tensor(
                    out=xn[:, col:col + 512], in0=xn[:, col:col + 512],
                    in1=rstd_s[:, sub * 512:(sub + 1) * 512], op=ALU.mult))
            r_stat.record(0, {"V": cx.val("V"), "S": cx.val("S")})
        xn_done = cx.val("V")
        r_xh.record(0, {"V": xn_done, "T": cx.val("T")})

        r_xn.gate(cx, "tensor", 0)
        for f in range(NF):
            if f % 2 == 0:
                par = w1load % 4
                q = sw1.queue(w1load)
                r_w1b.gate(cx, q, w1load)
                sw1.dma(w1load, w1b[:, par * 2048:(par + 1) * 2048],
                        w1[:, f * 1024:(f + 2) * 1024])
                sw1.wait("tensor", w1load)
                pw = (par, w1load)
                w1load += 1
                cx.wait("tensor", "V", xn_done)
            off = pw[0] * 2048 + (f % 2) * 1024
            for half in range(2):
                bank = pb[2 + half]
                r_banks[2 + half].gate(cx, "tensor", 0)
                for k in range(DC):
                    cx.op("tensor", lambda e, off=off, k=k, half=half, bank=bank:
                          e.matmul(out=bank[:],
                                   lhsT=w1b[:, off + k * 128:off + (k + 1) * 128],
                                   rhs=xn[:, k * 1024 + half * 512:k * 1024 + half * 512 + 512],
                                   start=(k == 0), stop=(k == DC - 1)),
                          inc=(k == DC - 1))
                cx.wait("scalar", "T")
                cx.op("scalar", lambda e, f=f, half=half, bank=bank: e.activation(
                    out=h[:, f * 1024 + half * 512:f * 1024 + half * 512 + 512],
                    in_=bank[:], func=AF.Gelu, bias=b1c[:, f:f + 1]))
                r_banks[2 + half].record(0, cx.mark("scalar"))
            if f % 2 == 1:
                r_w1b.record(pw[1], cx.mark("tensor"))
        h_done = cx.val("S")
        r_xn.record(0, cx.mark("tensor"))

        for sub in range(2):
            ts = t0 + sub * 512
            r_zt.gate(cx, "scalar", 0)
            for ef in range(8):
                slots = []
                for hh in range(2):
                    par = w2load % 4
                    q = sw2.queue(w2load)
                    r_w2b.gate(cx, q, w2load)
                    sw2.dma(w2load, w2b[:, par * 2048:(par + 1) * 2048],
                            w2[:, (ef * 2 + hh) * 2048:(ef * 2 + hh + 1) * 2048])
                    sw2.wait("tensor", w2load)
                    slots.append((par * 2048, w2load))
                    w2load += 1
                if ef == 0:
                    cx.wait("tensor", "S", h_done)
                bank = pb[2 + ef % 2]
                r_banks[2 + ef % 2].gate(cx, "tensor", 0)
                for k2 in range(NF):
                    slot = slots[k2 // 16][0] + (k2 % 16) * 128
                    cx.op("tensor", lambda e, slot=slot, k2=k2, sub=sub, bank=bank:
                          e.matmul(out=bank[:],
                                   lhsT=w2b[:, slot:slot + 128],
                                   rhs=h[:, k2 * 1024 + sub * 512:k2 * 1024 + sub * 512 + 512],
                                   start=(k2 == 0), stop=(k2 == NF - 1)),
                          inc=(k2 == NF - 1))
                r_w2b.record(slots[0][1], cx.mark("tensor"))
                r_w2b.record(slots[1][1], cx.mark("tensor"))
                cx.wait("scalar", "T")
                cx.op("scalar", lambda e, ef=ef, sub=sub, bank=bank: e.activation(
                    out=zt[:, ef * 1024 + sub * 512:ef * 1024 + sub * 512 + 512],
                    in_=bank[:], func=AF.Identity, bias=b2c[:, ef:ef + 1]))
                r_banks[2 + ef % 2].record(0, cx.mark("scalar"))
            zt_done = cx.val("S")
            for e_ in range(8):
                col = e_ * 1024 + sub * 512
                if e_ == 0:
                    r_sq.gate(cx, "scalar", 0)
                    cx.wait("scalar", "S", zt_done)
                cx.op("scalar", lambda e, col=col, e_=e_: e.activation(
                    out=sq[:, e_ * 512:(e_ + 1) * 512], in_=zt[:, col:col + 512],
                    func=AF.Square))
                scast = cx.val("S")
                if e_ == 0:
                    cx.wait("tensor", "S", zt_done)
                    r_banks[0].gate(cx, "tensor", 0)
                    r_banks[1].gate(cx, "tensor", 0)
                st, sp = (e_ == 0), (e_ == 7)
                cx.op("tensor", lambda e, col=col, st=st, sp=sp: e.matmul(
                    out=pb[0][:], lhsT=ones[:], rhs=zt[:, col:col + 512],
                    start=st, stop=sp), inc=sp)
                cx.wait("tensor", "S", scast)
                cx.op("tensor", lambda e, e_=e_, st=st, sp=sp: e.matmul(
                    out=pb[1][:], lhsT=ones[:], rhs=sq[:, e_ * 512:(e_ + 1) * 512],
                    start=st, stop=sp), inc=sp)
                if sp:
                    r_sq.record(0, cx.mark("tensor"))
            r_stat.gate(cx, "vector", 0)
            cx.wait("vector", "T")
            ln_stats_combine(cx, pb[0][:], pb[1][:],
                             mu_s[:, sub * 512:(sub + 1) * 512],
                             rstd_s[:, sub * 512:(sub + 1) * 512],
                             tmp1[:], tmp2[:], 1.0 / D)
            r_banks[0].record(0, cx.mark("vector"))
            r_banks[1].record(0, cx.mark("vector"))
            rstd2_done = cx.val("S")
            cx.wait("vector", "S", rstd2_done)
            for c in range(DC):
                par = (xload + eload) % 2
                r_xf.gate(cx, "sync", xload + eload)
                sx.dma(xload + eload, xf[:, par * 512:(par + 1) * 512],
                       xT[c * 128:(c + 1) * 128, ts:ts + 512])
                col = c * 1024 + sub * 512
                cx.op("vector", lambda e, col=col, sub=sub: e.tensor_tensor(
                    out=ftmp[:], in0=zt[:, col:col + 512],
                    in1=mu_s[:, sub * 512:(sub + 1) * 512], op=ALU.subtract))
                cx.wait("vector", "V")
                cx.op("vector", lambda e, sub=sub: e.tensor_tensor(
                    out=ftmp2[:], in0=ftmp[:],
                    in1=rstd_s[:, sub * 512:(sub + 1) * 512], op=ALU.mult))
                sx.wait("vector", xload + eload)
                opar = ostore % 2
                r_outf.gate(cx, "vector", ostore)
                cx.wait("vector", "V")
                cx.op("vector", lambda e, par=par, opar=opar: e.tensor_tensor(
                    out=outf[:, opar * 512:(opar + 1) * 512], in0=ftmp2[:],
                    in1=xf[:, par * 512:(par + 1) * 512], op=ALU.add))
                r_xf.record(xload + eload, cx.mark("vector"))
                vadd = cx.val("V")
                cx.wait("sync", "V", vadd)
                so.dma(ostore, yT[c * 128:(c + 1) * 128, ts:ts + 512],
                       outf[:, opar * 512:(opar + 1) * 512])
                r_outf.record(ostore, so.mark(ostore))
                eload += 1
                ostore += 1
            r_zt.record(0, {"V": cx.val("V"), "T": cx.val("T"), "S": cx.val("S")})
            r_stat.record(0, {"V": cx.val("V"), "S": cx.val("S")})
    so.drain()


def build_l1_ffn(T=4096):
    nc = bass.Bass("TRN2", target_bir_lowering=False)
    xT = nc.dram_tensor("xT", [D, T], F32, kind="ExternalInput").ap()
    w1 = nc.dram_tensor("w1", [128, 32 * 1024], F32, kind="ExternalInput").ap()
    w2 = nc.dram_tensor("w2", [128, 16 * 2048], F32, kind="ExternalInput").ap()
    b1l = nc.dram_tensor("b1l", [128, 32], F32, kind="ExternalInput").ap()
    b2l = nc.dram_tensor("b2l", [128, 8], F32, kind="ExternalInput").ap()
    yT = nc.dram_tensor("yT", [D, T], F32, kind="ExternalOutput").ap()

    es = ExitStack()
    cx = Ctx(nc, es)
    ones = cx.sb("ones", [128, 128], BF16)
    b1c = cx.sb("b1c", [128, 32], F32)
    b2c = cx.sb("b2c", [128, 8], F32)
    pb = [cx.ps(f"pb{i}", [128, 512]) for i in range(8)]

    epsc = cx.sb("epsc", [128, 1], F32)
    cx.eps_col = epsc[:, 0:1]
    sc = Stream(cx, "const")
    sc.dma(0, b1c[:], b1l)
    sc.dma(1, b2c[:], b2l)
    cx.op("vector", lambda e: e.memset(ones[:], 1.0))
    cx.op("vector", lambda e: e.memset(epsc[:], LN_EPS))
    sc.wait("scalar", 0)
    sc.wait("scalar", 1)
    ones_v = cx.val("V")
    cx.wait("tensor", "V", ones_v)
    cx.wait("scalar", "V", ones_v)

    emit_l1_ffn(cx, T, xT, w1, w2, b1c, b2c, yT, ones, pb)

    with es, nc.Block() as block:
        cx.finish(block)
    return nc


def _layout_bias(b):
    k = b.shape[0] // 128
    return np.ascontiguousarray(b.reshape(k, 128).T)




RET_H = 4
RET_DK = 256
RET_DV = 512
GAMMAS = [1.0 - 2.0 ** (-5.0 - h) for h in range(RET_H)]


def emit_l1_attn(cx, T, xT, Wq, Wk, Wv, Wg, Wo, yT, consts):
    ones = consts["ones"]
    NTC = T // 512

    xf = cx.sb("ra_xf", [128, 2 * 512], F32)
    xh = cx.sb("ra_xh", [128, DC * 512], BF16)
    sq = cx.sb("ra_sq", [128, 8 * 512], BF16)
    xn = cx.sb("ra_xn", [128, DC * 512], BF16)
    qT = cx.sb("ra_qT", [128, 8 * 512], BF16)
    kT = cx.sb("ra_kT", [128, 8 * 512], BF16)
    vT = cx.sb("ra_vT", [128, 16 * 512], BF16)
    gT = cx.sb("ra_gT", [128, 16 * 512], BF16)
    ob = cx.sb("ra_ob", [128, 16 * 512], BF16)
    At = cx.sb("ra_A", [128, RET_H * 2 * 512], F32)
    ktok = cx.sb("ra_ktok", [128, 2 * 256], BF16)
    vtok = cx.sb("ra_vtok", [128, 2 * 512], BF16)
    pbuf = cx.sb("ra_pbuf", [128, 128], BF16)
    Ab = cx.sb("ra_Ab", [128, RET_H * 2 * 512], BF16)
    wb = cx.sb("ra_wb", [128, 4 * 2048], BF16)
    rt1 = cx.sb("ra_rt1", [128, 512], F32)
    rt2 = cx.sb("ra_rt2", [128, 512], F32)
    zt = cx.sb("ra_zt", [128, DC * 512], BF16)
    mu_s = cx.sb("ra_mu", [128, 512], F32)
    rstd_s = cx.sb("ra_rstd", [128, 512], F32)
    tmp1 = cx.sb("ra_tmp1", [128, 512], F32)
    tmp2 = cx.sb("ra_tmp2", [128, 512], F32)
    ftmp = cx.sb("ra_ftmp", [128, 512], F32)
    ftmp2 = cx.sb("ra_ftmp2", [128, 512], F32)
    outf = cx.sb("ra_outf", [128, 2 * 512], F32)

    pb = cx.pb
    pbh = cx.pbh
    identb = consts["identb"]

    sx = Stream(cx, "rax")
    sw = Stream(cx, "raw", nbuf=4, alt=True)
    so = Stream(cx, "rao")

    r_xf = RBuf(2)
    r_sq = RBuf(1)
    r_wf = RBuf(2)
    r_wb = RBuf(2)
    r_banks = [RBuf(1) for _ in range(8)]
    r_pbh = RBuf(1)
    r_xh = RBuf(1)
    r_xn = RBuf(1)
    r_qkvg = RBuf(1)
    r_ktok = RBuf(2)
    r_vtok = RBuf(2)
    r_ob = RBuf(1)
    r_zt = RBuf(1)
    r_stat = RBuf(1)
    r_outf = RBuf(2)

    xload = wload = ostore = 0

    cx.op("vector", lambda e: e.memset(At[:], 0.0))
    cx.op("vector", lambda e: e.memset(Ab[:], 0.0))
    a_ready = cx.val("V")
    r_Ab = RBuf(1)
    r_pbuf = RBuf(1)

    def stats_and_xn(ts):
        nonlocal xload
        for c in range(DC):
            par = xload % 2
            r_xf.gate(cx, "sync", xload)
            sx.dma(xload, xf[:, par * 512:(par + 1) * 512],
                   xT[c * 128:(c + 1) * 128, ts:ts + 512])
            sx.wait("vector", xload)
            r_xh.gate(cx, "vector", 0) if c == 0 else None
            cx.op("vector", lambda e, c=c, par=par:
                  e.tensor_copy(out=xh[:, c * 512:(c + 1) * 512],
                                in_=xf[:, par * 512:(par + 1) * 512]))
            r_xf.record(xload, cx.mark("vector"))
            vcast = cx.val("V")
            cx.wait("scalar", "V", vcast)
            if c == 0:
                r_sq.gate(cx, "scalar", 0)
            cx.op("scalar", lambda e, c=c:
                  e.activation(out=sq[:, c * 512:(c + 1) * 512],
                               in_=xh[:, c * 512:(c + 1) * 512], func=AF.Square))
            scast = cx.val("S")
            cx.wait("tensor", "V", vcast)
            if c == 0:
                r_banks[0].gate(cx, "tensor", 0)
                r_banks[1].gate(cx, "tensor", 0)
            st, sp = (c == 0), (c == DC - 1)
            cx.op("tensor", lambda e, c=c, st=st, sp=sp: e.matmul(
                out=pb[0][:], lhsT=ones[:], rhs=xh[:, c * 512:(c + 1) * 512],
                start=st, stop=sp), inc=sp)
            cx.wait("tensor", "S", scast)
            cx.op("tensor", lambda e, c=c, st=st, sp=sp: e.matmul(
                out=pb[1][:], lhsT=ones[:], rhs=sq[:, c * 512:(c + 1) * 512],
                start=st, stop=sp), inc=sp)
            if sp:
                r_sq.record(0, cx.mark("tensor"))
            xload += 1
        r_stat.gate(cx, "vector", 0)
        cx.wait("vector", "T")
        ln_stats_combine(cx, pb[0][:], pb[1][:], mu_s[:], rstd_s[:],
                         tmp1[:], tmp2[:], 1.0 / D)
        r_banks[0].record(0, cx.mark("vector"))
        r_banks[1].record(0, cx.mark("vector"))
        rstd_done = cx.val("S")
        cx.wait("vector", "S", rstd_done)
        r_xn.gate(cx, "vector", 0)
        for c in range(DC):
            cx.op("vector", lambda e, c=c: e.tensor_tensor(
                out=xn[:, c * 512:(c + 1) * 512], in0=xh[:, c * 512:(c + 1) * 512],
                in1=mu_s[:], op=ALU.subtract))
            cx.wait("vector", "V")
            cx.op("vector", lambda e, c=c: e.tensor_tensor(
                out=xn[:, c * 512:(c + 1) * 512], in0=xn[:, c * 512:(c + 1) * 512],
                in1=rstd_s[:], op=ALU.mult))
        r_xh.record(0, {"V": cx.val("V"), "T": cx.val("T")})
        r_stat.record(0, {"V": cx.val("V"), "S": cx.val("S")})
        return cx.val("V")

    def load_w_slab(src2d):
        nonlocal wload
        w = src2d.shape[1]
        par = wload % 4
        q = sw.queue(wload)
        r_wb.gate(cx, q, wload)
        sw.dma(wload, wb[:, par * 2048:par * 2048 + w], src2d)
        sw.wait("tensor", wload)
        wload += 1
        return par, 0

    for tc in range(NTC):
        ts = tc * 512
        xn_done = stats_and_xn(ts)

        r_qkvg.gate(cx, "vector", 0)
        r_qkvg.gate(cx, "scalar", 0)
        r_ob.gate(cx, "vector", 0)
        for which, wmat, dstT in (("q", Wq, qT), ("k", Wk, kT)):
            for f in range(8):
                if f % 2 == 0:
                    par, wcast = load_w_slab(wmat[:, f * 1024:(f + 2) * 1024])
                    pw = (par, wload - 1)
                    cx.wait("tensor", "S", wcast)
                    cx.wait("tensor", "V", xn_done)
                par = pw[0]
                off = par * 2048 + (f % 2) * 1024
                bank = pb[2 + f % 2]
                r_banks[2 + f % 2].gate(cx, "tensor", 0)
                for k in range(DC):
                    cx.op("tensor", lambda e, off=off, k=k, bank=bank: e.matmul(
                        out=bank[:], lhsT=wb[:, off + k * 128:off + (k + 1) * 128],
                        rhs=xn[:, k * 512:(k + 1) * 512],
                        start=(k == 0), stop=(k == DC - 1)), inc=(k == DC - 1))
                if f % 2 == 1:
                    r_wb.record(pw[1], cx.mark("tensor"))
                h, c = f // 2, f % 2
                cx.wait("vector", "T")
                if c == 0:
                    cx.op("vector", lambda e, bank=bank: e.tensor_copy(out=rt1[:], in_=bank[:]))
                    r_banks[2 + f % 2].record(0, cx.mark("vector"))
                else:
                    cx.op("vector", lambda e, bank=bank: e.tensor_copy(out=rt2[:], in_=bank[:]))
                    r_banks[2 + f % 2].record(0, cx.mark("vector"))
                    cx.wait("vector", "V")
                    gq = consts["gq"] if which == "q" else consts["gk"]
                    cs = consts["cos1"][:, ts:ts + 512]
                    sn = consts["sin1"][:, ts:ts + 512]
                    ghead = gq[:, h * 512:(h + 1) * 512]
                    d0 = (h * 2 + 0) * 512
                    d1 = (h * 2 + 1) * 512
                    def chain(fn):
                        cx.op("vector", fn)
                        cx.wait("vector", "V")
                    chain(lambda e, cs=cs: e.tensor_tensor(
                        out=tmp1[:], in0=rt1[:], in1=cs, op=ALU.mult))
                    chain(lambda e, sn=sn: e.tensor_tensor(
                        out=tmp2[:], in0=rt2[:], in1=sn, op=ALU.mult))
                    chain(lambda e: e.tensor_tensor(
                        out=tmp1[:], in0=tmp1[:], in1=tmp2[:], op=ALU.subtract))
                    chain(lambda e, dstT=dstT, d0=d0, ghead=ghead: e.tensor_tensor(
                        out=dstT[:, d0:d0 + 512], in0=tmp1[:], in1=ghead, op=ALU.mult))
                    chain(lambda e, cs=cs: e.tensor_tensor(
                        out=tmp1[:], in0=rt2[:], in1=cs, op=ALU.mult))
                    chain(lambda e, sn=sn: e.tensor_tensor(
                        out=tmp2[:], in0=rt1[:], in1=sn, op=ALU.mult))
                    chain(lambda e: e.tensor_tensor(
                        out=tmp1[:], in0=tmp1[:], in1=tmp2[:], op=ALU.add))
                    chain(lambda e, dstT=dstT, d1=d1, ghead=ghead: e.tensor_tensor(
                        out=dstT[:, d1:d1 + 512], in0=tmp1[:], in1=ghead, op=ALU.mult))
        for which, wmat, dstT in (("v", Wv, vT), ("g", Wg, gT)):
            for f in range(16):
                if f % 2 == 0:
                    par, wcast = load_w_slab(wmat[:, f * 1024:(f + 2) * 1024])
                    pw = (par, wload - 1)
                    cx.wait("tensor", "S", wcast)
                    cx.wait("tensor", "V", xn_done)
                par = pw[0]
                off = par * 2048 + (f % 2) * 1024
                bank = pb[2 + f % 2]
                r_banks[2 + f % 2].gate(cx, "tensor", 0)
                for k in range(DC):
                    cx.op("tensor", lambda e, off=off, k=k, bank=bank: e.matmul(
                        out=bank[:], lhsT=wb[:, off + k * 128:off + (k + 1) * 128],
                        rhs=xn[:, k * 512:(k + 1) * 512],
                        start=(k == 0), stop=(k == DC - 1)), inc=(k == DC - 1))
                if f % 2 == 1:
                    r_wb.record(pw[1], cx.mark("tensor"))
                cx.wait("scalar", "T")
                cx.op("scalar", lambda e, dstT=dstT, f=f, bank=bank: e.activation(
                    out=dstT[:, f * 512:(f + 1) * 512], in_=bank[:], func=AF.Identity))
                r_banks[2 + f % 2].record(0, cx.mark("scalar"))
        qkvg_done_v = cx.val("V")
        qkvg_done_s = cx.val("S")

        for ml in range(4):
            tl = ml * 128
            for h in range(RET_H):
                r_pbh.gate(cx, "tensor", 0)
                cx.wait("tensor", "V", qkvg_done_v)
                cx.wait("tensor", "S", qkvg_done_s)
                kpar = (ml * RET_H + h) % 2
                for c in range(2):
                    cx.op("tensor", lambda e, h=h, c=c, tl=tl: e.transpose(
                        out=pbh[:, c * 128:(c + 1) * 128],
                        in_=kT[:, (h * 2 + c) * 512 + tl:(h * 2 + c) * 512 + tl + 128],
                        identity=identb[:]), inc=True)
                for d in range(4):
                    cx.op("tensor", lambda e, h=h, d=d, tl=tl: e.transpose(
                        out=pbh[:, 256 + d * 128:256 + (d + 1) * 128],
                        in_=vT[:, (h * 4 + d) * 512 + tl:(h * 4 + d) * 512 + tl + 128],
                        identity=identb[:]), inc=True)
                tdone = cx.val("T")
                r_ktok.gate(cx, "vector", kpar)
                r_vtok.gate(cx, "vector", kpar)
                cx.wait("vector", "T", tdone)
                cx.op("vector", lambda e, kpar=kpar: e.tensor_copy(
                    out=ktok[:, kpar * 256:kpar * 256 + 256], in_=pbh[:, 0:256]))
                cx.op("vector", lambda e, kpar=kpar: e.tensor_copy(
                    out=vtok[:, kpar * 512:kpar * 512 + 512], in_=pbh[:, 256:768]))
                r_pbh.record(0, cx.mark("vector"))
                ktok_v = cx.val("V")

                r_banks[3].gate(cx, "tensor", 0)
                for c in range(2):
                    cx.op("tensor", lambda e, h=h, c=c, tl=tl: e.matmul(
                        out=pb[3][:, 0:128],
                        lhsT=kT[:, (h * 2 + c) * 512 + tl:(h * 2 + c) * 512 + tl + 128],
                        rhs=qT[:, (h * 2 + c) * 512 + tl:(h * 2 + c) * 512 + tl + 128],
                        start=(c == 0), stop=(c == 1)), inc=(c == 1))
                cx.wait("vector", "T")
                r_pbuf.gate(cx, "vector", 0)
                cx.op("vector", lambda e: e.tensor_tensor(
                    out=pbuf[:], in0=pb[3][:, 0:128], in1=consts["maskT"],
                    op=ALU.mult))
                r_banks[3].record(0, cx.mark("vector"))
                p_done = cx.val("V")

                r_banks[4].gate(cx, "tensor", 0)
                cx.wait("tensor", "V", max(a_ready, ktok_v, p_done))
                for d in range(4):
                    for c in range(2):
                        cx.op("tensor", lambda e, h=h, c=c, d=d, tl=tl: e.matmul(
                            out=pb[4][:, d * 128:(d + 1) * 128],
                            lhsT=Ab[:, (h * 2 + c) * 512 + d * 128:(h * 2 + c) * 512 + (d + 1) * 128],
                            rhs=qT[:, (h * 2 + c) * 512 + tl:(h * 2 + c) * 512 + tl + 128],
                            start=(c == 0), stop=False), inc=False)
                    cx.op("tensor", lambda e, d=d, kpar=kpar: e.matmul(
                        out=pb[4][:, d * 128:(d + 1) * 128],
                        lhsT=vtok[:, kpar * 512 + d * 128:kpar * 512 + (d + 1) * 128],
                        rhs=pbuf[:],
                        start=False, stop=True), inc=True)
                o_mm = cx.val("T")
                r_pbuf.record(0, {"T": o_mm})
                r_Ab.record(0, {"T": o_mm})
                cx.wait("vector", "T", o_mm)
                for d in range(4):
                    cx.op("vector", lambda e, h=h, d=d, tl=tl: e.tensor_copy(
                        out=ob[:, (h * 4 + d) * 512 + tl:(h * 4 + d) * 512 + tl + 128],
                        in_=pb[4][:, d * 128:(d + 1) * 128]))
                r_banks[4].record(0, cx.mark("vector"))

                r_banks[5].gate(cx, "tensor", 0)
                r_banks[6].gate(cx, "tensor", 0)
                cx.wait("tensor", "V", ktok_v)
                for c in range(2):
                    cx.op("tensor", lambda e, c=c, kpar=kpar: e.matmul(
                        out=pb[5 + c][:],
                        lhsT=ktok[:, kpar * 256 + c * 128:kpar * 256 + (c + 1) * 128],
                        rhs=vtok[:, kpar * 512:kpar * 512 + 512],
                        start=True, stop=True), inc=True)
                da_mm = cx.val("T")
                r_ktok.record(kpar, cx.mark("tensor"))
                r_vtok.record(kpar, cx.mark("tensor"))
                cx.wait("vector", "T", da_mm)
                r_Ab.gate(cx, "vector", 0)
                for c in range(2):
                    acol = (h * 2 + c) * 512
                    cx.op("vector", lambda e, acol=acol, c=c: e.tensor_tensor(
                        out=At[:, acol:acol + 512], in0=At[:, acol:acol + 512],
                        in1=pb[5 + c][:], op=ALU.add))
                    cx.wait("vector", "V")
                    cx.op("vector", lambda e, acol=acol, h=h: e.tensor_scalar_mul(
                        out=At[:, acol:acol + 512], in0=At[:, acol:acol + 512],
                        scalar1=float(GAMMAS[h] ** 128)))
                    cx.wait("vector", "V")
                    cx.op("vector", lambda e, acol=acol: e.tensor_copy(
                        out=Ab[:, acol:acol + 512], in_=At[:, acol:acol + 512]))
                    r_banks[5 + c].record(0, cx.mark("vector"))
                a_ready = cx.val("V")
        ret_done_v = cx.val("V")

        for h in range(RET_H):
            for d in range(4):
                col = (h * 4 + d) * 512
                cx.wait("scalar", "V", ret_done_v)
                if d == 0:
                    r_sq.gate(cx, "scalar", 0)
                cx.op("scalar", lambda e, col=col, d=d: e.activation(
                    out=sq[:, d * 512:(d + 1) * 512], in_=ob[:, col:col + 512],
                    func=AF.Square))
                scast = cx.val("S")
                cx.wait("tensor", "S", scast)
                if d == 0:
                    r_banks[0].gate(cx, "tensor", 0)
                cx.op("tensor", lambda e, d=d: e.matmul(
                    out=pb[0][:], lhsT=ones[:], rhs=sq[:, d * 512:(d + 1) * 512],
                    start=(d == 0), stop=(d == 3)), inc=(d == 3))
            r_sq.record(0, cx.mark("tensor"))
            r_stat.gate(cx, "vector", 0)
            cx.wait("vector", "T")
            cx.op("vector", lambda e: e.tensor_scalar_mul(
                out=tmp2[:], in0=pb[0][:], scalar1=1.0 / RET_DV))
            r_banks[0].record(0, cx.mark("vector"))
            cx.wait("scalar", "V")
            cx.op("scalar", lambda e: e.activation(out=tmp1[:], in_=tmp2[:],
                                                   func=AF.Ln, bias=cx.eps_col))
            cx.wait("scalar", "S")
            cx.op("scalar", lambda e: e.activation(out=rstd_s[:], in_=tmp1[:],
                                                   func=AF.Exp, scale=-0.5))
            rstd_done = cx.val("S")
            cx.wait("vector", "S", rstd_done)
            for d in range(4):
                col = (h * 4 + d) * 512
                cx.op("vector", lambda e, col=col: e.tensor_tensor(
                    out=ob[:, col:col + 512], in0=ob[:, col:col + 512],
                    in1=rstd_s[:], op=ALU.mult))
                cx.wait("vector", "V")
                cx.op("vector", lambda e, col=col, d=d: e.tensor_scalar_mul(
                    out=ob[:, col:col + 512], in0=ob[:, col:col + 512],
                    scalar1=consts["gnw"][:, d:d + 1]))
                cx.wait("vector", "V")
                cx.op("vector", lambda e, col=col: e.tensor_tensor(
                    out=gT[:, col:col + 512], in0=gT[:, col:col + 512],
                    in1=ob[:, col:col + 512], op=ALU.mult))
            r_stat.record(0, {"V": cx.val("V"), "S": cx.val("S")})
        og_done = cx.val("V")

        r_zt.gate(cx, "scalar", 0)
        for f in range(8):
            par, wcast = load_w_slab(Wo[:, f * 2048:(f + 1) * 2048])
            cx.wait("tensor", "S", wcast)
            cx.wait("tensor", "V", og_done)
            bank = pb[2 + f % 2]
            r_banks[2 + f % 2].gate(cx, "tensor", 0)
            for kc in range(16):
                cx.op("tensor", lambda e, par=par, kc=kc, bank=bank: e.matmul(
                    out=bank[:],
                    lhsT=wb[:, par * 2048 + kc * 128:par * 2048 + (kc + 1) * 128],
                    rhs=gT[:, kc * 512:(kc + 1) * 512],
                    start=(kc == 0), stop=(kc == 15)), inc=(kc == 15))
            r_wb.record(wload - 1, cx.mark("tensor"))
            cx.wait("scalar", "T")
            cx.op("scalar", lambda e, f=f, bank=bank: e.activation(
                out=zt[:, f * 512:(f + 1) * 512], in_=bank[:], func=AF.Identity))
            r_banks[2 + f % 2].record(0, cx.mark("scalar"))
        zt_done = cx.val("S")

        for e_ in range(8):
            col = e_ * 512
            if e_ == 0:
                r_sq.gate(cx, "scalar", 0)
                cx.wait("scalar", "S", zt_done)
            cx.op("scalar", lambda e, col=col, e_=e_: e.activation(
                out=sq[:, e_ * 512:(e_ + 1) * 512], in_=zt[:, col:col + 512],
                func=AF.Square))
            scast = cx.val("S")
            if e_ == 0:
                cx.wait("tensor", "S", zt_done)
                r_banks[0].gate(cx, "tensor", 0)
                r_banks[1].gate(cx, "tensor", 0)
            st, sp = (e_ == 0), (e_ == 7)
            cx.op("tensor", lambda e, col=col, st=st, sp=sp: e.matmul(
                out=pb[0][:], lhsT=ones[:], rhs=zt[:, col:col + 512],
                start=st, stop=sp), inc=sp)
            cx.wait("tensor", "S", scast)
            cx.op("tensor", lambda e, e_=e_, st=st, sp=sp: e.matmul(
                out=pb[1][:], lhsT=ones[:], rhs=sq[:, e_ * 512:(e_ + 1) * 512],
                start=st, stop=sp), inc=sp)
            if sp:
                r_sq.record(0, cx.mark("tensor"))
        r_stat.gate(cx, "vector", 0)
        cx.wait("vector", "T")
        ln_stats_combine(cx, pb[0][:], pb[1][:], mu_s[:], rstd_s[:],
                         tmp1[:], tmp2[:], 1.0 / D)
        r_banks[0].record(0, cx.mark("vector"))
        r_banks[1].record(0, cx.mark("vector"))
        rstd2_done = cx.val("S")
        cx.wait("vector", "S", rstd2_done)
        for c in range(DC):
            par = xload % 2
            r_xf.gate(cx, "sync", xload)
            sx.dma(xload, xf[:, par * 512:(par + 1) * 512],
                   xT[c * 128:(c + 1) * 128, ts:ts + 512])
            cx.op("vector", lambda e, c=c: e.tensor_tensor(
                out=ftmp[:], in0=zt[:, c * 512:(c + 1) * 512],
                in1=mu_s[:], op=ALU.subtract))
            cx.wait("vector", "V")
            cx.op("vector", lambda e: e.tensor_tensor(
                out=ftmp2[:], in0=ftmp[:], in1=rstd_s[:], op=ALU.mult))
            sx.wait("vector", xload)
            opar = ostore % 2
            r_outf.gate(cx, "vector", ostore)
            cx.wait("vector", "V")
            cx.op("vector", lambda e, par=par, opar=opar: e.tensor_tensor(
                out=outf[:, opar * 512:(opar + 1) * 512], in0=ftmp2[:],
                in1=xf[:, par * 512:(par + 1) * 512], op=ALU.add))
            r_xf.record(xload, cx.mark("vector"))
            vadd = cx.val("V")
            cx.wait("sync", "V", vadd)
            so.dma(ostore, yT[c * 128:(c + 1) * 128, ts:ts + 512],
                   outf[:, opar * 512:(opar + 1) * 512])
            r_outf.record(ostore, so.mark(ostore))
            xload += 1
            ostore += 1
        r_zt.record(0, {"V": cx.val("V"), "T": cx.val("T"), "S": cx.val("S")})
        r_stat.record(0, {"V": cx.val("V"), "S": cx.val("S")})
        allm = {"V": cx.val("V"), "T": cx.val("T"), "S": cx.val("S")}
        r_xn.record(0, allm)
        r_qkvg.record(0, allm)
        r_ob.record(0, allm)
    so.drain()


def build_l1_attn(T=4096):
    nc = bass.Bass("TRN2", target_bir_lowering=False)
    xT = nc.dram_tensor("xT", [D, T], F32, kind="ExternalInput").ap()
    Wq = nc.dram_tensor("Wq", [128, 8 * 1024], F32, kind="ExternalInput").ap()
    Wk = nc.dram_tensor("Wk", [128, 8 * 1024], F32, kind="ExternalInput").ap()
    Wv = nc.dram_tensor("Wv", [128, 16 * 1024], F32, kind="ExternalInput").ap()
    Wg = nc.dram_tensor("Wg", [128, 16 * 1024], F32, kind="ExternalInput").ap()
    Wo = nc.dram_tensor("Wo", [128, 8 * 2048], F32, kind="ExternalInput").ap()
    cos1 = nc.dram_tensor("cos1", [128, T], BF16, kind="ExternalInput").ap()
    sin1 = nc.dram_tensor("sin1", [128, T], BF16, kind="ExternalInput").ap()
    gq = nc.dram_tensor("gq", [128, RET_H * 512], BF16, kind="ExternalInput").ap()
    gk = nc.dram_tensor("gk", [128, RET_H * 512], BF16, kind="ExternalInput").ap()
    maskT = nc.dram_tensor("maskT", [128, 128], BF16, kind="ExternalInput").ap()
    gnw = nc.dram_tensor("gnw", [128, 4], F32, kind="ExternalInput").ap()
    ident_in = nc.dram_tensor("ident_in", [128, 128], BF16, kind="ExternalInput").ap()
    yT = nc.dram_tensor("yT", [D, T], F32, kind="ExternalOutput").ap()

    es = ExitStack()
    cx = Ctx(nc, es)
    ones = cx.sb("ones", [128, 128], BF16)
    identb = cx.sb("identb", [128, 128], BF16)
    epsc = cx.sb("epsc", [128, 1], F32)
    cx.eps_col = epsc[:, 0:1]
    c_cos = cx.sb("c_cos", [128, T], BF16)
    c_sin = cx.sb("c_sin", [128, T], BF16)
    c_gq = cx.sb("c_gq", [128, RET_H * 512], BF16)
    c_gk = cx.sb("c_gk", [128, RET_H * 512], BF16)
    c_maskT = cx.sb("c_maskT", [128, 128], BF16)
    c_gnw = cx.sb("c_gnw", [128, 4], F32)
    pbig = cx.ps("pbig", [128, 2048], F32)
    cx.pb = [cx.ps("pb0", [128, 512]), cx.ps("pb1", [128, 512]),
             pbig[:, 0:512], pbig[:, 512:1024],
             pbig[:, 1024:1536], pbig[:, 1536:2048],
             cx.ps("pb6", [128, 512])]
    cx.pbB = pbig[:, 0:1024]
    cx.pbB2 = pbig[:, 1024:2048]
    cx.pbh = cx.ps("pbh", [128, 1024], BF16)[:]

    sc = Stream(cx, "const", nbuf=8)
    for i, (dst, srcap) in enumerate([(c_cos, cos1), (c_sin, sin1), (c_gq, gq),
                                      (c_gk, gk), (c_maskT, maskT), (c_gnw, gnw),
                                      (identb, ident_in)]):
        sc.dma(i, dst[:], srcap)
    cx.op("vector", lambda e: e.memset(ones[:], 1.0))
    cx.op("vector", lambda e: e.memset(epsc[:], LN_EPS))
    consts_v = cx.val("V")
    sc.drain("vector")
    sc.drain("scalar")
    sc.drain("tensor")
    cx.wait("tensor", "V", consts_v)
    cx.wait("scalar", "V", consts_v)
    cx.wait("vector", "V", consts_v)

    consts = {"ones": ones[:], "identb": identb[:], "cos1": c_cos[:],
              "sin1": c_sin[:], "gq": c_gq[:], "gk": c_gk[:],
              "maskT": c_maskT[:], "gnw": c_gnw[:]}
    emit_l1_attn(cx, T, xT, Wq, Wk, Wv, Wg, Wo, yT, consts)

    with es, nc.Block() as block:
        cx.finish(block)
    return nc


def emit_l0_attn(cx, T, xT, wqkvT, wprojT, bqkvc, bprojc, qnc, knc, yT, consts):
    ones = consts["ones"]
    identb = consts["identb"]
    eps6 = consts["eps6"]
    NTC = T // 512
    nc = cx.nc
    OT = nc.dram_tensor("ga_oscr", [D, T], BF16, kind="Internal").ap()

    xf = cx.sb("ga_xf", [128, 2 * 512], F32)
    xh = cx.sb("ga_xh", [128, 2 * 512], BF16)
    xn = cx.sb("ga_xn", [128, DC * 512], BF16)
    sqb = cx.sb("ga_sq", [128, 8 * 512], BF16)
    muA = cx.sb("ga_muA", [128, NTC * 512], BF16)
    rsA = cx.sb("ga_rsA", [128, NTC * 512], BF16)
    kA = cx.sb("ga_kA", [65, 8 * T], BF16)
    vh = cx.sb("ga_vh", [128, (T // 128) * 520], BF16)
    qa = cx.sb("ga_qa", [65, 8 * 512], BF16)
    wres = cx.sb("ga_wres", [128, 4 * 2048], BF16)
    qkt = cx.sb("ga_qkt", [128, 2 * 512], BF16)
    pt = cx.sb("ga_pt", [128, 2 * 1024], BF16)
    dbc = cx.sb("ga_dbc", [128, 512], BF16)
    mng = cx.sb("ga_mng", [128, 8], BF16)
    oc = cx.sb("ga_oc", [128, 8 * 512], BF16)
    zt = cx.sb("ga_zt", [128, DC * 512], BF16)
    mu_s = cx.sb("ga_mu", [128, 512], F32)
    rstd_s = cx.sb("ga_rstd", [128, 512], F32)
    tmp1 = cx.sb("ga_tmp1", [128, 512], F32)
    tmp2 = cx.sb("ga_tmp2", [128, 512], F32)
    ftmp = cx.sb("ga_ftmp", [128, 512], F32)
    ftmp2 = cx.sb("ga_ftmp2", [128, 512], F32)
    outf = cx.sb("ga_outf", [128, 2 * 512], F32)
    mnf = cx.sb("ga_mnf", [128, 1], F32)
    mnf8 = cx.sb("ga_mnf8", [128, 8], F32)

    pb = cx.pb
    pbh = cx.pbh

    sx = Stream(cx, "gax")
    sw = Stream(cx, "gaw", nbuf=4, alt=False)
    sk = Stream(cx, "gak")
    sov = Stream(cx, "gao")
    so = Stream(cx, "gas")

    r_xf = RBuf(2)
    r_xh = RBuf(2)
    r_xn = RBuf(1)
    r_sq = RBuf(1)
    r_wf = RBuf(2)
    r_wslab = RBuf(2)
    r_wres = RBuf(1)
    r_banks = [RBuf(1) for _ in range(8)]
    r_pbh = RBuf(1)
    r_kA = RBuf(1)
    r_vh = RBuf(1)
    r_qa = RBuf(1)
    r_qkt = RBuf(1)
    r_pt = RBuf(2)
    r_oc = RBuf(1)
    r_zt = RBuf(1)
    r_stat = RBuf(1)
    r_outf = RBuf(2)
    r_mng = RBuf(4)
    r_dbc = RBuf(1)

    xload = wload = kcopy = ostore = ovio = 0

    for j in range(NTC):
        ts = j * 512
        for c in range(DC):
            par = xload % 2
            r_xf.gate(cx, "sync", xload)
            sx.dma(xload, xf[:, par * 512:(par + 1) * 512],
                   xT[c * 128:(c + 1) * 128, ts:ts + 512])
            sx.wait("vector", xload)
            r_xh.gate(cx, "vector", xload)
            cx.op("vector", lambda e, par=par:
                  e.tensor_copy(out=xh[:, par * 512:(par + 1) * 512],
                                in_=xf[:, par * 512:(par + 1) * 512]))
            r_xf.record(xload, cx.mark("vector"))
            vcast = cx.val("V")
            cx.wait("scalar", "V", vcast)
            if c == 0:
                r_sq.gate(cx, "scalar", 0)
            cx.op("scalar", lambda e, c=c, par=par:
                  e.activation(out=sqb[:, c * 512:(c + 1) * 512],
                               in_=xh[:, par * 512:(par + 1) * 512], func=AF.Square))
            scast = cx.val("S")
            r_xh.record(xload, cx.mark("scalar"))
            cx.wait("tensor", "V", vcast)
            if c == 0:
                r_banks[0].gate(cx, "tensor", 0)
                r_banks[1].gate(cx, "tensor", 0)
            st, sp = (c == 0), (c == DC - 1)
            cx.op("tensor", lambda e, par=par, st=st, sp=sp: e.matmul(
                out=pb[0][:], lhsT=ones[:], rhs=xh[:, par * 512:(par + 1) * 512],
                start=st, stop=sp), inc=True)
            r_xh.record(xload, cx.mark("tensor"))
            cx.wait("tensor", "S", scast)
            cx.op("tensor", lambda e, c=c, st=st, sp=sp: e.matmul(
                out=pb[1][:], lhsT=ones[:], rhs=sqb[:, c * 512:(c + 1) * 512],
                start=st, stop=sp), inc=sp)
            if sp:
                r_sq.record(0, cx.mark("tensor"))
            xload += 1
        r_stat.gate(cx, "vector", 0)
        cx.wait("vector", "T")
        ln_stats_combine(cx, pb[0][:], pb[1][:], mu_s[:], rstd_s[:],
                         tmp1[:], tmp2[:], 1.0 / D)
        r_banks[0].record(0, cx.mark("vector"))
        r_banks[1].record(0, cx.mark("vector"))
        rstd_done = cx.val("S")
        cx.wait("vector", "S", rstd_done)
        cx.op("vector", lambda e, ts=ts: e.tensor_copy(
            out=muA[:, ts:ts + 512], in_=mu_s[:]))
        cx.op("vector", lambda e, ts=ts: e.tensor_copy(
            out=rsA[:, ts:ts + 512], in_=rstd_s[:]))
        r_stat.record(0, {"V": cx.val("V"), "S": cx.val("S")})
    mu_ready = cx.val("V")

    def make_xn(ts):
        nonlocal xload
        r_xn.gate(cx, "vector", 0)
        for c in range(DC):
            par = xload % 2
            r_xf.gate(cx, "sync", xload)
            sx.dma(xload, xf[:, par * 512:(par + 1) * 512],
                   xT[c * 128:(c + 1) * 128, ts:ts + 512])
            sx.wait("vector", xload)
            if c == 0:
                cx.wait("vector", "V", mu_ready)
            cx.op("vector", lambda e, c=c, par=par, ts=ts: e.tensor_tensor(
                out=xn[:, c * 512:(c + 1) * 512],
                in0=xf[:, par * 512:(par + 1) * 512],
                in1=muA[:, ts:ts + 512], op=ALU.subtract))
            cx.wait("vector", "V")
            cx.op("vector", lambda e, c=c, ts=ts: e.tensor_tensor(
                out=xn[:, c * 512:(c + 1) * 512], in0=xn[:, c * 512:(c + 1) * 512],
                in1=rsA[:, ts:ts + 512], op=ALU.mult))
            r_xf.record(xload, cx.mark("vector"))
            xload += 1
        return cx.val("V")

    def load_slab(src2d):
        nonlocal wload
        par = wload % 4
        slot = par * 2048
        w = src2d.shape[1]
        q = sw.queue(wload)
        cx.wait_marks(q, cx.convm_l0a)
        r_wslab.gate(cx, q, wload)
        sw.dma(wload, wres[:, slot:slot + w], src2d)
        sw.wait("tensor", wload)
        idx = wload
        wload += 1
        return slot, idx, 0

    def rms_norm_evac(bank, fglob, nw_col, extra_scale, dst_cb):
        cx.wait("scalar", "T")
        r_qkt.gate(cx, "scalar", 0)
        cx.op("scalar", lambda e, bank=bank, fglob=fglob: e.activation(
            out=qkt[:, 0:512], in_=bank[:], func=AF.Identity,
            bias=bqkvc[:, fglob:fglob + 1]))
        cx.wait("scalar", "S")
        cx.op("scalar", lambda e: e.activation(
            out=qkt[:, 512:1024], in_=qkt[:, 0:512], func=AF.Square))
        sq_done = cx.val("S")
        cx.wait("tensor", "S", sq_done)
        r_banks[4].gate(cx, "tensor", 0)
        cx.op("tensor", lambda e: e.matmul(
            out=pb[4][0:2, :], lhsT=consts["i2"], rhs=qkt[:, 512:1024],
            start=True, stop=True), inc=True)
        hsum = cx.val("T")
        r_stat.gate(cx, "vector", 0)
        cx.wait("vector", "T", hsum)
        cx.op("vector", lambda e: e.tensor_scalar_mul(
            out=tmp1[0:2, :], in0=pb[4][0:2, :], scalar1=1.0 / 64.0))
        r_banks[4].record(0, cx.mark("vector"))
        cx.wait("scalar", "V")
        cx.op("scalar", lambda e: e.activation(
            out=tmp2[0:2, :], in_=tmp1[0:2, :], func=AF.Ln, bias=eps6[0:2, 0:1]))
        cx.wait("scalar", "S")
        cx.op("scalar", lambda e: e.activation(
            out=tmp1[0:2, :], in_=tmp2[0:2, :], func=AF.Exp, scale=-0.5))
        rstd_done = cx.val("S")
        r_dbc.gate(cx, "vector", 0)
        cx.wait("vector", "S", rstd_done)
        cx.op("vector", lambda e: e.tensor_copy(out=dbc[0:2, :], in_=tmp1[0:2, :]))
        r_stat.record(0, {"V": cx.val("V"), "S": cx.val("S")})
        bc_v = cx.val("V")
        cx.wait("tensor", "V", bc_v)
        r_banks[5].gate(cx, "tensor", 0)
        cx.op("tensor", lambda e: e.matmul(
            out=pb[5][:], lhsT=consts["i2t"], rhs=dbc[0:2, :],
            start=True, stop=True), inc=True)
        r_dbc.record(0, cx.mark("tensor"))
        bc_t = cx.val("T")
        cx.wait("vector", "T", bc_t)
        cx.op("vector", lambda e: e.tensor_tensor(
            out=qkt[:, 512:1024], in0=qkt[:, 0:512], in1=pb[5][:], op=ALU.mult))
        r_banks[5].record(0, cx.mark("vector"))
        cx.wait("vector", "V")
        cx.op("vector", lambda e, nw_col=nw_col, extra_scale=extra_scale:
              e.tensor_scalar(out=qkt[:, 512:1024], in0=qkt[:, 512:1024],
                              scalar1=nw_col, scalar2=extra_scale,
                              op0=ALU.mult, op1=ALU.mult))
        dst_cb(cx.val("V"))

    for g in range(2):
        r_kA.gate(cx, "vector", 0)
        r_vh.gate(cx, "vector", 0)
        cx.op("vector", lambda e: e.memset(kA[64:65, :], 1.0))
        cx.op("vector", lambda e: e.memset(vh[:], 1.0))
        prep_v = cx.val("V")
        cx.wait("tensor", "V", prep_v)
        for j in range(NTC):
            ts = j * 512
            xn_done = make_xn(ts)
            for sec, base in (("k", 1024 + g * 512), ("v", 2048 + g * 512)):
                for fc in range(4):
                    fglob = (base + fc * 128) // 128
                    if fc % 2 == 0:
                        pslot, pwidx, wcast = load_slab(
                            wqkvT[:, ((base // 128) + fc) * 1024:((base // 128) + fc + 2) * 1024])
                        cx.wait("tensor", "S", wcast)
                        cx.wait("tensor", "V", xn_done)
                    slot = pslot + (fc % 2) * 1024
                    widx = pwidx
                    bank = pb[2 + fc % 2]
                    r_banks[2 + fc % 2].gate(cx, "tensor", 0)
                    for k in range(DC):
                        cx.op("tensor", lambda e, k=k, bank=bank, slot=slot: e.matmul(
                            out=bank[:], lhsT=wres[:, slot + k * 128:slot + (k + 1) * 128],
                            rhs=xn[:, k * 512:(k + 1) * 512],
                            start=(k == 0), stop=(k == DC - 1)), inc=(k == DC - 1))
                    r_wslab.record(widx, cx.mark("tensor"))
                    r_xn.record(0, cx.mark("tensor"))
                    if sec == "k":
                        def store_k(vdone, fc=fc, ts=ts):
                            nonlocal kcopy
                            h0col = (2 * fc) * T + ts
                            h1col = (2 * fc + 1) * T + ts
                            cx.wait("vector", "V", vdone)
                            cx.op("vector", lambda e, h0col=h0col: e.tensor_copy(
                                out=kA[0:64, h0col:h0col + 512],
                                in_=qkt[0:64, 512:1024]))
                            r_qkt.record(0, cx.mark("vector"))
                            cx.wait("sync", "V", cx.val("V"))
                            sk.dma(kcopy, kA[0:64, h1col:h1col + 512],
                                   qkt[64:128, 512:1024])
                            r_qkt.record(0, sk.mark(kcopy))
                            kcopy += 1
                        rms_norm_evac(bank, fglob, knc, 1.0, store_k)
                        r_banks[2 + fc % 2].record(0, cx.mark("scalar"))
                    else:
                        cx.wait("scalar", "T")
                        r_qkt.gate(cx, "scalar", 0)
                        cx.op("scalar", lambda e, bank=bank, fglob=fglob: e.activation(
                            out=qkt[:, 0:512], in_=bank[:], func=AF.Identity,
                            bias=bqkvc[:, fglob:fglob + 1]))
                        r_banks[2 + fc % 2].record(0, cx.mark("scalar"))
                        ev_done = cx.val("S")
                        cx.wait("tensor", "S", ev_done)
                        r_pbh.gate(cx, "tensor", 0)
                        for tt in range(4):
                            cx.op("tensor", lambda e, tt=tt: e.transpose(
                                out=pbh[:, tt * 128:(tt + 1) * 128],
                                in_=qkt[:, tt * 128:tt * 128 + 128],
                                identity=identb[:]), inc=(tt == 3))
                        r_qkt.record(0, cx.mark("tensor"))
                        tp_done = cx.val("T")
                        cx.wait("vector", "T", tp_done)
                        for tt in range(4):
                            tile = (ts // 128) + tt
                            c0 = tile * 520 + (2 * fc) * 65
                            c1 = tile * 520 + (2 * fc + 1) * 65
                            cx.op("vector", lambda e, tt=tt, c0=c0: e.tensor_copy(
                                out=vh[:, c0:c0 + 64],
                                in_=pbh[:, tt * 128:tt * 128 + 64]))
                            cx.op("vector", lambda e, tt=tt, c1=c1: e.tensor_copy(
                                out=vh[:, c1:c1 + 64],
                                in_=pbh[:, tt * 128 + 64:tt * 128 + 128]))
                        r_pbh.record(0, cx.mark("vector"))
        kv_ready = {"V": cx.val("V"), "T": cx.val("T"), "S": cx.val("S")}
        for i, n in enumerate(sk.names):
            kv_ready[n] = 16 * sk.counts[i]

        for j in range(NTC):
            ts = j * 512
            xn_done = make_xn(ts)
            r_qa.gate(cx, "vector", 0)
            cx.op("vector", lambda e: e.memset(qa[64:65, :], 0.0))
            for fc in range(4):
                base = g * 512
                fglob = (base + fc * 128) // 128
                if fc % 2 == 0:
                    pslot, pwidx, wcast = load_slab(
                        wqkvT[:, ((base // 128) + fc) * 1024:((base // 128) + fc + 2) * 1024])
                    cx.wait("tensor", "S", wcast)
                    cx.wait("tensor", "V", xn_done)
                slot = pslot + (fc % 2) * 1024
                widx = pwidx
                bank = pb[2 + fc % 2]
                r_banks[2 + fc % 2].gate(cx, "tensor", 0)
                for k in range(DC):
                    cx.op("tensor", lambda e, k=k, bank=bank, slot=slot: e.matmul(
                        out=bank[:], lhsT=wres[:, slot + k * 128:slot + (k + 1) * 128],
                        rhs=xn[:, k * 512:(k + 1) * 512],
                        start=(k == 0), stop=(k == DC - 1)), inc=(k == DC - 1))
                r_wslab.record(widx, cx.mark("tensor"))
                r_xn.record(0, cx.mark("tensor"))

                def store_q(vdone, fc=fc):
                    nonlocal kcopy
                    c0 = (2 * fc) * 512
                    c1 = (2 * fc + 1) * 512
                    cx.wait("vector", "V", vdone)
                    cx.op("vector", lambda e, c0=c0: e.tensor_copy(
                        out=qa[0:64, c0:c0 + 512], in_=qkt[0:64, 512:1024]))
                    r_qkt.record(0, cx.mark("vector"))
                    cx.wait("sync", "V", cx.val("V"))
                    sk.dma(kcopy, qa[0:64, c1:c1 + 512], qkt[64:128, 512:1024])
                    r_qkt.record(0, sk.mark(kcopy))
                    kcopy += 1
                rms_norm_evac(bank, fglob, qnc, 0.125, store_q)
                r_banks[2 + fc % 2].record(0, cx.mark("scalar"))
            qa_marks = {"V": cx.val("V")}
            for i, n in enumerate(sk.names):
                qa_marks[n] = 16 * sk.counts[i]
            cx.wait_marks("tensor", qa_marks)
            cx.wait_marks("tensor", kv_ready)
            r_oc.gate(cx, "vector", 0)

            for hl in range(8):
                hcol = hl * T
                pend = []

                def pa_tail():
                    hlp, ttp, mng_v = pend.pop(0)
                    cx.wait("tensor", "V", mng_v)
                    r_pbh.gate(cx, "tensor", 0)
                    cx.op("tensor", lambda e, ttp=ttp: e.transpose(
                        out=pbh[0:1, 0:128], in_=mng[:, ttp:ttp + 1],
                        identity=identb[:]), inc=True)
                    tr_done = cx.val("T")
                    cx.wait("vector", "T", tr_done)
                    cx.op("vector", lambda e, hlp=hlp, ttp=ttp: e.tensor_copy(
                        out=qa[64:65, hlp * 512 + ttp * 128:hlp * 512 + (ttp + 1) * 128],
                        in_=pbh[0:1, 0:128]))
                    r_pbh.record(0, cx.mark("vector"))
                    r_mng.record(ttp, cx.mark("vector"))

                for tt in range(4):
                    for sc in range(T // 512):
                        bank = pb[2 + sc % 2]
                        r_banks[2 + sc % 2].gate(cx, "tensor", 0)
                        cx.op("tensor", lambda e, hl=hl, tt=tt, sc=sc, hcol=hcol,
                              bank=bank: e.matmul(
                            out=bank[:],
                            lhsT=qa[0:64, hl * 512 + tt * 128:hl * 512 + (tt + 1) * 128],
                            rhs=kA[0:64, hcol + sc * 512:hcol + (sc + 1) * 512],
                            start=True, stop=True), inc=True)
                        smm = cx.val("T")
                        if sc == 1 and pend:
                            pa_tail()
                        cx.wait("vector", "T", smm)
                        if sc == 0:
                            cx.wait("vector", "V")
                        cx.op("vector", lambda e, bank=bank, sc=sc: e.reduce_max(
                            out=mnf8[:, sc:sc + 1], in_=bank[:], axis=AX.X,
                            negate=True))
                        cx.wait("vector", "V")
                        r_banks[2 + sc % 2].record(0, cx.mark("vector"))
                    r_mng.gate(cx, "vector", tt)
                    cx.op("vector", lambda e, tt=tt: e.tensor_reduce(
                        out=mng[:, tt:tt + 1], in_=mnf8[:, 0:T // 512], axis=AX.X,
                        op=ALU.min))
                    pend.append((hl, tt, cx.val("V")))
                while pend:
                    pa_tail()
                qa_row_v = cx.val("V")
                cx.wait("tensor", "V", qa_row_v)
                r_banks[6].gate(cx, "tensor", 0)
                NP = T // 256
                exp_of = {}

                def emit_exp_av(kp, hl=hl):
                    ppar = kp % 2
                    buf = cx.pbB if ppar == 0 else cx.pbB2
                    cx.wait("scalar", "T", exp_of[kp])
                    r_pt.gate(cx, "scalar", kp)
                    cx.op("scalar", lambda e, ppar=ppar, buf=buf: e.activation(
                        out=pt[:, ppar * 1024:(ppar + 1) * 1024], in_=buf,
                        func=AF.Exp))
                    for b in ((2, 3) if ppar == 0 else (4, 5)):
                        r_banks[b].record(0, cx.mark("scalar"))
                    exp_done = cx.val("S")
                    cx.wait("tensor", "S", exp_done)
                    for half in range(2):
                        sc = kp * 2 + half
                        cx.op("tensor", lambda e, sc=sc, ppar=ppar, half=half,
                              hl=hl: e.matmul(
                            out=pb[6][0:65, :],
                            lhsT=vh[:, sc * 520 + hl * 65:sc * 520 + (hl + 1) * 65],
                            rhs=pt[:, ppar * 1024 + half * 512:ppar * 1024 + (half + 1) * 512],
                            start=(sc == 0), stop=(sc == 2 * NP - 1)),
                            inc=(sc == 2 * NP - 1))
                    r_pt.record(kp, cx.mark("tensor"))

                for kp in range(NP):
                    ppar = kp % 2
                    for b in ((2, 3) if ppar == 0 else (4, 5)):
                        r_banks[b].gate(cx, "tensor", 0)
                    for half in range(2):
                        sc = kp * 2 + half
                        cx.op("tensor", lambda e, sc=sc, hcol=hcol, hl=hl,
                              ppar=ppar, half=half: e.matmul(
                            out=(cx.pbB if ppar == 0 else cx.pbB2)[:, half * 512:(half + 1) * 512],
                            lhsT=kA[:, hcol + sc * 128:hcol + (sc + 1) * 128],
                            rhs=qa[:, hl * 512:(hl + 1) * 512],
                            start=True, stop=True), inc=(half == 1))
                    exp_of[kp] = cx.val("T")
                    if kp >= 1:
                        emit_exp_av(kp - 1)
                emit_exp_av(NP - 1)
                av_mm = cx.val("T")
                cx.wait("vector", "T", av_mm)
                r_dbc.gate(cx, "vector", 0)
                cx.op("vector", lambda e: e.reciprocal(
                    out=tmp1[64:65, :], in_=pb[6][64:65, :]))
                cx.wait("vector", "V")
                cx.op("vector", lambda e: e.tensor_copy(
                    out=dbc[64:65, :], in_=tmp1[64:65, :]))
                rec_v = cx.val("V")
                cx.wait("tensor", "V", rec_v)
                r_banks[5].gate(cx, "tensor", 0)
                cx.op("tensor", lambda e: e.matmul(
                    out=pb[5][:], lhsT=consts["ones"][64:65, :], rhs=dbc[64:65, :],
                    start=True, stop=True), inc=True)
                r_dbc.record(0, cx.mark("tensor"))
                bc_t = cx.val("T")
                cx.wait("scalar", "T", bc_t)
                r_dbc.gate(cx, "scalar", 0)
                cx.op("scalar", lambda e: e.activation(
                    out=dbc[:], in_=pb[5][:], func=AF.Identity))
                r_banks[5].record(0, cx.mark("scalar"))
                dbc_s = cx.val("S")
                cx.wait("vector", "S", dbc_s)
                pbase = (hl % 2) * 64
                ocol = (hl // 2) * 512
                cx.op("vector", lambda e, pbase=pbase, ocol=ocol: e.tensor_tensor(
                    out=oc[pbase:pbase + 64, ocol:ocol + 512], in0=pb[6][0:64, :],
                    in1=dbc[pbase:pbase + 64, :], op=ALU.mult))
                r_banks[6].record(0, cx.mark("vector"))
                r_dbc.record(0, cx.mark("vector"))
            oc_v = cx.val("V")
            cx.wait("sync", "V", oc_v)
            for cc in range(4):
                sov.dma(ovio, OT[g * 512 + cc * 128:g * 512 + (cc + 1) * 128,
                                 ts:ts + 512],
                        oc[:, cc * 512:(cc + 1) * 512])
                ovio += 1
            for i, n in enumerate(sov.names):
                r_oc.record(0, {n: 16 * sov.counts[i]})
            r_qa.record(0, {"T": cx.val("T")})
        r_kA.record(0, {"T": cx.val("T")})
        r_vh.record(0, {"T": cx.val("T")})

    for j in range(NTC):
        ts = j * 512
        r_oc.gate(cx, "sync", 0)
        for cc in range(DC):
            sov.dma(ovio, oc[:, cc * 512:(cc + 1) * 512],
                    OT[cc * 128:(cc + 1) * 128, ts:ts + 512])
            ovio += 1
        oload_marks = {}
        for i, n in enumerate(sov.names):
            oload_marks[n] = 16 * sov.counts[i]
        cx.wait_marks("tensor", oload_marks)
        r_zt.gate(cx, "scalar", 0)
        for f in range(8):
            if f % 2 == 0:
                pslot, pwidx, wcast = load_slab(wprojT[:, f * 1024:(f + 2) * 1024])
                cx.wait("tensor", "S", wcast)
            slot = pslot + (f % 2) * 1024
            widx = pwidx
            bank = pb[2 + f % 2]
            r_banks[2 + f % 2].gate(cx, "tensor", 0)
            for k in range(DC):
                cx.op("tensor", lambda e, k=k, bank=bank, slot=slot: e.matmul(
                    out=bank[:], lhsT=wres[:, slot + k * 128:slot + (k + 1) * 128],
                    rhs=oc[:, k * 512:(k + 1) * 512],
                    start=(k == 0), stop=(k == DC - 1)), inc=(k == DC - 1))
            r_wslab.record(widx, cx.mark("tensor"))
            cx.wait("scalar", "T")
            cx.op("scalar", lambda e, f=f, bank=bank: e.activation(
                out=zt[:, f * 512:(f + 1) * 512], in_=bank[:], func=AF.Identity,
                bias=bprojc[:, f:f + 1]))
            r_banks[2 + f % 2].record(0, cx.mark("scalar"))
        zt_done = cx.val("S")
        r_oc.record(0, {"T": cx.val("T")})
        for e_ in range(DC):
            col = e_ * 512
            if e_ == 0:
                r_sq.gate(cx, "scalar", 0)
                cx.wait("scalar", "S", zt_done)
            cx.op("scalar", lambda e, col=col, e_=e_: e.activation(
                out=sqb[:, e_ * 512:(e_ + 1) * 512], in_=zt[:, col:col + 512],
                func=AF.Square))
            scast = cx.val("S")
            if e_ == 0:
                cx.wait("tensor", "S", zt_done)
                r_banks[0].gate(cx, "tensor", 0)
                r_banks[1].gate(cx, "tensor", 0)
            st, sp = (e_ == 0), (e_ == 7)
            cx.op("tensor", lambda e, col=col, st=st, sp=sp: e.matmul(
                out=pb[0][:], lhsT=ones[:], rhs=zt[:, col:col + 512],
                start=st, stop=sp), inc=sp)
            cx.wait("tensor", "S", scast)
            cx.op("tensor", lambda e, e_=e_, st=st, sp=sp: e.matmul(
                out=pb[1][:], lhsT=ones[:], rhs=sqb[:, e_ * 512:(e_ + 1) * 512],
                start=st, stop=sp), inc=sp)
            if sp:
                r_sq.record(0, cx.mark("tensor"))
        r_stat.gate(cx, "vector", 0)
        cx.wait("vector", "T")
        ln_stats_combine(cx, pb[0][:], pb[1][:], mu_s[:], rstd_s[:],
                         tmp1[:], tmp2[:], 1.0 / D)
        r_banks[0].record(0, cx.mark("vector"))
        r_banks[1].record(0, cx.mark("vector"))
        rstd2_done = cx.val("S")
        cx.wait("vector", "S", rstd2_done)
        for c in range(DC):
            par = xload % 2
            r_xf.gate(cx, "sync", xload)
            sx.dma(xload, xf[:, par * 512:(par + 1) * 512],
                   xT[c * 128:(c + 1) * 128, ts:ts + 512])
            cx.op("vector", lambda e, c=c: e.tensor_tensor(
                out=ftmp[:], in0=zt[:, c * 512:(c + 1) * 512],
                in1=mu_s[:], op=ALU.subtract))
            cx.wait("vector", "V")
            cx.op("vector", lambda e: e.tensor_tensor(
                out=ftmp2[:], in0=ftmp[:], in1=rstd_s[:], op=ALU.mult))
            sx.wait("vector", xload)
            opar = ostore % 2
            r_outf.gate(cx, "vector", ostore)
            cx.wait("vector", "V")
            cx.op("vector", lambda e, par=par, opar=opar: e.tensor_tensor(
                out=outf[:, opar * 512:(opar + 1) * 512], in0=ftmp2[:],
                in1=xf[:, par * 512:(par + 1) * 512], op=ALU.add))
            r_xf.record(xload, cx.mark("vector"))
            vadd = cx.val("V")
            cx.wait("sync", "V", vadd)
            so.dma(ostore, yT[c * 128:(c + 1) * 128, ts:ts + 512],
                   outf[:, opar * 512:(opar + 1) * 512])
            r_outf.record(ostore, so.mark(ostore))
            xload += 1
            ostore += 1
        r_zt.record(0, {"V": cx.val("V"), "S": cx.val("S"), "T": cx.val("T")})
        r_stat.record(0, {"V": cx.val("V"), "S": cx.val("S")})
    so.drain()


def build_l0_attn(T=4096):
    nc = bass.Bass("TRN2", target_bir_lowering=False)
    xT = nc.dram_tensor("xT", [D, T], F32, kind="ExternalInput").ap()
    wqkvT = nc.dram_tensor("wqkvT", [128, 24 * 1024], F32, kind="ExternalInput").ap()
    wprojT = nc.dram_tensor("wprojT", [128, 8 * 1024], F32, kind="ExternalInput").ap()
    bqkv = nc.dram_tensor("bqkv", [128, 24], F32, kind="ExternalInput").ap()
    bproj = nc.dram_tensor("bproj", [128, 8], F32, kind="ExternalInput").ap()
    qn = nc.dram_tensor("qn", [128, 1], F32, kind="ExternalInput").ap()
    kn = nc.dram_tensor("kn", [128, 1], F32, kind="ExternalInput").ap()
    i2_in = nc.dram_tensor("i2_in", [128, 2], BF16, kind="ExternalInput").ap()
    i2t_in = nc.dram_tensor("i2t_in", [128, 128], BF16, kind="ExternalInput").ap()
    ident_in = nc.dram_tensor("ident_in", [128, 128], BF16, kind="ExternalInput").ap()
    yT = nc.dram_tensor("yT", [D, T], F32, kind="ExternalOutput").ap()

    es = ExitStack()
    cx = Ctx(nc, es)
    ones = cx.sb("ones", [128, 128], BF16)
    identb = cx.sb("identb", [128, 128], BF16)
    i2 = cx.sb("i2", [128, 2], BF16)
    i2t = cx.sb("i2t", [2, 128], BF16)
    epsc = cx.sb("epsc", [128, 1], F32)
    eps6c = cx.sb("eps6c", [128, 1], F32)
    bqkvc = cx.sb("bqkvc", [128, 24], F32)
    bprojc = cx.sb("bprojc", [128, 8], F32)
    qnc = cx.sb("qnc", [128, 1], F32)
    knc = cx.sb("knc", [128, 1], F32)
    cx.eps_col = epsc[:, 0:1]
    pbig = cx.ps("pbig", [128, 2048], F32)
    cx.pb = [cx.ps("pb0", [128, 512]), cx.ps("pb1", [128, 512]),
             pbig[:, 0:512], pbig[:, 512:1024],
             pbig[:, 1024:1536], pbig[:, 1536:2048],
             cx.ps("pb6", [128, 512])]
    cx.pbB = pbig[:, 0:1024]
    cx.pbB2 = pbig[:, 1024:2048]
    cx.pbh = cx.ps("pbh", [128, 1024], BF16)[:]

    sc = Stream(cx, "const", nbuf=8)
    for i, (dst, srcap) in enumerate([(bqkvc, bqkv), (bprojc, bproj), (qnc, qn),
                                      (knc, kn), (identb, ident_in),
                                      (i2, i2_in)]):
        sc.dma(i, dst[:], srcap)
    sc.dma(6, i2t[:], i2t_in[0:2, :])
    cx.op("vector", lambda e: e.memset(ones[:], 1.0))
    cx.op("vector", lambda e: e.memset(epsc[:], LN_EPS))
    cx.op("vector", lambda e: e.memset(eps6c[:], 1e-6))
    consts_v = cx.val("V")
    sc.drain("vector")
    sc.drain("scalar")
    sc.drain("tensor")
    sc.drain("sync")
    cx.wait("tensor", "V", consts_v)
    cx.wait("scalar", "V", consts_v)
    cx.wait("vector", "V", consts_v)

    consts = {"ones": ones[:], "identb": identb[:], "i2": i2[:], "i2t": i2t[:],
              "eps6": eps6c[:], "ones1": ones[0:1, :]}
    emit_l0_attn(cx, T, xT, wqkvT, wprojT, bqkvc[:], bprojc[:],
                 qnc[:, 0:1], knc[:, 0:1], yT, consts)

    with es, nc.Block() as block:
        cx.finish(block)
    return nc



LF_DEPTH = 6
LF_FI = 2730
LF_NA = 22


def emit_l0_ffn(cx, T, xT, Wqkv6, Wo6, W16, W26, yT, consts, depth=LF_DEPTH):
    ones = consts["ones"]
    identb = consts["identb"]
    P64 = consts["p64"]
    NTC = T // 512
    NW = T // 256
    TP = T + 1

    xr = cx.sb("lf_xr", [128, DC * TP], BF16)
    share = cx.sb("lf_share", [128, 26720], BF16)
    kaug = share[0:65, 0:16 * 768]
    qaug = share[0:65, 16 * 768:16 * 768 + 16 * 512]
    vb = share[:, 20480:20480 + 6 * 1040]
    SLOTW = 22 * 128
    wslab = cx.sb("lf_wslab", [128, 4 * SLOTW], BF16)
    un = cx.sb("lf_un", [128, 8192], BF16)
    rotb = un[:, 0:2048]
    qkt = un[:, 2048:4096]
    pt = cx.sb("lf_pt", [128, 1024], BF16)
    sqb = cx.sb("lf_sqb", [128, 512], BF16)
    stt2 = cx.sb("lf_stt", [128, 512], F32)
    r_sqb = RBuf(1)
    oT = un[:, 4096:8192]
    xn = un[:, 0:8192]
    dbc = cx.sb("lf_dbc", [128, 512], BF16)
    mng = cx.sb("lf_mng", [128, 8], BF16)
    mnf = cx.sb("lf_mnf", [128, 1], F32)
    ztmp = cx.sb("lf_ztmp", [128, 2 * 512], BF16)
    sqb = cx.sb("lf_sq", [128, 4 * 512], BF16)
    xf = cx.sb("lf_xf", [128, 2 * 512], F32)
    mu_s = cx.sb("lf_mu", [128, 512], BF16)
    rstd_s = cx.sb("lf_rstd", [128, 512], BF16)
    tmp1 = cx.sb("lf_tmp1", [128, 512], F32)
    tmp2 = cx.sb("lf_tmp2", [128, 512], F32)
    ftmp = tmp1
    ftmp2 = tmp2
    outf = cx.sb("lf_outf", [128, 512], F32)
    ha = cx.sb("lf_ha", [128, 4 * 512], BF16)

    pb = cx.pb
    pbB = cx.pbB
    pbh = cx.pbh

    sx = Stream(cx, "lfx")
    sw = Stream(cx, "lfw", nbuf=4, alt=True)
    sk = Stream(cx, "lfk")
    srt = Stream(cx, "lfr")
    so = Stream(cx, "lfo")

    r_xf = RBuf(2)
    r_wf = RBuf(2)
    r_wslab = RBuf(2)
    r_wo = RBuf(1)
    r_banks = [RBuf(1) for _ in range(8)]
    r_pbB = RBuf(1)
    r_banks[2] = r_pbB
    r_banks[3] = r_pbB
    r_pbh = RBuf(1)
    r_share = RBuf(1)
    r_qkt = RBuf(1)
    r_qraw = RBuf(2)
    r_qfin = RBuf(2)
    r_pt = RBuf(1)
    r_oT = RBuf(1)
    r_rot = RBuf(1)
    r_mng = RBuf(4)
    r_dbc = RBuf(2)
    r_stat = RBuf(1)
    r_xn = RBuf(1)
    r_sq = RBuf(4)
    r_ha = RBuf(1)
    r_un = RBuf(1)
    r_ztmp = RBuf(2)
    r_outf = RBuf(1)
    r_xr = RBuf(1)

    xload = wload = kcopy = ostore = rtload = 0
    eps_col = cx.eps_col
    epsl2 = consts["epsl2"]

    def xr_view(c, ts, w, shifted):
        off = 0 if (shifted and c >= 4) else 1
        base = c * TP + off + ts
        return xr[:, base:base + w]

    def load_slab(src2d):
        nonlocal wload
        par = wload % 4
        slot = par * SLOTW
        w = src2d.shape[1]
        q = sw.queue(wload)
        r_wslab.gate(cx, q, wload)
        sw.dma(wload, wslab[:, slot:slot + w], src2d)
        sw.wait("tensor", wload)
        idx = wload
        wload += 1
        return slot, idx, 0

    for j in range(NTC):
        ts = j * 512
        for c in range(DC):
            par = xload % 2
            r_xf.gate(cx, "sync", xload)
            sx.dma(xload, xf[:, par * 512:(par + 1) * 512],
                   xT[c * 128:(c + 1) * 128, ts:ts + 512])
            sx.wait("vector", xload)
            cx.op("vector", lambda e, c=c, par=par, ts=ts: e.tensor_copy(
                out=xr_view(c, ts, 512, False), in_=xf[:, par * 512:(par + 1) * 512]))
            r_xf.record(xload, cx.mark("vector"))
            vcast = cx.val("V")
            cx.wait("scalar", "V", vcast)
            r_sq.gate(cx, "scalar", c)
            cx.op("scalar", lambda e, c=c, ts=ts: e.activation(
                out=sqb[:, (c % 4) * 512:(c % 4 + 1) * 512],
                in_=xr_view(c, ts, 512, False), func=AF.Square))
            scast = cx.val("S")
            cx.wait("tensor", "V", vcast)
            if c == 0:
                r_banks[0].gate(cx, "tensor", 0)
                r_banks[1].gate(cx, "tensor", 0)
            st, sp = (c == 0), (c == DC - 1)
            cx.op("tensor", lambda e, c=c, ts=ts, st=st, sp=sp: e.matmul(
                out=pb[0][:], lhsT=ones[:], rhs=xr_view(c, ts, 512, False),
                start=st, stop=sp), inc=sp)
            cx.wait("tensor", "S", scast)
            cx.op("tensor", lambda e, c=c, st=st, sp=sp: e.matmul(
                out=pb[1][:], lhsT=ones[:], rhs=sqb[:, (c % 4) * 512:(c % 4 + 1) * 512],
                start=st, stop=sp), inc=True)
            r_sq.record(c, cx.mark("tensor"))
            xload += 1
        r_stat.gate(cx, "vector", 0)
        cx.wait("vector", "T")
        ln_stats_combine(cx, pb[0][:], pb[1][:], mu_s[:], rstd_s[:],
                         tmp1[:], tmp2[:], 1.0 / D)
        r_banks[0].record(0, cx.mark("vector"))
        r_banks[1].record(0, cx.mark("vector"))
        rstd_done = cx.val("S")
        cx.wait("vector", "S", rstd_done)
        for c in range(DC):
            cx.op("vector", lambda e, c=c, ts=ts: e.tensor_tensor(
                out=xr_view(c, ts, 512, False), in0=xr_view(c, ts, 512, False),
                in1=mu_s[:], op=ALU.subtract))
            cx.wait("vector", "V")
            cx.op("vector", lambda e, c=c, ts=ts: e.tensor_tensor(
                out=xr_view(c, ts, 512, False), in0=xr_view(c, ts, 512, False),
                in1=rstd_s[:], op=ALU.mult))
        r_stat.record(0, {"V": cx.val("V"), "S": cx.val("S")})
    for c in range(DC):
        cx.op("vector", lambda e, c=c: e.memset(xr[:, c * TP:c * TP + 1], 0.0))
    xr_ready = cx.val("V")
    cx.wait("tensor", "V", xr_ready)
    cx.wait("scalar", "V", xr_ready)

    for L in range(depth):
        r_share.gate(cx, "vector", 0)
        cx.op("vector", lambda e: e.memset(share[:, 0:16 * 768], 0.0))
        cx.wait("vector", "V")
        cx.op("vector", lambda e: e.memset(kaug[64:65, :], 1.0))
        cx.wait("vector", "V")
        cx.op("vector", lambda e: e.memset(vb, 1.0))
        prep_v = cx.val("V")
        cx.wait("tensor", "V", prep_v)

        for jt in range(NTC):
            ts = jt * 512
            if jt == 0:
                r_un.gate(cx, "vector", 0)
                r_un.gate(cx, "scalar", 0)
                r_un.gate(cx, "sync", 0)
            r_rot.gate(cx, "sync", 0)
            for i, tab in enumerate((consts["csq"], consts["ssq"],
                                     consts["csk"], consts["ssk"])):
                srt.dma(rtload + i, rotb[:, i * 512:(i + 1) * 512],
                        tab[:, ts:ts + 512])
            rot_marks = {}
            for i, n in enumerate(srt.names):
                rot_marks[n] = 16 * srt.counts[i]
            rtload += 4
            cx.wait_marks("vector", rot_marks)

            pair_info = {}
            c1 = c2 = c3 = None

            for sec in range(3):
                for fc in range(8):
                    gidx = sec * 8 + fc
                    if gidx % 2 == 0:
                        pslot, pwidx, pwcast = load_slab(
                            Wqkv6[L][:, gidx * 1024:(gidx + 2) * 1024])
                        pair_info = {"slot": pslot, "widx": pwidx}
                        cx.wait("tensor", "V", xr_ready)
                    slot = pair_info["slot"] + (gidx % 2) * 1024
                    bank = pb[2 + fc % 2]
                    r_banks[2 + fc % 2].gate(cx, "tensor", 0)
                    for k in range(DC):
                        cx.op("tensor", lambda e, k=k, bank=bank, slot=slot,
                              ts=ts: e.matmul(
                            out=bank[:], lhsT=wslab[:, slot + k * 128:slot + (k + 1) * 128],
                            rhs=xr_view(k, ts, 512, True),
                            start=(k == 0), stop=(k == DC - 1)), inc=(k == DC - 1))
                    if gidx % 2 == 1:
                        r_wslab.record(pair_info["widx"], cx.mark("tensor"))
                    prod_t = cx.val("T")
                    h0, h1 = 2 * fc, 2 * fc + 1
                    qoff = (gidx % 2) * 1024
                    ent = {"hsum": None, "l2v": None, "ev": None}
                    if sec < 2:
                        def p1(sec=sec, fc=fc, bank=bank, prod_t=prod_t,
                               qoff=qoff, ent=ent):
                            cx.wait("scalar", "T", prod_t)
                            r_qraw.gate(cx, "scalar", qoff // 1024)
                            cx.op("scalar", lambda e: e.activation(
                                out=qkt[:, qoff:qoff + 512], in_=bank[:],
                                func=AF.Identity))
                            r_banks[2 + fc % 2].record(0, cx.mark("scalar"))
                            r_sqb.gate(cx, "scalar", 0)
                            cx.wait("scalar", "S")
                            cx.op("scalar", lambda e: e.activation(
                                out=sqb[:, 0:512], in_=qkt[:, qoff:qoff + 512],
                                func=AF.Square))
                            sq_done = cx.val("S")
                            cx.wait("tensor", "S", sq_done)
                            r_banks[4].gate(cx, "tensor", 0)
                            cx.op("tensor", lambda e: e.matmul(
                                out=pb[4][0:2, :], lhsT=consts["i2"], rhs=sqb[:, 0:512],
                                start=True, stop=True), inc=True)
                            r_sqb.record(0, cx.mark("tensor"))
                            ent["hsum"] = cx.val("T")

                        def p2(sec=sec, qoff=qoff, ent=ent):
                            cx.wait("scalar", "T", ent["hsum"])
                            cx.op("scalar", lambda e: e.activation(
                                out=stt2[0:2, :], in_=pb[4][0:2, :], func=AF.Ln,
                                bias=epsl2[0:2, 0:1]))
                            r_banks[4].record(0, cx.mark("scalar"))
                            r_dbc.gate(cx, "scalar", 0)
                            cx.wait("scalar", "S")
                            cx.op("scalar", lambda e: e.activation(
                                out=dbc[0:2, 0:512], in_=stt2[0:2, :], func=AF.Exp,
                                scale=-0.5))
                            bc_s = cx.val("S")
                            cx.wait("tensor", "S", bc_s)
                            r_banks[5].gate(cx, "tensor", 0)
                            cx.op("tensor", lambda e: e.matmul(
                                out=pb[5][:], lhsT=consts["i2t"], rhs=dbc[0:2, 0:512],
                                start=True, stop=True), inc=True)
                            r_dbc.record(0, cx.mark("tensor"))
                            bc_t = cx.val("T")
                            r_qfin.gate(cx, "vector", qoff // 1024)
                            cx.wait("vector", "T", bc_t)
                            cx.op("vector", lambda e: e.tensor_tensor(
                                out=qkt[:, qoff + 512:qoff + 1024],
                                in0=qkt[:, qoff:qoff + 512], in1=pb[5][:],
                                op=ALU.mult))
                            r_banks[5].record(0, cx.mark("vector"))
                            r_qraw.record(qoff // 1024, cx.mark("vector"))
                            cx.wait("vector", "V")
                            scol = consts["qs6"][:, L:L + 1] if sec == 0 else consts["ks6"][:, L:L + 1]
                            xsc = 8.0 if sec == 0 else 1.0
                            cx.op("vector", lambda e, scol=scol, xsc=xsc: e.tensor_scalar(
                                out=qkt[:, qoff + 512:qoff + 1024],
                                in0=qkt[:, qoff + 512:qoff + 1024],
                                scalar1=scol, scalar2=xsc, op0=ALU.mult, op1=ALU.mult))
                            ent["l2v"] = cx.val("V")

                        def p3(sec=sec, h0=h0, h1=h1, qoff=qoff, ent=ent):
                            cx.wait("tensor", "V", ent["l2v"])
                            r_banks[6].gate(cx, "tensor", 0)
                            cx.op("tensor", lambda e: e.matmul(
                                out=pb[6][:], lhsT=P64, rhs=qkt[:, qoff + 512:qoff + 1024],
                                start=True, stop=True), inc=True)
                            perm_t = cx.val("T")
                            cs_off = 0 if sec == 0 else 1024
                            cx.wait("vector", "T", perm_t)
                            cx.op("vector", lambda e, cs_off=cs_off: e.tensor_tensor(
                                out=tmp1[:], in0=qkt[:, qoff + 512:qoff + 1024],
                                in1=rotb[:, cs_off:cs_off + 512], op=ALU.mult))
                            cx.wait("vector", "V")
                            cx.op("vector", lambda e, cs_off=cs_off: e.tensor_tensor(
                                out=tmp2[:], in0=pb[6][:],
                                in1=rotb[:, cs_off + 512:cs_off + 1024], op=ALU.mult))
                            r_banks[6].record(0, cx.mark("vector"))
                            cx.wait("vector", "V")
                            cx.op("vector", lambda e: e.tensor_tensor(
                                out=qkt[:, qoff + 512:qoff + 1024], in0=tmp1[:],
                                in1=tmp2[:], op=ALU.add))
                            rot_v = cx.val("V")
                            cx.wait("vector", "V", rot_v)
                            nonlocal kcopy
                            if sec == 0:
                                c0 = h0 * 512
                                cc1 = h1 * 512
                                cx.op("vector", lambda e, c0=c0: e.tensor_copy(
                                    out=qaug[0:64, c0:c0 + 512],
                                    in_=qkt[0:64, qoff + 512:qoff + 1024]))
                                cx.wait("sync", "V", cx.val("V"))
                                sk.dma(kcopy, qaug[0:64, cc1:cc1 + 512],
                                       qkt[64:128, qoff + 512:qoff + 1024])
                                r_qfin.record(qoff // 1024, sk.mark(kcopy))
                                kcopy += 1
                            else:
                                c0 = h0 * 768 + 256
                                cc1 = h1 * 768 + 256
                                cx.op("vector", lambda e, c0=c0: e.tensor_copy(
                                    out=kaug[0:64, c0:c0 + 512],
                                    in_=qkt[0:64, qoff + 512:qoff + 1024]))
                                cx.wait("sync", "V", cx.val("V"))
                                sk.dma(kcopy, kaug[0:64, cc1:cc1 + 512],
                                       qkt[64:128, qoff + 512:qoff + 1024])
                                r_qfin.record(qoff // 1024, sk.mark(kcopy))
                                kcopy += 1
                        newchain = {"p1": p1, "p2": p2, "p3": p3}
                    else:
                        def vp1(fc=fc, bank=bank, prod_t=prod_t, qoff=qoff, ent=ent):
                            cx.wait("scalar", "T", prod_t)
                            r_qraw.gate(cx, "scalar", qoff // 1024)
                            cx.op("scalar", lambda e: e.activation(
                                out=qkt[:, qoff:qoff + 512], in_=bank[:],
                                func=AF.Identity))
                            r_banks[2 + fc % 2].record(0, cx.mark("scalar"))
                            ent["ev"] = cx.val("S")

                        def vp2(h0=h0, h1=h1, qoff=qoff, ent=ent):
                            cx.wait("tensor", "S", ent["ev"])
                            r_pbh.gate(cx, "tensor", 0)
                            for tt in range(4):
                                cx.op("tensor", lambda e, tt=tt: e.transpose(
                                    out=pbh[:, tt * 128:(tt + 1) * 128],
                                    in_=qkt[:, qoff + tt * 128:qoff + tt * 128 + 128],
                                    identity=identb[:]), inc=(tt == 3))
                            r_qraw.record(qoff // 1024, cx.mark("tensor"))
                            tp_done = cx.val("T")
                            cx.wait("vector", "T", tp_done)
                            for tt in range(4):
                                tile = 2 + tt
                                c0 = tile * 1040 + h0 * 65
                                cc1 = tile * 1040 + h1 * 65
                                cx.op("vector", lambda e, tt=tt, c0=c0: e.tensor_copy(
                                    out=vb[:, c0:c0 + 64],
                                    in_=pbh[:, tt * 128:tt * 128 + 64]))
                                cx.op("vector", lambda e, tt=tt, cc1=cc1: e.tensor_copy(
                                    out=vb[:, cc1:cc1 + 64],
                                    in_=pbh[:, tt * 128 + 64:tt * 128 + 128]))
                            r_pbh.record(0, cx.mark("vector"))
                        newchain = {"p1": vp1, "p2": vp2, "p3": None}
                    if c3 is not None:
                        if c3["p3"] is not None:
                            c3["p3"]()
                        c3 = None
                    if c2 is not None:
                        c2["p2"]()
                        c3 = c2
                        c2 = None
                    if c1 is not None:
                        c1["p1"]()
                        c2 = c1
                        c1 = None
                    c1 = newchain
            for _ in range(3):
                if c3 is not None and c3["p3"] is not None:
                    c3["p3"]()
                c3 = None
                if c2 is not None:
                    c2["p2"]()
                    c3 = c2
                    c2 = None
                if c1 is not None:
                    c1["p1"]()
                    c2 = c1
                    c1 = None
            qkv_marks = {"V": cx.val("V"), "S": cx.val("S")}
            for i, n in enumerate(sk.names):
                qkv_marks[n] = 16 * sk.counts[i]
            cx.wait_marks("tensor", qkv_marks)

            pend = []

            def pa_tail():
                hlp, ttp, mng_v = pend.pop(0)
                cx.wait("tensor", "V", mng_v)
                r_pbh.gate(cx, "tensor", 0)
                cx.op("tensor", lambda e, ttp=ttp: e.transpose(
                    out=pbh[0:1, 0:128], in_=mng[:, ttp:ttp + 1],
                    identity=identb[:]), inc=True)
                tr_done = cx.val("T")
                cx.wait("vector", "T", tr_done)
                cx.op("vector", lambda e, hlp=hlp, ttp=ttp: e.tensor_copy(
                    out=qaug[64:65, hlp * 512 + ttp * 128:hlp * 512 + (ttp + 1) * 128],
                    in_=pbh[0:1, 0:128]))
                r_pbh.record(0, cx.mark("vector"))
                r_mng.record(ttp, cx.mark("vector"))

            for hl in range(16):
                for tt in range(4):
                    w = tt // 2
                    bank = pb[2 + tt % 2]
                    r_banks[2 + tt % 2].gate(cx, "tensor", 0)
                    cx.op("tensor", lambda e, hl=hl, tt=tt, w=w, bank=bank: e.matmul(
                        out=bank[:],
                        lhsT=qaug[0:64, hl * 512 + tt * 128:hl * 512 + (tt + 1) * 128],
                        rhs=kaug[0:64, hl * 768 + w * 256:hl * 768 + w * 256 + 512],
                        start=True, stop=False), inc=False)
                    blk0 = (jt == 0 and w == 0)
                    mvar = consts["maskA0"] if blk0 else consts["maskA"]
                    cx.op("tensor", lambda e, tt=tt, bank=bank, mvar=mvar: e.matmul(
                        out=bank[:], lhsT=identb[:],
                        rhs=mvar[:, (tt % 2) * 512:(tt % 2) * 512 + 512],
                        start=False, stop=True), inc=True)
                    smm = cx.val("T")
                    if pend:
                        pa_tail()
                    cx.wait("vector", "T", smm)
                    r_mng.gate(cx, "vector", tt)
                    cx.op("vector", lambda e, bank=bank, tt=tt: e.reduce_max(
                        out=mng[:, tt:tt + 1], in_=bank[:], axis=AX.X, negate=True))
                    cx.wait("vector", "V")
                    r_banks[2 + tt % 2].record(0, cx.mark("vector"))
                    pend.append((hl, tt, cx.val("V")))
            while pend:
                pa_tail()
            qrow_v = cx.val("V")
            cx.wait("tensor", "V", qrow_v)

            units = [(hl, w) for hl in range(16) for w in range(2)]
            uinfo = {}

            def unit_head(i):
                hl, w = units[i]
                blk0 = (jt == 0 and w == 0)
                scs = (2, 3) if blk0 else (0, 1, 2, 3)
                e_off = 512 if blk0 else 0
                buf = cx.pbB if i % 2 == 0 else cx.pbB2
                bks = (2, 3) if i % 2 == 0 else (4, 5)
                for bk in bks:
                    r_banks[bk].gate(cx, "tensor", 0)
                for sc in scs:
                    st = (sc == scs[0]) or (sc == 2)
                    cx.op("tensor", lambda e, hl=hl, w=w, sc=sc, st=st, buf=buf:
                          e.matmul(
                        out=buf[:, sc * 256:(sc + 1) * 256],
                        lhsT=kaug[:, hl * 768 + w * 256 + sc * 128:hl * 768 + w * 256 + (sc + 1) * 128],
                        rhs=qaug[:, hl * 512 + w * 256:hl * 512 + w * 256 + 256],
                        start=st, stop=False), inc=False)
                for mh in range(e_off // 512, 2):
                    cx.op("tensor", lambda e, mh=mh, buf=buf: e.matmul(
                        out=buf[:, mh * 512:(mh + 1) * 512], lhsT=identb[:],
                        rhs=consts["maskB"][:, mh * 512:(mh + 1) * 512],
                        start=False, stop=True), inc=(mh == 1))
                uinfo[i] = (cx.val("T"), scs, e_off, buf, bks)

            def unit_expav(i):
                hl, w = units[i]
                st_mm, scs, e_off, buf, bks = uinfo[i]
                avb = pb[6] if i % 2 == 0 else pb[0]
                cx.wait("scalar", "T", st_mm)
                r_pt.gate(cx, "scalar", 0)
                cx.op("scalar", lambda e, e_off=e_off, buf=buf, i=i: e.activation(
                    out=pt[:, e_off:1024], in_=buf[:, e_off:1024], func=AF.Exp))
                for bk in bks:
                    r_banks[bk].record(0, cx.mark("scalar"))
                exp_s = cx.val("S")
                cx.wait("tensor", "S", exp_s)
                r_banks[6 if i % 2 == 0 else 0].gate(cx, "tensor", 0)
                for sc in scs:
                    tile = w * 2 + sc
                    st, sp = (sc == scs[0]), (sc == scs[-1])
                    cx.op("tensor", lambda e, sc=sc, tile=tile, hl=hl, st=st,
                          sp=sp, avb=avb: e.matmul(
                        out=avb[0:65, 0:256],
                        lhsT=vb[:, tile * 1040 + hl * 65:tile * 1040 + (hl + 1) * 65],
                        rhs=pt[:, sc * 256:(sc + 1) * 256],
                        start=st, stop=sp), inc=sp)
                av_mm = cx.val("T")
                r_pt.record(0, {"T": av_mm})
                uinfo[i] = (av_mm, avb)
                cx.wait("vector", "T", av_mm)
                r_dbc.gate(cx, "vector", i)
                cx.op("vector", lambda e, avb=avb, i=i: e.reciprocal(
                    out=tmp1[64:65, (i % 2) * 256:(i % 2) * 256 + 256],
                    in_=avb[64:65, 0:256]))
                cx.wait("vector", "V")
                cx.op("vector", lambda e, i=i: e.tensor_copy(
                    out=dbc[64:65, (i % 2) * 256:(i % 2) * 256 + 256],
                    in_=tmp1[64:65, (i % 2) * 256:(i % 2) * 256 + 256]))
                uinfo[i] = (av_mm, avb, cx.val("V"))

            def unit_norm(i):
                hl, w = units[i]
                av_mm, avb, rec_v = uinfo[i]
                dof = (i % 2) * 256
                cx.wait("tensor", "V", rec_v)
                r_banks[1].gate(cx, "tensor", 0)
                cx.op("tensor", lambda e, dof=dof: e.matmul(
                    out=pb[1][:, 0:256], lhsT=ones[64:65, :],
                    rhs=dbc[64:65, dof:dof + 256],
                    start=True, stop=True), inc=True)
                r_dbc.record(i, cx.mark("tensor"))
                bc_t = cx.val("T")
                cx.wait("scalar", "T", bc_t)
                r_dbc.gate(cx, "scalar", i)
                cx.op("scalar", lambda e, dof=dof: e.activation(
                    out=dbc[:, dof:dof + 256], in_=pb[1][:, 0:256], func=AF.Identity))
                r_banks[1].record(0, cx.mark("scalar"))
                dbc_s = cx.val("S")
                cx.wait("vector", "S", dbc_s)
                pbase = (hl % 2) * 64
                ocol = (hl // 2) * 512 + w * 256
                if i == 0:
                    r_oT.gate(cx, "vector", 0)
                cx.op("vector", lambda e, pbase=pbase, ocol=ocol, avb=avb,
                      dof=dof: e.tensor_tensor(
                    out=oT[pbase:pbase + 64, ocol:ocol + 256],
                    in0=avb[0:64, 0:256], in1=dbc[pbase:pbase + 64, dof:dof + 256],
                    op=ALU.mult))
                r_banks[6 if i % 2 == 0 else 0].record(0, cx.mark("vector"))
                r_dbc.record(i, cx.mark("vector"))

            for i in range(len(units)):
                unit_head(i)
                if i >= 1:
                    unit_expav(i - 1)
                if i >= 2:
                    unit_norm(i - 2)
            unit_expav(len(units) - 1)
            unit_norm(len(units) - 2)
            unit_norm(len(units) - 1)
            oT_v = cx.val("V")
            cx.wait("tensor", "V", oT_v)

            for fp in range(4):
                slot, widx, wcast = load_slab(Wo6[L][:, fp * 2048:(fp + 1) * 2048])
                cx.wait("tensor", "S", wcast)
                for fh in range(2):
                    f = fp * 2 + fh
                    bank = pb[2 + f % 2]
                    r_banks[2 + f % 2].gate(cx, "tensor", 0)
                    for k in range(DC):
                        cx.op("tensor", lambda e, fh=fh, k=k, bank=bank, slot=slot:
                              e.matmul(
                            out=bank[:],
                            lhsT=wslab[:, slot + fh * 1024 + k * 128:slot + fh * 1024 + (k + 1) * 128],
                            rhs=oT[:, k * 512:(k + 1) * 512],
                            start=(k == 0), stop=(k == DC - 1)), inc=(k == DC - 1))
                    cx.wait("vector", "T")
                    r_xr.gate(cx, "vector", 0)
                    cx.op("vector", lambda e, f=f, ts=ts, bank=bank: e.tensor_tensor(
                        out=xr_view(f, ts, 512, False), in0=xr_view(f, ts, 512, False),
                        in1=bank[:], op=ALU.add))
                    r_banks[2 + f % 2].record(0, cx.mark("vector"))
                r_wslab.record(widx, cx.mark("tensor"))
            r_oT.record(0, cx.mark("tensor"))
            xr_ready = cx.val("V")
            cx.wait("tensor", "V", xr_ready)
            cx.wait("scalar", "V", xr_ready)

            if jt < NTC - 1:
                tail_t = cx.val("T")
                cx.wait("vector", "T", tail_t)
                for hl in range(16):
                    cx.op("vector", lambda e, hl=hl: e.tensor_copy(
                        out=kaug[0:64, hl * 768:hl * 768 + 256],
                        in_=kaug[0:64, hl * 768 + 512:hl * 768 + 768]))
                cx.op("vector", lambda e: e.tensor_copy(
                    out=vb[:, 0:2 * 1040], in_=vb[:, 4 * 1040:6 * 1040]))
                tails_v = cx.val("V")
                cx.wait("tensor", "V", tails_v)
        r_share.record(0, {"T": cx.val("T"), "V": cx.val("V"), "S": cx.val("S")})
        r_un.record(0, {"T": cx.val("T"), "V": cx.val("V"), "S": cx.val("S")})

        hbuf = share[:, 0:LF_NA * 1024]
        r_share.gate(cx, "scalar", 0)
        r_share.gate(cx, "vector", 0)
        cx.op("vector", lambda e: e.memset(hbuf, 0.0))
        cx.wait("vector", "V")
        for tb in range(T // 1024):
            t0 = tb * 1024
            r_un.gate(cx, "vector", 0)
            for sub in range(2):
                ts = t0 + sub * 512
                for c in range(DC):
                    r_sq.gate(cx, "scalar", c)
                    cx.op("scalar", lambda e, c=c, ts=ts: e.activation(
                        out=sqb[:, (c % 4) * 512:(c % 4 + 1) * 512],
                        in_=xr_view(c, ts, 512, True), func=AF.Square))
                    scast = cx.val("S")
                    if c == 0:
                        r_banks[0].gate(cx, "tensor", 0)
                        r_banks[1].gate(cx, "tensor", 0)
                    st, sp = (c == 0), (c == DC - 1)
                    cx.op("tensor", lambda e, c=c, ts=ts, st=st, sp=sp: e.matmul(
                        out=pb[0][:], lhsT=ones[:], rhs=xr_view(c, ts, 512, True),
                        start=st, stop=sp), inc=sp)
                    cx.wait("tensor", "S", scast)
                    cx.op("tensor", lambda e, c=c, st=st, sp=sp: e.matmul(
                        out=pb[1][:], lhsT=ones[:], rhs=sqb[:, (c % 4) * 512:(c % 4 + 1) * 512],
                        start=st, stop=sp), inc=True)
                    r_sq.record(c, cx.mark("tensor"))
                r_stat.gate(cx, "vector", 0)
                cx.wait("vector", "T")
                ln_stats_combine(cx, pb[0][:], pb[1][:], mu_s[:], rstd_s[:],
                                 tmp1[:], tmp2[:], 1.0 / D)
                r_banks[0].record(0, cx.mark("vector"))
                r_banks[1].record(0, cx.mark("vector"))
                rstd_done = cx.val("S")
                cx.wait("vector", "S", rstd_done)
                for c in range(DC):
                    col = c * 1024 + sub * 512
                    cx.op("vector", lambda e, c=c, ts=ts, col=col: e.tensor_tensor(
                        out=xn[:, col:col + 512], in0=xr_view(c, ts, 512, True),
                        in1=mu_s[:], op=ALU.subtract))
                    cx.wait("vector", "V")
                    cx.op("vector", lambda e, col=col: e.tensor_tensor(
                        out=xn[:, col:col + 512], in0=xn[:, col:col + 512],
                        in1=rstd_s[:], op=ALU.mult))
                    cx.wait("vector", "V")
                    cx.op("vector", lambda e, col=col, c=c, L=L: e.tensor_scalar(
                        out=xn[:, col:col + 512], in0=xn[:, col:col + 512],
                        scalar1=consts["lng6"][:, L * 8 + c:L * 8 + c + 1],
                        scalar2=consts["lnb6"][:, L * 8 + c:L * 8 + c + 1],
                        op0=ALU.mult, op1=ALU.add))
                r_stat.record(0, {"V": cx.val("V"), "S": cx.val("S")})
            xn_done = cx.val("V")

            for j2 in range(LF_NA):
                aw = 128 if j2 < LF_NA - 1 else LF_FI - 128 * (LF_NA - 1)
                slot, widx, wcast = load_slab(
                    W16[L][:, (2 * j2) * 1024:(2 * j2 + 2) * 1024])
                cx.wait("tensor", "S", wcast)
                cx.wait("tensor", "V", xn_done)
                r_banks[2].gate(cx, "tensor", 0)
                r_banks[3].gate(cx, "tensor", 0)
                for half in range(2):
                    for k in range(DC):
                        cx.op("tensor", lambda e, k=k, slot=slot, half=half: e.matmul(
                            out=pb[2 + half][:],
                            lhsT=wslab[:, slot + k * 128:slot + (k + 1) * 128],
                            rhs=xn[:, k * 1024 + half * 512:k * 1024 + half * 512 + 512],
                            start=(k == 0), stop=(k == DC - 1)), inc=(k == DC - 1))
                cx.wait("scalar", "T")
                r_ha.gate(cx, "scalar", 0)
                for half in range(2):
                    cx.op("scalar", lambda e, aw=aw, j2=j2, L=L, half=half: e.activation(
                        out=ha[0:aw, half * 512:half * 512 + 512],
                        in_=pb[2 + half][0:aw, :], func=AF.Identity,
                        bias=consts["b16"][:, L * 44 + j2:L * 44 + j2 + 1][0:aw, :]))
                    r_banks[2 + half].record(0, cx.mark("scalar"))
                a_ev = cx.val("S")
                gslot = slot + 1024
                r_banks[2].gate(cx, "tensor", 0)
                r_banks[3].gate(cx, "tensor", 0)
                for half in range(2):
                    for k in range(DC):
                        cx.op("tensor", lambda e, k=k, gslot=gslot, half=half: e.matmul(
                            out=pb[2 + half][:],
                            lhsT=wslab[:, gslot + k * 128:gslot + (k + 1) * 128],
                            rhs=xn[:, k * 1024 + half * 512:k * 1024 + half * 512 + 512],
                            start=(k == 0), stop=(k == DC - 1)), inc=(k == DC - 1))
                r_wslab.record(widx, cx.mark("tensor"))
                cx.wait("scalar", "T")
                for half in range(2):
                    cx.op("scalar", lambda e, aw=aw, j2=j2, L=L, half=half: e.activation(
                        out=ha[0:aw, 1024 + half * 512:1024 + half * 512 + 512],
                        in_=pb[2 + half][0:aw, :], func=AF.Gelu,
                        bias=consts["b16"][:, L * 44 + 22 + j2:L * 44 + 22 + j2 + 1][0:aw, :]))
                    r_banks[2 + half].record(0, cx.mark("scalar"))
                hga = cx.val("S")
                cx.wait("vector", "S", hga)
                for half in range(2):
                    cx.op("vector", lambda e, aw=aw, j2=j2, half=half: e.tensor_tensor(
                        out=hbuf[0:aw, j2 * 1024 + half * 512:j2 * 1024 + half * 512 + 512],
                        in0=ha[0:aw, half * 512:half * 512 + 512],
                        in1=ha[0:aw, 1024 + half * 512:1024 + half * 512 + 512],
                        op=ALU.mult))
                r_ha.record(0, cx.mark("vector"))
            h_done = cx.val("V")

            for e_ in range(DC):
                slotA, widxA, wcastA = load_slab(
                    W26[L][:, (e_ * 2) * 1408:(e_ * 2 + 2) * 1408])
                slotB, widxB = slotA + 1408, widxA
                cx.wait("tensor", "S", wcastA)
                cx.wait("tensor", "V", h_done)
                bp = (2, 3) if e_ % 2 == 0 else (4, 5)
                r_banks[bp[0]].gate(cx, "tensor", 0)
                r_banks[bp[1]].gate(cx, "tensor", 0)
                for sub in range(2):
                    for j2 in range(LF_NA):
                        slot = slotA if j2 < 11 else slotB
                        jo = j2 if j2 < 11 else j2 - 11
                        cx.op("tensor", lambda e, jo=jo, j2=j2, slot=slot,
                              sub=sub, bk=bp[sub]: e.matmul(
                            out=pb[bk][:],
                            lhsT=wslab[:, slot + jo * 128:slot + jo * 128 + 128],
                            rhs=hbuf[:, j2 * 1024 + sub * 512:j2 * 1024 + sub * 512 + 512],
                            start=(j2 == 0), stop=(j2 == LF_NA - 1)),
                            inc=(j2 == LF_NA - 1))
                r_wslab.record(widxA, cx.mark("tensor"))
                r_wslab.record(widxB, cx.mark("tensor"))
                cx.wait("scalar", "T")
                for sub in range(2):
                    ts = t0 + sub * 512
                    r_ztmp.gate(cx, "scalar", sub)
                    cx.op("scalar", lambda e, e_=e_, L=L, bk=bp[sub], sub=sub:
                          e.activation(
                        out=ztmp[:, sub * 512:(sub + 1) * 512], in_=pb[bk][:],
                        func=AF.Identity, bias=consts["b26"][:, L * 8 + e_:L * 8 + e_ + 1]))
                    zs = cx.val("S")
                    cx.wait("vector", "S", zs)
                    r_xr.gate(cx, "vector", 0)
                    cx.op("vector", lambda e, e_=e_, ts=ts, sub=sub: e.tensor_tensor(
                        out=xr_view(e_, ts, 512, False), in0=xr_view(e_, ts, 512, False),
                        in1=ztmp[:, sub * 512:(sub + 1) * 512], op=ALU.add))
                    r_ztmp.record(sub, cx.mark("vector"))
                    r_banks[bp[sub]].record(0, cx.mark("vector"))
            xr_ready = cx.val("V")
            cx.wait("tensor", "V", xr_ready)
            cx.wait("scalar", "V", xr_ready)
            r_un.record(0, {"T": cx.val("T"), "V": cx.val("V"), "S": cx.val("S")})
        r_share.record(0, {"T": cx.val("T"), "V": cx.val("V"), "S": cx.val("S")})

    for j in range(NTC):
        ts = j * 512
        for c in range(DC):
            if c == 0:
                cx.wait("scalar", "V", xr_ready)
            r_sq.gate(cx, "scalar", c)
            cx.op("scalar", lambda e, c=c, ts=ts: e.activation(
                out=sqb[:, (c % 4) * 512:(c % 4 + 1) * 512],
                in_=xr_view(c, ts, 512, False), func=AF.Square))
            scast = cx.val("S")
            if c == 0:
                cx.wait("tensor", "V", xr_ready)
                r_banks[0].gate(cx, "tensor", 0)
                r_banks[1].gate(cx, "tensor", 0)
            st, sp = (c == 0), (c == DC - 1)
            cx.op("tensor", lambda e, c=c, ts=ts, st=st, sp=sp: e.matmul(
                out=pb[0][:], lhsT=ones[:], rhs=xr_view(c, ts, 512, False),
                start=st, stop=sp), inc=sp)
            cx.wait("tensor", "S", scast)
            cx.op("tensor", lambda e, c=c, st=st, sp=sp: e.matmul(
                out=pb[1][:], lhsT=ones[:], rhs=sqb[:, (c % 4) * 512:(c % 4 + 1) * 512],
                start=st, stop=sp), inc=True)
            r_sq.record(c, cx.mark("tensor"))
        r_stat.gate(cx, "vector", 0)
        cx.wait("vector", "T")
        ln_stats_combine(cx, pb[0][:], pb[1][:], mu_s[:], rstd_s[:],
                         tmp1[:], tmp2[:], 1.0 / D)
        r_banks[0].record(0, cx.mark("vector"))
        r_banks[1].record(0, cx.mark("vector"))
        rstd2_done = cx.val("S")
        cx.wait("vector", "S", rstd2_done)
        for c in range(DC):
            par = xload % 2
            r_xf.gate(cx, "sync", xload)
            sx.dma(xload, xf[:, par * 512:(par + 1) * 512],
                   xT[c * 128:(c + 1) * 128, ts:ts + 512])
            cx.op("vector", lambda e, c=c, ts=ts: e.tensor_tensor(
                out=ftmp[:], in0=xr_view(c, ts, 512, False), in1=mu_s[:],
                op=ALU.subtract))
            cx.wait("vector", "V")
            cx.op("vector", lambda e: e.tensor_tensor(
                out=ftmp2[:], in0=ftmp[:], in1=rstd_s[:], op=ALU.mult))
            sx.wait("vector", xload)
            r_outf.gate(cx, "vector", 0)
            cx.wait("vector", "V")
            cx.op("vector", lambda e, par=par: e.tensor_tensor(
                out=outf[:], in0=ftmp2[:],
                in1=xf[:, par * 512:(par + 1) * 512], op=ALU.add))
            r_xf.record(xload, cx.mark("vector"))
            vadd = cx.val("V")
            cx.wait("sync", "V", vadd)
            so.dma(ostore, yT[c * 128:(c + 1) * 128, ts:ts + 512], outf[:])
            r_outf.record(0, so.mark(ostore))
            xload += 1
            ostore += 1
        r_stat.record(0, {"V": cx.val("V"), "S": cx.val("S")})
    so.drain()


def build_l0_ffn(T=4096, depth=LF_DEPTH):
    nc = bass.Bass("TRN2", target_bir_lowering=False)
    xT = nc.dram_tensor("xT", [D, T], F32, kind="ExternalInput").ap()
    Wqkv = nc.dram_tensor("Wqkv", [128, LF_DEPTH * 24 * 1024], F32, kind="ExternalInput").ap()
    Wo = nc.dram_tensor("Wo", [128, LF_DEPTH * 8 * 1024], F32, kind="ExternalInput").ap()
    W1 = nc.dram_tensor("W1", [128, LF_DEPTH * 44 * 1024], F32, kind="ExternalInput").ap()
    W2 = nc.dram_tensor("W2", [128, LF_DEPTH * 16 * 1408], F32, kind="ExternalInput").ap()
    qs6 = nc.dram_tensor("qs6", [128, LF_DEPTH], F32, kind="ExternalInput").ap()
    ks6 = nc.dram_tensor("ks6", [128, LF_DEPTH], F32, kind="ExternalInput").ap()
    lng6 = nc.dram_tensor("lng6", [128, LF_DEPTH * 8], F32, kind="ExternalInput").ap()
    lnb6 = nc.dram_tensor("lnb6", [128, LF_DEPTH * 8], F32, kind="ExternalInput").ap()
    b16 = nc.dram_tensor("b16", [128, LF_DEPTH * 44], F32, kind="ExternalInput").ap()
    b26 = nc.dram_tensor("b26", [128, LF_DEPTH * 8], F32, kind="ExternalInput").ap()
    csq = nc.dram_tensor("csq", [128, T], BF16, kind="ExternalInput").ap()
    ssq = nc.dram_tensor("ssq", [128, T], BF16, kind="ExternalInput").ap()
    csk = nc.dram_tensor("csk", [128, T], BF16, kind="ExternalInput").ap()
    ssk = nc.dram_tensor("ssk", [128, T], BF16, kind="ExternalInput").ap()
    maskA = nc.dram_tensor("maskA", [128, 2 * 512], BF16, kind="ExternalInput").ap()
    maskA0 = nc.dram_tensor("maskA0", [128, 2 * 512], BF16, kind="ExternalInput").ap()
    maskB = nc.dram_tensor("maskB", [128, 4 * 256], BF16, kind="ExternalInput").ap()
    p64 = nc.dram_tensor("p64", [128, 128], BF16, kind="ExternalInput").ap()
    i2_in = nc.dram_tensor("i2_in", [128, 2], BF16, kind="ExternalInput").ap()
    i2t_in = nc.dram_tensor("i2t_in", [128, 128], BF16, kind="ExternalInput").ap()
    ident_in = nc.dram_tensor("ident_in", [128, 128], BF16, kind="ExternalInput").ap()
    yT = nc.dram_tensor("yT", [D, T], F32, kind="ExternalOutput").ap()

    es = ExitStack()
    cx = Ctx(nc, es)
    ones = cx.sb("ones", [128, 128], BF16)
    identb = cx.sb("identb", [128, 128], BF16)
    p64b = cx.sb("p64b", [128, 128], BF16)
    i2 = cx.sb("i2", [128, 2], BF16)
    i2t = cx.sb("i2t", [2, 128], BF16)
    epsc = cx.sb("epsc", [128, 1], F32)
    epsl2 = cx.sb("epsl2", [128, 1], F32)
    cqs = cx.sb("cqs", [128, LF_DEPTH], F32)
    cks = cx.sb("cks", [128, LF_DEPTH], F32)
    clng = cx.sb("clng", [128, LF_DEPTH * 8], F32)
    clnb = cx.sb("clnb", [128, LF_DEPTH * 8], F32)
    cb1 = cx.sb("cb1", [128, LF_DEPTH * 44], F32)
    cb2 = cx.sb("cb2", [128, LF_DEPTH * 8], F32)
    cmaskA = cx.sb("cmaskA", [128, 2 * 512], BF16)
    cmaskA0 = cx.sb("cmaskA0", [128, 2 * 512], BF16)
    cmaskB = cx.sb("cmaskB", [128, 4 * 256], BF16)
    cx.eps_col = epsc[:, 0:1]
    pbig = cx.ps("pbig", [128, 2048], F32)
    cx.pb = [cx.ps("pb0", [128, 512]), cx.ps("pb1", [128, 512]),
             pbig[:, 0:512], pbig[:, 512:1024],
             pbig[:, 1024:1536], pbig[:, 1536:2048],
             cx.ps("pb6", [128, 512])]
    cx.pbB = pbig[:, 0:1024]
    cx.pbB2 = pbig[:, 1024:2048]
    cx.pbh = cx.ps("pbh", [128, 1024], BF16)[:]

    sc = Stream(cx, "const", nbuf=4)
    loads = [(i2, i2_in), (identb, ident_in), (p64b, p64), (cqs, qs6),
             (cks, ks6), (clng, lng6), (clnb, lnb6), (cb1, b16), (cb2, b26),
             (cmaskA, maskA), (cmaskA0, maskA0), (cmaskB, maskB)]
    for i, (dst, srcap) in enumerate(loads):
        sc.dma(i, dst[:], srcap)
    sc.dma(len(loads), i2t[:], i2t_in[0:2, :])
    cx.op("vector", lambda e: e.memset(ones[:], 1.0))
    cx.op("vector", lambda e: e.memset(epsc[:], LN_EPS))
    cx.op("vector", lambda e: e.memset(epsl2[:], 1e-24))
    consts_v = cx.val("V")
    for eng in ("vector", "scalar", "tensor", "sync"):
        sc.drain(eng)
    cx.wait("tensor", "V", consts_v)
    cx.wait("scalar", "V", consts_v)
    cx.wait("vector", "V", consts_v)

    consts = {"ones": ones[:], "identb": identb[:], "p64": p64b[:],
              "i2": i2[:], "i2t": i2t[:], "epsl2": epsl2[:],
              "qs6": cqs[:], "ks6": cks[:], "lng6": clng[:], "lnb6": clnb[:],
              "b16": cb1[:], "b26": cb2[:], "maskA": cmaskA[:],
              "maskA0": cmaskA0[:], "maskB": cmaskB[:],
              "csq": csq, "ssq": ssq, "csk": csk, "ssk": ssk}
    Wqkv6 = [Wqkv[:, L * 24576:(L + 1) * 24576] for L in range(LF_DEPTH)]
    Wo6 = [Wo[:, L * 8192:(L + 1) * 8192] for L in range(LF_DEPTH)]
    W16 = [W1[:, L * 45056:(L + 1) * 45056] for L in range(LF_DEPTH)]
    W26 = [W2[:, L * 22528:(L + 1) * 22528] for L in range(LF_DEPTH)]
    emit_l0_ffn(cx, T, xT, Wqkv6, Wo6, W16, W26, yT, consts, depth=depth)

    with es, nc.Block() as block:
        cx.finish(block)
    return nc



def _pre_slab(W, cw=128):
    K, F = W.shape
    nk = K // 128
    nf = F // cw
    t = W.reshape(nk, 128, nf, cw).transpose(1, 2, 0, 3)
    return np.ascontiguousarray(t.reshape(128, nf * nk * cw)).astype(np.float32)


def _pre_w1_l0ffn(W1):
    FI = LF_FI
    pad = np.zeros((1024, 44 * 128), np.float32)
    for j2 in range(22):
        aw = 128 if j2 < 21 else FI - 21 * 128
        pad[:, (2 * j2) * 128:(2 * j2) * 128 + aw] = W1[:, j2 * 128:j2 * 128 + aw]
        pad[:, (2 * j2 + 1) * 128:(2 * j2 + 1) * 128 + aw] = W1[:, FI + j2 * 128:FI + j2 * 128 + aw]
    return _pre_slab(pad)


def _pre_w2_l0ffn(W2):
    pad = np.zeros((2816, 1024), np.float32)
    pad[:2730] = W2
    blocks = []
    for e in range(8):
        for hh in range(2):
            blocks.append(_pre_slab(pad[hh * 1408:(hh + 1) * 1408,
                                        e * 128:(e + 1) * 128]))
    return np.ascontiguousarray(np.concatenate(blocks, 1))


def _pre_o1w(Wo):
    return _pre_slab(Wo)


def _pre_w2_l1ffn(w2):
    blocks = []
    for e in range(8):
        for hh in range(2):
            blocks.append(_pre_slab(w2[hh * 2048:(hh + 1) * 2048,
                                       e * 128:(e + 1) * 128]))
    return np.ascontiguousarray(np.concatenate(blocks, 1))


def emit_weight_convert(cx, wscr, plan, cvb):
    cvf, cvh = cvb
    scv_in = Stream(cx, "wcvi", nbuf=2)
    scv_out = Stream(cx, "wcvo", nbuf=2)
    marks = []
    off = 0
    i = 0
    CH = 1024
    r_cvf = RBuf(1)
    r_cvh = RBuf(1)
    for src_ap, ncols in plan:
        for c0 in range(0, ncols, CH):
            w = min(CH, ncols - c0)
            par = 0
            r_cvf.gate(cx, "gpsimd", i)
            scv_in.dma(i, cvf[:, par * CH:par * CH + w], src_ap[:, c0:c0 + w],
                       engine="gpsimd")
            scv_in.wait("gpsimd", i)
            r_cvh.gate(cx, "gpsimd", i)
            cx.op("gpsimd", lambda e, par=par, w=w: e.tensor_copy(
                out=cvh[:, par * CH:par * CH + w], in_=cvf[:, par * CH:par * CH + w]))
            r_cvf.record(i, cx.mark("gpsimd"))
            cx.wait("gpsimd", "P")
            scv_out.dma(i, wscr[:, off + c0:off + c0 + w],
                        cvh[:, par * CH:par * CH + w], engine="gpsimd")
            r_cvh.record(i, scv_out.mark(i))
            i += 1
        m = {}
        for j, n in enumerate(scv_out.names):
            m[n] = 16 * scv_out.counts[j]
        marks.append((off, ncols, m))
        off += ncols
    return marks


def _layout_rows(b):
    k = b.shape[0] // 128
    return np.ascontiguousarray(b.reshape(k, 128).T).astype(np.float32)


def _l0attn_consts():
    import ml_dtypes
    i2 = np.zeros((128, 2), np.float32); i2[:64, 0] = 1; i2[64:, 1] = 1
    i2t = np.zeros((128, 128), np.float32); i2t[0, :64] = 1; i2t[1, 64:] = 1
    bf = lambda a: a.astype(ml_dtypes.bfloat16)
    return dict(i2_in=bf(i2), i2t_in=bf(i2t), ident_in=bf(np.eye(128)))


def _l0ffn_consts(T):
    import ml_dtypes
    DH, WIN = 64, 256
    half = np.arange(0, DH, 2, dtype=np.float64)
    inv_freq = 1.0 / 10000.0 ** (half / DH)
    t = np.arange(T, dtype=np.float64)
    freqs = t[:, None] * inv_freq[None, :]
    freqs = np.concatenate([freqs, freqs], -1)
    base_scale = (half + 0.4 * DH) / (1.4 * DH)
    power = (t - T // 2) / (WIN // 2)
    scale = base_scale[None, :] ** power[:, None]
    scale = np.concatenate([scale, scale], -1)
    cos, sin = np.cos(freqs), np.sin(freqs)
    def tab(m):
        m64 = np.ascontiguousarray(m.T)
        return np.vstack([m64, m64]).astype(ml_dtypes.bfloat16)
    p64 = np.zeros((128, 128))
    for m in range(32):
        p64[m + 32, m] = -1
        p64[m, m + 32] = 1
    p64[64:, 64:] = p64[:64, :64]
    mA = np.zeros((128, 1024)); mA0 = np.zeros((128, 1024))
    for p in range(2):
        i_rel = p * 128 + np.arange(128)[:, None]
        j_rel = np.arange(512)[None, :] - 256
        bad = (i_rel - j_rel < 0) | (i_rel - j_rel > 256)
        mA[:, p * 512:(p + 1) * 512] = np.where(bad, -1e30, 0.0)
        bad0 = bad | (np.arange(512)[None, :] < 256)
        mA0[:, p * 512:(p + 1) * 512] = np.where(bad0, -1e30, 0.0)
    mB = np.zeros((128, 1024))
    for sc in range(4):
        j_rel = sc * 128 + np.arange(128)[:, None] - 256
        i_rel = np.arange(256)[None, :]
        bad = (i_rel - j_rel < 0) | (i_rel - j_rel > 256)
        mB[:, sc * 256:(sc + 1) * 256] = np.where(bad, -1e30, 0.0)
    i2 = np.zeros((128, 2)); i2[:64, 0] = 1; i2[64:, 1] = 1
    i2t = np.zeros((128, 128)); i2t[0, :64] = 1; i2t[1, 64:] = 1
    bf = lambda a: a.astype(ml_dtypes.bfloat16)
    return dict(csq=tab(cos * scale), ssq=tab(sin * scale),
                csk=tab(cos / scale), ssk=tab(sin / scale), p64=bf(p64),
                maskA=bf(mA), maskA0=bf(mA0), maskB=bf(mB),
                i2_in=bf(i2), i2t_in=bf(i2t), ident_in=bf(np.eye(128)))


def _pack_b1(b1):
    FI = LF_FI
    out = np.zeros((128, 44), np.float32)
    for j2 in range(22):
        aw = 128 if j2 < 21 else FI - 21 * 128
        out[:aw, j2] = b1[j2 * 128:j2 * 128 + aw]
        out[:aw, 22 + j2] = b1[FI + j2 * 128:FI + j2 * 128 + aw]
    return out


def _l1attn_consts(T):
    import ml_dtypes
    inv_freq = 1.0 / 10000.0 ** (np.arange(0, 256, 2) / 256)
    fr = np.arange(T)[:, None] * inv_freq[None, :]
    gam = GAMMAS
    i_loc = np.arange(512) % 128
    gq = np.zeros((128, RET_H * 512), np.float64)
    gk = np.zeros((128, RET_H * 512), np.float64)
    for h in range(RET_H):
        gq[:, h * 512:(h + 1) * 512] = (gam[h] ** (i_loc + 1.0) / 16.0)[None, :]
        gk[:, h * 512:(h + 1) * 512] = (gam[h] ** (-(i_loc + 1.0)))[None, :]
    jj, ii = np.meshgrid(np.arange(128), np.arange(128), indexing="ij")
    bf = lambda a: a.astype(ml_dtypes.bfloat16)
    return dict(cos1=bf(np.ascontiguousarray(np.cos(fr).T)),
                sin1=bf(np.ascontiguousarray(np.sin(fr).T)),
                gq=bf(gq), gk=bf(gk), maskT=bf((ii >= jj).astype(np.float64)),
                ident_in=bf(np.eye(128)))


_BUILT = {}


def _get(name, builder):
    if name not in _BUILT:
        _BUILT[name] = builder()
    return _BUILT[name]


def kernel_unfused(L0_attn_x, L0_attn_w_qkv, L0_attn_b_qkv, L0_attn_qn_w, L0_attn_kn_w,
           L0_attn_w_proj, L0_attn_b_proj, L0_ffn_Wqkv, L0_ffn_q_scale,
           L0_ffn_k_scale, L0_ffn_Wo, L0_ffn_ln_g, L0_ffn_ln_b, L0_ffn_W1,
           L0_ffn_b1, L0_ffn_W2, L0_ffn_b2, L1_attn_Wq, L1_attn_Wk, L1_attn_Wv,
           L1_attn_Wg, L1_attn_Wo, L1_attn_gn_weight, L1_ffn_w1, L1_ffn_b1,
           L1_ffn_w2, L1_ffn_b2):
    T = 4096
    B = 8
    f32 = lambda a: np.ascontiguousarray(np.asarray(a), dtype=np.float32)
    xTs = [np.ascontiguousarray(f32(L0_attn_x[b]).T) for b in range(B)]

    nc = _get("l0attn", lambda: build_l0_attn(T))
    common = dict(
        wqkvT=_pre_slab(np.ascontiguousarray(f32(L0_attn_w_qkv).T)),
        wprojT=_pre_slab(np.ascontiguousarray(f32(L0_attn_w_proj).T)),
        bqkv=_layout_rows(f32(L0_attn_b_qkv)),
        bproj=_layout_rows(f32(L0_attn_b_proj)),
        qn=np.tile(f32(L0_attn_qn_w), 2)[:, None],
        kn=np.tile(f32(L0_attn_kn_w), 2)[:, None],
        **_l0attn_consts())
    res = run_bass_kernel_spmd(nc, [dict(xT=xTs[b], **common) for b in range(B)],
                               core_ids=list(range(B)))
    xTs = [res.results[b]["yT"] for b in range(B)]

    nc = _get("l0ffn", lambda: build_l0_ffn(T))
    Wqkv6 = f32(L0_ffn_Wqkv).reshape(LF_DEPTH * D, 3072)
    Wo6 = f32(L0_ffn_Wo).reshape(LF_DEPTH * D, D)
    W16 = f32(L0_ffn_W1).reshape(LF_DEPTH * D, 5460)
    W26 = f32(L0_ffn_W2).reshape(LF_DEPTH * LF_FI, D)
    qs = f32(L0_ffn_q_scale); ks = f32(L0_ffn_k_scale)
    lng = f32(L0_ffn_ln_g); lnb = f32(L0_ffn_ln_b)
    b1 = f32(L0_ffn_b1); b2 = f32(L0_ffn_b2)
    common = dict(
        Wqkv=Wqkv6, Wo=Wo6, W1=W16, W2=W26,
        qs6=np.stack([np.tile(qs[i], 2) for i in range(LF_DEPTH)], 1),
        ks6=np.stack([np.tile(ks[i], 2) for i in range(LF_DEPTH)], 1),
        lng6=np.concatenate([_layout_rows(lng[i]) for i in range(LF_DEPTH)], 1),
        lnb6=np.concatenate([_layout_rows(lnb[i]) for i in range(LF_DEPTH)], 1),
        b16=np.concatenate([_pack_b1(b1[i]) for i in range(LF_DEPTH)], 1),
        b26=np.concatenate([_layout_rows(b2[i]) for i in range(LF_DEPTH)], 1),
        **_l0ffn_consts(T))
    res = run_bass_kernel_spmd(nc, [dict(xT=xTs[b], **common) for b in range(B)],
                               core_ids=list(range(B)))
    xTs = [res.results[b]["yT"] for b in range(B)]

    nc = _get("l1attn", lambda: build_l1_attn(T))
    common = dict(Wq=f32(L1_attn_Wq), Wk=f32(L1_attn_Wk), Wv=f32(L1_attn_Wv),
                  Wg=f32(L1_attn_Wg), Wo=f32(L1_attn_Wo),
                  gnw=_layout_rows(f32(L1_attn_gn_weight)),
                  **_l1attn_consts(T))
    res = run_bass_kernel_spmd(nc, [dict(xT=xTs[b], **common) for b in range(B)],
                               core_ids=list(range(B)))
    xTs = [res.results[b]["yT"] for b in range(B)]

    nc = _get("l1ffn", lambda: build_l1_ffn(T))
    common = dict(w1=f32(L1_ffn_w1), w2=f32(L1_ffn_w2),
                  b1l=_layout_rows(f32(L1_ffn_b1)),
                  b2l=_layout_rows(f32(L1_ffn_b2)))
    res = run_bass_kernel_spmd(nc, [dict(xT=xTs[b], **common) for b in range(B)],
                               core_ids=list(range(B)))
    out = np.stack([np.ascontiguousarray(res.results[b]["yT"].T) for b in range(B)])
    return out



def build_fused(T=4096):
    nc = bass.Bass("TRN2", target_bir_lowering=False)
    dt = lambda n, s, d=F32, k="ExternalInput": nc.dram_tensor(n, s, d, kind=k).ap()
    xT = dt("xT", [D, T])
    wqkvT = dt("wqkvT", [128, 24 * 1024])
    wprojT = dt("wprojT", [128, 8 * 1024])
    bqkv = dt("bqkv", [128, 24])
    bproj = dt("bproj", [128, 8])
    qn = dt("qn", [128, 1])
    kn = dt("kn", [128, 1])
    Wqkv = dt("Wqkv", [128, LF_DEPTH * 24 * 1024])
    Wo = dt("Wo", [128, LF_DEPTH * 8 * 1024])
    W1 = dt("W1", [128, LF_DEPTH * 44 * 1024])
    W2 = dt("W2", [128, LF_DEPTH * 16 * 1408])
    qs6 = dt("qs6", [128, LF_DEPTH])
    ks6 = dt("ks6", [128, LF_DEPTH])
    lng6 = dt("lng6", [128, LF_DEPTH * 8])
    lnb6 = dt("lnb6", [128, LF_DEPTH * 8])
    b16 = dt("b16", [128, LF_DEPTH * 44])
    b26 = dt("b26", [128, LF_DEPTH * 8])
    csq = dt("csq", [128, T], BF16)
    ssq = dt("ssq", [128, T], BF16)
    csk = dt("csk", [128, T], BF16)
    ssk = dt("ssk", [128, T], BF16)
    maskA = dt("maskA", [128, 2 * 512], BF16)
    maskA0 = dt("maskA0", [128, 2 * 512], BF16)
    maskB = dt("maskB", [128, 4 * 256], BF16)
    p64 = dt("p64", [128, 128], BF16)
    q1w = dt("q1w", [128, 8 * 1024])
    k1w = dt("k1w", [128, 8 * 1024])
    v1w = dt("v1w", [128, 16 * 1024])
    g1w = dt("g1w", [128, 16 * 1024])
    o1w = dt("o1w", [128, 8 * 2048])
    cos1 = dt("cos1", [128, T], BF16)
    sin1 = dt("sin1", [128, T], BF16)
    gq = dt("gq", [128, RET_H * 512], BF16)
    gk = dt("gk", [128, RET_H * 512], BF16)
    maskT = dt("maskT", [128, 128], BF16)
    gnw = dt("gnw", [128, 4])
    fw1 = dt("fw1", [128, 32 * 1024])
    fw2 = dt("fw2", [128, 16 * 2048])
    fb1 = dt("fb1", [128, 32])
    fb2 = dt("fb2", [128, 8])
    i2_in = dt("i2_in", [128, 2], BF16)
    i2t_in = dt("i2t_in", [128, 128], BF16)
    ident_in = dt("ident_in", [128, 128], BF16)
    wscr = dt("wscr", [128, 765952], BF16, "Internal")
    s1 = dt("s1", [D, T], F32, "Internal")
    s2 = dt("s2", [D, T], F32, "Internal")
    s3 = dt("s3", [D, T], F32, "Internal")
    yT = dt("yT", [D, T], F32, "ExternalOutput")

    es = ExitStack()
    cx = Ctx(nc, es)
    ones = cx.sb("ones", [128, 128], BF16)
    identb = cx.sb("identb", [128, 128], BF16)
    p64b = cx.sb("p64b", [128, 128], BF16)
    i2 = cx.sb("i2", [128, 2], BF16)
    i2t = cx.sb("i2t", [2, 128], BF16)
    epsc = cx.sb("epsc", [128, 1], F32)
    eps6c = cx.sb("eps6c", [128, 1], F32)
    epsl2 = cx.sb("epsl2", [128, 1], F32)
    cx.eps_col = epsc[:, 0:1]
    pbig = cx.ps("pbig", [128, 2048], F32)
    cx.pb = [cx.ps("pb0", [128, 512]), cx.ps("pb1", [128, 512]),
             pbig[:, 0:512], pbig[:, 512:1024],
             pbig[:, 1024:1536], pbig[:, 1536:2048],
             cx.ps("pb6", [128, 512])]
    cx.pbB = pbig[:, 0:1024]
    cx.pbB2 = pbig[:, 1024:2048]
    cx.pbh = cx.ps("pbh", [128, 1024], BF16)[:]

    sc = Stream(cx, "c0", nbuf=4)
    for i, (dst, srcap) in enumerate([(identb, i2_in), (i2, i2_in),
                                      (p64b, p64)]):
        pass
    sc.dma(0, identb[:], ident_in)
    sc.dma(1, i2[:], i2_in)
    sc.dma(2, p64b[:], p64)
    sc.dma(3, i2t[:], i2t_in[0:2, :])
    cx.op("vector", lambda e: e.memset(ones[:], 1.0))
    cx.op("vector", lambda e: e.memset(epsc[:], LN_EPS))
    cx.op("vector", lambda e: e.memset(eps6c[:], 1e-6))
    cx.op("vector", lambda e: e.memset(epsl2[:], 1e-24))
    consts_v = cx.val("V")
    for eng in ("vector", "scalar", "tensor", "sync"):
        sc.drain(eng)
    cx.wait("tensor", "V", consts_v)
    cx.wait("scalar", "V", consts_v)
    cx.wait("vector", "V", consts_v)

    cx.push_scope()
    cvf = cx.sb("cvf", [128, 1024], F32)
    cvh = cx.sb("cvh", [128, 1024], BF16)
    plan = [(wqkvT, 24576), (wprojT, 8192), (Wqkv, 6 * 24576), (Wo, 6 * 8192),
            (W1, 6 * 45056), (W2, 6 * 22528), (q1w, 8192), (k1w, 8192),
            (v1w, 16384), (g1w, 16384), (o1w, 16384), (fw1, 32768),
            (fw2, 32768)]
    cmarks = emit_weight_convert(cx, wscr, plan, (cvf, cvh))
    offs = {}
    names = ["wqkvT", "wprojT", "Wqkv", "Wo", "W1", "W2", "q1w", "k1w",
             "v1w", "g1w", "o1w", "fw1", "fw2"]
    for nm, (off, ncols, m) in zip(names, cmarks):
        offs[nm] = (off, ncols)
    cx.convm_l0a = cmarks[1][2]
    convm_all = cmarks[-1][2]

    def wsl(nm):
        off, ncols = offs[nm]
        return wscr[:, off:off + ncols]
    bqkvc = cx.sb("bqkvc", [128, 24], F32)
    bprojc = cx.sb("bprojc", [128, 8], F32)
    qnc = cx.sb("qnc", [128, 1], F32)
    knc = cx.sb("knc", [128, 1], F32)
    s1c = Stream(cx, "c1", nbuf=4)
    s1c.dma(0, bqkvc[:], bqkv)
    s1c.dma(1, bprojc[:], bproj)
    s1c.dma(2, qnc[:], qn)
    s1c.dma(3, knc[:], kn)
    for eng in ("vector", "scalar", "tensor", "sync"):
        s1c.drain(eng)
    consts = {"ones": ones[:], "identb": identb[:], "i2": i2[:], "i2t": i2t[:],
              "eps6": eps6c[:], "ones1": ones[0:1, :]}
    emit_l0_attn(cx, T, xT, wsl("wqkvT"), wsl("wprojT"), bqkvc[:], bprojc[:],
                 qnc[:, 0:1], knc[:, 0:1], s1, consts)
    cx.barrier()
    cx.pop_scope()

    cx.push_scope()
    for eng in ("sync", "gpsimd", "tensor"):
        cx.wait_marks(eng, convm_all)
    cqs = cx.sb("cqs", [128, LF_DEPTH], F32)
    cks = cx.sb("cks", [128, LF_DEPTH], F32)
    clng = cx.sb("clng", [128, LF_DEPTH * 8], F32)
    clnb = cx.sb("clnb", [128, LF_DEPTH * 8], F32)
    cb1 = cx.sb("cb1", [128, LF_DEPTH * 44], F32)
    cb2 = cx.sb("cb2", [128, LF_DEPTH * 8], F32)
    cmaskA = cx.sb("cmaskA", [128, 2 * 512], BF16)
    cmaskA0 = cx.sb("cmaskA0", [128, 2 * 512], BF16)
    cmaskB = cx.sb("cmaskB", [128, 4 * 256], BF16)
    s2c = Stream(cx, "c2", nbuf=4)
    for i, (d2, s_) in enumerate([(cqs, qs6), (cks, ks6), (clng, lng6),
                                  (clnb, lnb6), (cb1, b16), (cb2, b26),
                                  (cmaskA, maskA), (cmaskA0, maskA0),
                                  (cmaskB, maskB)]):
        s2c.dma(i, d2[:], s_)
    for eng in ("vector", "scalar", "tensor", "sync"):
        s2c.drain(eng)
    consts = {"ones": ones[:], "identb": identb[:], "p64": p64b[:],
              "i2": i2[:], "i2t": i2t[:], "epsl2": epsl2[:],
              "qs6": cqs[:], "ks6": cks[:], "lng6": clng[:], "lnb6": clnb[:],
              "b16": cb1[:], "b26": cb2[:], "maskA": cmaskA[:],
              "maskA0": cmaskA0[:], "maskB": cmaskB[:],
              "csq": csq, "ssq": ssq, "csk": csk, "ssk": ssk}
    Wq_s, Wo_s, W1_s, W2_s = wsl("Wqkv"), wsl("Wo"), wsl("W1"), wsl("W2")
    Wqkv6 = [Wq_s[:, L * 24576:(L + 1) * 24576] for L in range(LF_DEPTH)]
    Wo6 = [Wo_s[:, L * 8192:(L + 1) * 8192] for L in range(LF_DEPTH)]
    W16 = [W1_s[:, L * 45056:(L + 1) * 45056] for L in range(LF_DEPTH)]
    W26 = [W2_s[:, L * 22528:(L + 1) * 22528] for L in range(LF_DEPTH)]
    emit_l0_ffn(cx, T, s1, Wqkv6, Wo6, W16, W26, s2, consts)
    cx.barrier()
    cx.pop_scope()

    cx.push_scope()
    c_cos = cx.sb("c_cos", [128, T], BF16)
    c_sin = cx.sb("c_sin", [128, T], BF16)
    c_gq = cx.sb("c_gq", [128, RET_H * 512], BF16)
    c_gk = cx.sb("c_gk", [128, RET_H * 512], BF16)
    c_maskT = cx.sb("c_maskT", [128, 128], BF16)
    c_gnw = cx.sb("c_gnw", [128, 4], F32)
    s3c = Stream(cx, "c3", nbuf=4)
    for i, (d3, s_) in enumerate([(c_cos, cos1), (c_sin, sin1), (c_gq, gq),
                                  (c_gk, gk), (c_maskT, maskT), (c_gnw, gnw)]):
        s3c.dma(i, d3[:], s_)
    for eng in ("vector", "scalar", "tensor", "sync"):
        s3c.drain(eng)
    consts = {"ones": ones[:], "identb": identb[:], "cos1": c_cos[:],
              "sin1": c_sin[:], "gq": c_gq[:], "gk": c_gk[:],
              "maskT": c_maskT[:], "gnw": c_gnw[:]}
    emit_l1_attn(cx, T, s2, wsl("q1w"), wsl("k1w"), wsl("v1w"), wsl("g1w"),
                 wsl("o1w"), s3, consts)
    cx.barrier()
    cx.pop_scope()

    cx.push_scope()
    b1c = cx.sb("b1c", [128, 32], F32)
    b2c = cx.sb("b2c", [128, 8], F32)
    s4c = Stream(cx, "c4", nbuf=2)
    s4c.dma(0, b1c[:], fb1)
    s4c.dma(1, b2c[:], fb2)
    for eng in ("vector", "scalar", "tensor", "sync"):
        s4c.drain(eng)
    emit_l1_ffn(cx, T, s3, wsl("fw1"), wsl("fw2"), b1c[:], b2c[:], yT,
                ones[:], cx.pb)
    cx.barrier()
    cx.pop_scope()

    with es, nc.Block() as block:
        cx.finish(block)
    return nc


def kernel(L0_attn_x, L0_attn_w_qkv, L0_attn_b_qkv, L0_attn_qn_w, L0_attn_kn_w,
           L0_attn_w_proj, L0_attn_b_proj, L0_ffn_Wqkv, L0_ffn_q_scale,
           L0_ffn_k_scale, L0_ffn_Wo, L0_ffn_ln_g, L0_ffn_ln_b, L0_ffn_W1,
           L0_ffn_b1, L0_ffn_W2, L0_ffn_b2, L1_attn_Wq, L1_attn_Wk, L1_attn_Wv,
           L1_attn_Wg, L1_attn_Wo, L1_attn_gn_weight, L1_ffn_w1, L1_ffn_b1,
           L1_ffn_w2, L1_ffn_b2):
    T = 4096
    B = 8
    f32 = lambda a: np.ascontiguousarray(np.asarray(a), dtype=np.float32)
    nc = _get("fused", lambda: build_fused(T))
    qs = f32(L0_ffn_q_scale); ks = f32(L0_ffn_k_scale)
    lng = f32(L0_ffn_ln_g); lnb = f32(L0_ffn_ln_b)
    b1 = f32(L0_ffn_b1); b2 = f32(L0_ffn_b2)
    l0f = _l0ffn_consts(T)
    l1a = _l1attn_consts(T)
    common = dict(
        wqkvT=_pre_slab(np.ascontiguousarray(f32(L0_attn_w_qkv).T)),
        wprojT=_pre_slab(np.ascontiguousarray(f32(L0_attn_w_proj).T)),
        bqkv=_layout_rows(f32(L0_attn_b_qkv)),
        bproj=_layout_rows(f32(L0_attn_b_proj)),
        qn=np.tile(f32(L0_attn_qn_w), 2)[:, None],
        kn=np.tile(f32(L0_attn_kn_w), 2)[:, None],
        Wqkv=np.concatenate([_pre_slab(f32(L0_ffn_Wqkv[i])) for i in range(LF_DEPTH)], 1),
        Wo=np.concatenate([_pre_slab(f32(L0_ffn_Wo[i])) for i in range(LF_DEPTH)], 1),
        W1=np.concatenate([_pre_w1_l0ffn(f32(L0_ffn_W1[i])) for i in range(LF_DEPTH)], 1),
        W2=np.concatenate([_pre_w2_l0ffn(f32(L0_ffn_W2[i])) for i in range(LF_DEPTH)], 1),
        qs6=np.stack([np.tile(qs[i], 2) for i in range(LF_DEPTH)], 1),
        ks6=np.stack([np.tile(ks[i], 2) for i in range(LF_DEPTH)], 1),
        lng6=np.concatenate([_layout_rows(lng[i]) for i in range(LF_DEPTH)], 1),
        lnb6=np.concatenate([_layout_rows(lnb[i]) for i in range(LF_DEPTH)], 1),
        b16=np.concatenate([_pack_b1(b1[i]) for i in range(LF_DEPTH)], 1),
        b26=np.concatenate([_layout_rows(b2[i]) for i in range(LF_DEPTH)], 1),
        csq=l0f["csq"], ssq=l0f["ssq"], csk=l0f["csk"], ssk=l0f["ssk"],
        maskA=l0f["maskA"], maskA0=l0f["maskA0"], maskB=l0f["maskB"],
        p64=l0f["p64"],
        q1w=_pre_slab(f32(L1_attn_Wq)), k1w=_pre_slab(f32(L1_attn_Wk)),
        v1w=_pre_slab(f32(L1_attn_Wv)), g1w=_pre_slab(f32(L1_attn_Wg)),
        o1w=_pre_slab(f32(L1_attn_Wo), cw=2048) if False else _pre_o1w(f32(L1_attn_Wo)),
        gnw=_layout_rows(f32(L1_attn_gn_weight)),
        cos1=l1a["cos1"], sin1=l1a["sin1"], gq=l1a["gq"], gk=l1a["gk"],
        maskT=l1a["maskT"],
        fw1=_pre_slab(f32(L1_ffn_w1)), fw2=_pre_w2_l1ffn(f32(L1_ffn_w2)),
        fb1=_layout_rows(f32(L1_ffn_b1)), fb2=_layout_rows(f32(L1_ffn_b2)),
        i2_in=l0f["i2_in"], i2t_in=l0f["i2t_in"], ident_in=l0f["ident_in"])
    in_maps = [dict(xT=np.ascontiguousarray(f32(L0_attn_x[b]).T), **common)
               for b in range(B)]
    res = run_bass_kernel_spmd(nc, in_maps, core_ids=list(range(B)))
    return np.stack([np.ascontiguousarray(res.results[b]["yT"].T)
                     for b in range(B)])


# version 19
# speedup vs baseline: 1.1814x; 1.1814x over previous
import numpy as np
from contextlib import ExitStack

import concourse.bass as bass
import concourse.mybir as mybir
from concourse.bass_utils import run_bass_kernel_spmd

F32 = mybir.dt.float32
BF16 = mybir.dt.bfloat16
AF = mybir.ActivationFunctionType
ALU = mybir.AluOpType
AX = mybir.AxisListType

D = 1024
DC = D // 128
LN_EPS = 1e-5

ENG2SEM = {"tensor": "T", "vector": "V", "scalar": "S", "gpsimd": "P"}


class Ctx:
    ENGINES = ("tensor", "vector", "scalar", "gpsimd", "sync")

    def __init__(self, nc, es):
        self.nc = nc
        self.es = es
        self.q = {e: [] for e in self.ENGINES}
        self._sems = {}
        self.cnt = {}

    def sem(self, name):
        if name not in self._sems:
            self._sems[name] = self.es.enter_context(self.nc.semaphore(name))
            self.cnt[name] = 0
        return self._sems[name]

    def emit(self, engine, fn, inc=None, amt=1):
        pend = getattr(self, "_pend", None)
        if pend is None:
            pend = self._pend = {e: [] for e in self.ENGINES}
        waits = pend[engine]
        pend[engine] = []
        sems = self._sems
        for s, v in waits[:-1]:
            self.q[engine].append(lambda eng, s=s, v=v: eng.wait_ge(sems[s], v))
        att = waits[-1] if waits else None
        if inc is not None:
            self.sem(inc)
            self.cnt[inc] += amt

            def wrapped(eng, fn=fn, s=inc, a=amt, att=att):
                inst = fn(eng)
                if att is not None:
                    inst._wait_ge(sems[att[0]], att[1])
                inst.then_inc(sems[s], a)
            self.q[engine].append(wrapped)
        else:
            def wrapped2(eng, fn=fn, att=att):
                inst = fn(eng)
                if att is not None and inst is not None:
                    inst._wait_ge(sems[att[0]], att[1])
            self.q[engine].append(wrapped2)

    def wait(self, engine, semname, val=None):
        self.sem(semname)
        v = self.cnt[semname] if val is None else val
        if v <= 0:
            return
        hwm = getattr(self, "_hwm", None)
        if hwm is None:
            hwm = self._hwm = {e: {} for e in self.ENGINES}
        if hwm[engine].get(semname, -1) >= v:
            return
        hwm[engine][semname] = v
        pend = getattr(self, "_pend", None)
        if pend is None:
            pend = self._pend = {e: [] for e in self.ENGINES}
        pend[engine].append((semname, v))

    def wait_marks(self, engine, marks):
        for s, v in marks.items():
            if engine == "tensor" and s == "T":
                continue
            self.wait(engine, s, v)

    def val(self, semname):
        self.sem(semname)
        return self.cnt[semname]

    def op(self, engine, fn, inc=True):
        self.emit(engine, fn, inc=(ENG2SEM[engine] if inc else None))

    def mark(self, engine):
        return {ENG2SEM[engine]: self.val(ENG2SEM[engine])}

    def sb(self, name, shape, dtype):
        return self.es.enter_context(self.nc.sbuf_tensor(name, shape, dtype))

    def ps(self, name, shape, dtype=F32):
        return self.es.enter_context(self.nc.psum_tensor(name, shape, dtype))

    def push_scope(self):
        self._scopes = getattr(self, "_scopes", [])
        self._scopes.append(self.es)
        self.es = ExitStack()

    def pop_scope(self):
        self.es.close()
        self.es = self._scopes.pop()

    def barrier(self):
        marks = dict(self.cnt)
        for eng in ("tensor", "vector", "scalar", "sync", "gpsimd"):
            if not self.q[eng] and eng != "sync":
                continue
            self.wait_marks(eng, marks)

    def finish(self, block):
        pend = getattr(self, "_pend", {})
        sems = self._sems
        for e, waits in pend.items():
            for s, v in waits:
                self.q[e].append(lambda eng, s=s, v=v: eng.wait_ge(sems[s], v))
            pend[e] = []
        for e in self.ENGINES:
            if not self.q[e]:
                continue
            ops = self.q[e]

            def body(eng, ops=ops):
                for f in ops:
                    f(eng)
            getattr(block, e)(body)


class RBuf:

    def __init__(self, n=2):
        self.n = n
        self.marks = [dict() for _ in range(n)]

    def record(self, idx, mark):
        d = self.marks[idx % self.n]
        for s, v in mark.items():
            d[s] = max(d.get(s, 0), v)

    def gate(self, cx, engine, idx):
        cx.wait_marks(engine, self.marks[idx % self.n])
        self.marks[idx % self.n] = {}


class Stream:

    def __init__(self, cx, name, nbuf=2, alt=False):
        self.cx = cx
        self.nbuf = nbuf
        self.alt = alt
        self.names = [f"{name}{i}" for i in range(nbuf)]
        for n in self.names:
            cx.sem(n)
        self.counts = [0] * nbuf

    def queue(self, idx):
        par = idx % self.nbuf
        return ("sync" if par % 2 == 0 else "gpsimd") if self.alt else "sync"

    def dma(self, idx, out, in_, engine=None):
        par = idx % self.nbuf
        if engine is None:
            engine = self.queue(idx)
        self.counts[par] += 1
        self.cx.emit(engine,
                     lambda eng, o=out, i=in_: eng.dma_start(out=o, in_=i),
                     inc=self.names[par], amt=16)

    def wait(self, engine, idx):
        par = idx % self.nbuf
        self.cx.wait(engine, self.names[par], 16 * self.counts[par])

    def mark(self, idx):
        par = idx % self.nbuf
        return {self.names[par]: 16 * self.counts[par]}

    def drain(self, engine="sync"):
        for i, n in enumerate(self.names):
            self.cx.wait(engine, n, 16 * self.counts[i])


def ln_stats_combine(cx, bank_sum, bank_sq, mu_out, rstd_out, tmp1, tmp2, inv_n,
                     eps_col=None):
    if eps_col is None:
        eps_col = cx.eps_col
    cx.op("vector", lambda e: e.tensor_scalar_mul(out=mu_out, in0=bank_sum, scalar1=inv_n))
    cx.wait("vector", "V")
    cx.op("vector", lambda e: e.tensor_mul(out=tmp1, in0=mu_out, in1=mu_out))
    cx.wait("vector", "V")
    cx.op("vector", lambda e: e.scalar_tensor_tensor(
        out=tmp2, in0=bank_sq, scalar=inv_n, in1=tmp1,
        op0=ALU.mult, op1=ALU.subtract))
    cx.wait("scalar", "V")
    cx.op("scalar", lambda e: e.activation(out=tmp1, in_=tmp2, func=AF.Ln, bias=eps_col))
    cx.wait("scalar", "S")
    cx.op("scalar", lambda e: e.activation(out=rstd_out, in_=tmp1, func=AF.Exp, scale=-0.5))


def emit_l1_ffn(cx, T, xT, w1, w2, b1c, b2c, yT, ones, pb):
    NT = T // 1024
    F = 4096
    NF = F // 128

    xf = cx.sb("ff_xf", [128, 2 * 512], F32)
    xh = cx.sb("ff_xh", [128, DC * 1024], BF16)
    sq = cx.sb("ff_sq", [128, 8 * 512], BF16)
    xn = cx.sb("ff_xn", [128, DC * 1024], BF16)
    h = cx.sb("ff_h", [128, NF * 1024], BF16)
    zt = cx.sb("ff_zt", [128, DC * 1024], BF16)
    w1b = cx.sb("ff_w1b", [128, 4 * 2048], BF16)
    w2b = cx.sb("ff_w2b", [128, 4 * 2048], BF16)
    mu_s = cx.sb("ff_mu", [128, 2 * 512], F32)
    rstd_s = cx.sb("ff_rstd", [128, 2 * 512], F32)
    tmp1 = cx.sb("ff_tmp1", [128, 512], F32)
    tmp2 = cx.sb("ff_tmp2", [128, 512], F32)
    ftmp = cx.sb("ff_ftmp", [128, 512], F32)
    ftmp2 = cx.sb("ff_ftmp2", [128, 512], F32)
    outf = cx.sb("ff_outf", [128, 2 * 512], F32)

    sx = Stream(cx, "ffx")
    sw1 = Stream(cx, "ffw1", nbuf=4, alt=True)
    sw2 = Stream(cx, "ffw2", nbuf=4, alt=True)
    so = Stream(cx, "ffo")

    r_xf = RBuf(2)
    r_sq = RBuf(2)
    r_w1f = RBuf(2)
    r_w1b = RBuf(2)
    r_w2f = RBuf(2)
    r_w2b = RBuf(2)
    r_outf = RBuf(2)
    r_banks = [RBuf(1) for _ in range(8)]
    r_xh = RBuf(1)
    r_xn = RBuf(1)
    r_h = RBuf(1)
    r_zt = RBuf(1)
    r_stat = RBuf(1)

    xload = w1load = w2load = eload = ostore = 0

    for j in range(NT):
        t0 = j * 1024
        r_xh.gate(cx, "vector", 0)
        for sub in range(2):
            ts = t0 + sub * 512
            for c in range(DC):
                par = xload % 2
                r_xf.gate(cx, "sync", xload)
                sx.dma(xload, xf[:, par * 512:(par + 1) * 512],
                       xT[c * 128:(c + 1) * 128, ts:ts + 512])
                sx.wait("vector", xload)
                col = c * 1024 + sub * 512
                cx.op("vector", lambda e, col=col, par=par:
                      e.tensor_copy(out=xh[:, col:col + 512],
                                    in_=xf[:, par * 512:(par + 1) * 512]))
                r_xf.record(xload, cx.mark("vector"))
                vcast = cx.val("V")
                cx.wait("scalar", "V", vcast)
                if c == 0:
                    r_sq.gate(cx, "scalar", 0)
                cx.op("scalar", lambda e, col=col, c=c:
                      e.activation(out=sq[:, c * 512:(c + 1) * 512],
                                   in_=xh[:, col:col + 512], func=AF.Square))
                scast = cx.val("S")
                cx.wait("tensor", "V", vcast)
                if c == 0:
                    r_banks[0].gate(cx, "tensor", 0)
                    r_banks[1].gate(cx, "tensor", 0)
                st, sp = (c == 0), (c == DC - 1)
                cx.op("tensor", lambda e, col=col, st=st, sp=sp: e.matmul(
                    out=pb[0][:], lhsT=ones[:], rhs=xh[:, col:col + 512],
                    start=st, stop=sp), inc=sp)
                cx.wait("tensor", "S", scast)
                cx.op("tensor", lambda e, c=c, st=st, sp=sp: e.matmul(
                    out=pb[1][:], lhsT=ones[:], rhs=sq[:, c * 512:(c + 1) * 512],
                    start=st, stop=sp), inc=sp)
                if sp:
                    r_sq.record(0, cx.mark("tensor"))
                xload += 1
            r_stat.gate(cx, "vector", 0)
            cx.wait("vector", "T")
            ln_stats_combine(cx, pb[0][:], pb[1][:],
                             mu_s[:, sub * 512:(sub + 1) * 512],
                             rstd_s[:, sub * 512:(sub + 1) * 512],
                             tmp1[:], tmp2[:], 1.0 / D)
            r_banks[0].record(0, cx.mark("vector"))
            r_banks[1].record(0, cx.mark("vector"))
            rstd_done = cx.val("S")
            cx.wait("vector", "S", rstd_done)
            for c in range(DC):
                col = c * 1024 + sub * 512
                cx.op("vector", lambda e, col=col, sub=sub: e.tensor_tensor(
                    out=xn[:, col:col + 512], in0=xh[:, col:col + 512],
                    in1=mu_s[:, sub * 512:(sub + 1) * 512], op=ALU.subtract))
                cx.wait("vector", "V")
                cx.op("vector", lambda e, col=col, sub=sub: e.tensor_tensor(
                    out=xn[:, col:col + 512], in0=xn[:, col:col + 512],
                    in1=rstd_s[:, sub * 512:(sub + 1) * 512], op=ALU.mult))
            r_stat.record(0, {"V": cx.val("V"), "S": cx.val("S")})
        xn_done = cx.val("V")
        r_xh.record(0, {"V": xn_done, "T": cx.val("T")})

        r_xn.gate(cx, "tensor", 0)
        for f in range(NF):
            if f % 2 == 0:
                par = w1load % 4
                q = sw1.queue(w1load)
                r_w1b.gate(cx, q, w1load)
                sw1.dma(w1load, w1b[:, par * 2048:(par + 1) * 2048],
                        w1[:, f * 1024:(f + 2) * 1024])
                sw1.wait("tensor", w1load)
                pw = (par, w1load)
                w1load += 1
                cx.wait("tensor", "V", xn_done)
            off = pw[0] * 2048 + (f % 2) * 1024
            for half in range(2):
                bank = pb[2 + half]
                r_banks[2 + half].gate(cx, "tensor", 0)
                for k in range(DC):
                    cx.op("tensor", lambda e, off=off, k=k, half=half, bank=bank:
                          e.matmul(out=bank[:],
                                   lhsT=w1b[:, off + k * 128:off + (k + 1) * 128],
                                   rhs=xn[:, k * 1024 + half * 512:k * 1024 + half * 512 + 512],
                                   start=(k == 0), stop=(k == DC - 1)),
                          inc=(k == DC - 1))
                cx.wait("scalar", "T")
                cx.op("scalar", lambda e, f=f, half=half, bank=bank: e.activation(
                    out=h[:, f * 1024 + half * 512:f * 1024 + half * 512 + 512],
                    in_=bank[:], func=AF.Gelu, bias=b1c[:, f:f + 1]))
                r_banks[2 + half].record(0, cx.mark("scalar"))
            if f % 2 == 1:
                r_w1b.record(pw[1], cx.mark("tensor"))
        h_done = cx.val("S")
        r_xn.record(0, cx.mark("tensor"))

        for sub in range(2):
            ts = t0 + sub * 512
            r_zt.gate(cx, "scalar", 0)
            for ef in range(8):
                slots = []
                for hh in range(2):
                    par = w2load % 4
                    q = sw2.queue(w2load)
                    r_w2b.gate(cx, q, w2load)
                    sw2.dma(w2load, w2b[:, par * 2048:(par + 1) * 2048],
                            w2[:, (ef * 2 + hh) * 2048:(ef * 2 + hh + 1) * 2048])
                    sw2.wait("tensor", w2load)
                    slots.append((par * 2048, w2load))
                    w2load += 1
                if ef == 0:
                    cx.wait("tensor", "S", h_done)
                bank = pb[2 + ef % 2]
                r_banks[2 + ef % 2].gate(cx, "tensor", 0)
                for k2 in range(NF):
                    slot = slots[k2 // 16][0] + (k2 % 16) * 128
                    cx.op("tensor", lambda e, slot=slot, k2=k2, sub=sub, bank=bank:
                          e.matmul(out=bank[:],
                                   lhsT=w2b[:, slot:slot + 128],
                                   rhs=h[:, k2 * 1024 + sub * 512:k2 * 1024 + sub * 512 + 512],
                                   start=(k2 == 0), stop=(k2 == NF - 1)),
                          inc=(k2 == NF - 1))
                r_w2b.record(slots[0][1], cx.mark("tensor"))
                r_w2b.record(slots[1][1], cx.mark("tensor"))
                cx.wait("scalar", "T")
                cx.op("scalar", lambda e, ef=ef, sub=sub, bank=bank: e.activation(
                    out=zt[:, ef * 1024 + sub * 512:ef * 1024 + sub * 512 + 512],
                    in_=bank[:], func=AF.Identity, bias=b2c[:, ef:ef + 1]))
                r_banks[2 + ef % 2].record(0, cx.mark("scalar"))
            zt_done = cx.val("S")
            for e_ in range(8):
                col = e_ * 1024 + sub * 512
                if e_ == 0:
                    r_sq.gate(cx, "scalar", 0)
                    cx.wait("scalar", "S", zt_done)
                cx.op("scalar", lambda e, col=col, e_=e_: e.activation(
                    out=sq[:, e_ * 512:(e_ + 1) * 512], in_=zt[:, col:col + 512],
                    func=AF.Square))
                scast = cx.val("S")
                if e_ == 0:
                    cx.wait("tensor", "S", zt_done)
                    r_banks[0].gate(cx, "tensor", 0)
                    r_banks[1].gate(cx, "tensor", 0)
                st, sp = (e_ == 0), (e_ == 7)
                cx.op("tensor", lambda e, col=col, st=st, sp=sp: e.matmul(
                    out=pb[0][:], lhsT=ones[:], rhs=zt[:, col:col + 512],
                    start=st, stop=sp), inc=sp)
                cx.wait("tensor", "S", scast)
                cx.op("tensor", lambda e, e_=e_, st=st, sp=sp: e.matmul(
                    out=pb[1][:], lhsT=ones[:], rhs=sq[:, e_ * 512:(e_ + 1) * 512],
                    start=st, stop=sp), inc=sp)
                if sp:
                    r_sq.record(0, cx.mark("tensor"))
            r_stat.gate(cx, "vector", 0)
            cx.wait("vector", "T")
            ln_stats_combine(cx, pb[0][:], pb[1][:],
                             mu_s[:, sub * 512:(sub + 1) * 512],
                             rstd_s[:, sub * 512:(sub + 1) * 512],
                             tmp1[:], tmp2[:], 1.0 / D)
            r_banks[0].record(0, cx.mark("vector"))
            r_banks[1].record(0, cx.mark("vector"))
            rstd2_done = cx.val("S")
            cx.wait("vector", "S", rstd2_done)
            for c in range(DC):
                par = (xload + eload) % 2
                r_xf.gate(cx, "sync", xload + eload)
                sx.dma(xload + eload, xf[:, par * 512:(par + 1) * 512],
                       xT[c * 128:(c + 1) * 128, ts:ts + 512])
                col = c * 1024 + sub * 512
                cx.op("vector", lambda e, col=col, sub=sub: e.tensor_tensor(
                    out=ftmp[:], in0=zt[:, col:col + 512],
                    in1=mu_s[:, sub * 512:(sub + 1) * 512], op=ALU.subtract))
                cx.wait("vector", "V")
                cx.op("vector", lambda e, sub=sub: e.tensor_tensor(
                    out=ftmp2[:], in0=ftmp[:],
                    in1=rstd_s[:, sub * 512:(sub + 1) * 512], op=ALU.mult))
                sx.wait("vector", xload + eload)
                opar = ostore % 2
                r_outf.gate(cx, "vector", ostore)
                cx.wait("vector", "V")
                cx.op("vector", lambda e, par=par, opar=opar: e.tensor_tensor(
                    out=outf[:, opar * 512:(opar + 1) * 512], in0=ftmp2[:],
                    in1=xf[:, par * 512:(par + 1) * 512], op=ALU.add))
                r_xf.record(xload + eload, cx.mark("vector"))
                vadd = cx.val("V")
                cx.wait("sync", "V", vadd)
                so.dma(ostore, yT[c * 128:(c + 1) * 128, ts:ts + 512],
                       outf[:, opar * 512:(opar + 1) * 512])
                r_outf.record(ostore, so.mark(ostore))
                eload += 1
                ostore += 1
            r_zt.record(0, {"V": cx.val("V"), "T": cx.val("T"), "S": cx.val("S")})
            r_stat.record(0, {"V": cx.val("V"), "S": cx.val("S")})
    so.drain()


def build_l1_ffn(T=4096):
    nc = bass.Bass("TRN2", target_bir_lowering=False)
    xT = nc.dram_tensor("xT", [D, T], F32, kind="ExternalInput").ap()
    w1 = nc.dram_tensor("w1", [128, 32 * 1024], F32, kind="ExternalInput").ap()
    w2 = nc.dram_tensor("w2", [128, 16 * 2048], F32, kind="ExternalInput").ap()
    b1l = nc.dram_tensor("b1l", [128, 32], F32, kind="ExternalInput").ap()
    b2l = nc.dram_tensor("b2l", [128, 8], F32, kind="ExternalInput").ap()
    yT = nc.dram_tensor("yT", [D, T], F32, kind="ExternalOutput").ap()

    es = ExitStack()
    cx = Ctx(nc, es)
    ones = cx.sb("ones", [128, 128], BF16)
    b1c = cx.sb("b1c", [128, 32], F32)
    b2c = cx.sb("b2c", [128, 8], F32)
    pb = [cx.ps(f"pb{i}", [128, 512]) for i in range(8)]

    epsc = cx.sb("epsc", [128, 1], F32)
    cx.eps_col = epsc[:, 0:1]
    sc = Stream(cx, "const")
    sc.dma(0, b1c[:], b1l)
    sc.dma(1, b2c[:], b2l)
    cx.op("vector", lambda e: e.memset(ones[:], 1.0))
    cx.op("vector", lambda e: e.memset(epsc[:], LN_EPS))
    sc.wait("scalar", 0)
    sc.wait("scalar", 1)
    ones_v = cx.val("V")
    cx.wait("tensor", "V", ones_v)
    cx.wait("scalar", "V", ones_v)

    emit_l1_ffn(cx, T, xT, w1, w2, b1c, b2c, yT, ones, pb)

    with es, nc.Block() as block:
        cx.finish(block)
    return nc


def _layout_bias(b):
    k = b.shape[0] // 128
    return np.ascontiguousarray(b.reshape(k, 128).T)




RET_H = 4
RET_DK = 256
RET_DV = 512
GAMMAS = [1.0 - 2.0 ** (-5.0 - h) for h in range(RET_H)]


def emit_l1_attn(cx, T, xT, Wq, Wk, Wv, Wg, Wo, yT, consts):
    ones = consts["ones"]
    NTC = T // 512

    xf = cx.sb("ra_xf", [128, 2 * 512], F32)
    xh = cx.sb("ra_xh", [128, DC * 512], BF16)
    sq = cx.sb("ra_sq", [128, 8 * 512], BF16)
    xn = cx.sb("ra_xn", [128, DC * 512], BF16)
    qT = cx.sb("ra_qT", [128, 8 * 512], BF16)
    kT = cx.sb("ra_kT", [128, 8 * 512], BF16)
    vT = cx.sb("ra_vT", [128, 16 * 512], BF16)
    gT = cx.sb("ra_gT", [128, 16 * 512], BF16)
    ob = cx.sb("ra_ob", [128, 16 * 512], BF16)
    At = cx.sb("ra_A", [128, RET_H * 2 * 512], F32)
    ktok = cx.sb("ra_ktok", [128, 2 * 256], BF16)
    vtok = cx.sb("ra_vtok", [128, 2 * 512], BF16)
    pbuf = cx.sb("ra_pbuf", [128, 128], BF16)
    Ab = cx.sb("ra_Ab", [128, RET_H * 2 * 512], BF16)
    wb = cx.sb("ra_wb", [128, 4 * 2048], BF16)
    rt1 = cx.sb("ra_rt1", [128, 512], F32)
    rt2 = cx.sb("ra_rt2", [128, 512], F32)
    zt = cx.sb("ra_zt", [128, DC * 512], BF16)
    mu_s = cx.sb("ra_mu", [128, 512], F32)
    rstd_s = cx.sb("ra_rstd", [128, 512], F32)
    tmp1 = cx.sb("ra_tmp1", [128, 512], F32)
    tmp2 = cx.sb("ra_tmp2", [128, 512], F32)
    ftmp = cx.sb("ra_ftmp", [128, 512], F32)
    ftmp2 = cx.sb("ra_ftmp2", [128, 512], F32)
    outf = cx.sb("ra_outf", [128, 2 * 512], F32)

    pb = cx.pb
    pbh = cx.pbh
    identb = consts["identb"]

    sx = Stream(cx, "rax")
    sw = Stream(cx, "raw", nbuf=4, alt=True)
    so = Stream(cx, "rao")

    r_xf = RBuf(2)
    r_sq = RBuf(1)
    r_wf = RBuf(2)
    r_wb = RBuf(2)
    r_banks = [RBuf(1) for _ in range(8)]
    r_pbh = RBuf(1)
    r_xh = RBuf(1)
    r_xn = RBuf(1)
    r_qkvg = RBuf(1)
    r_ktok = RBuf(2)
    r_vtok = RBuf(2)
    r_ob = RBuf(1)
    r_zt = RBuf(1)
    r_stat = RBuf(1)
    r_outf = RBuf(2)

    xload = wload = ostore = 0

    cx.op("vector", lambda e: e.memset(At[:], 0.0))
    cx.op("vector", lambda e: e.memset(Ab[:], 0.0))
    a_ready = cx.val("V")
    r_Ab = RBuf(1)
    r_pbuf = RBuf(1)

    def stats_and_xn(ts):
        nonlocal xload
        for c in range(DC):
            par = xload % 2
            r_xf.gate(cx, "sync", xload)
            sx.dma(xload, xf[:, par * 512:(par + 1) * 512],
                   xT[c * 128:(c + 1) * 128, ts:ts + 512])
            sx.wait("vector", xload)
            r_xh.gate(cx, "vector", 0) if c == 0 else None
            cx.op("vector", lambda e, c=c, par=par:
                  e.tensor_copy(out=xh[:, c * 512:(c + 1) * 512],
                                in_=xf[:, par * 512:(par + 1) * 512]))
            r_xf.record(xload, cx.mark("vector"))
            vcast = cx.val("V")
            cx.wait("scalar", "V", vcast)
            if c == 0:
                r_sq.gate(cx, "scalar", 0)
            cx.op("scalar", lambda e, c=c:
                  e.activation(out=sq[:, c * 512:(c + 1) * 512],
                               in_=xh[:, c * 512:(c + 1) * 512], func=AF.Square))
            scast = cx.val("S")
            cx.wait("tensor", "V", vcast)
            if c == 0:
                r_banks[0].gate(cx, "tensor", 0)
                r_banks[1].gate(cx, "tensor", 0)
            st, sp = (c == 0), (c == DC - 1)
            cx.op("tensor", lambda e, c=c, st=st, sp=sp: e.matmul(
                out=pb[0][:], lhsT=ones[:], rhs=xh[:, c * 512:(c + 1) * 512],
                start=st, stop=sp), inc=sp)
            cx.wait("tensor", "S", scast)
            cx.op("tensor", lambda e, c=c, st=st, sp=sp: e.matmul(
                out=pb[1][:], lhsT=ones[:], rhs=sq[:, c * 512:(c + 1) * 512],
                start=st, stop=sp), inc=sp)
            if sp:
                r_sq.record(0, cx.mark("tensor"))
            xload += 1
        r_stat.gate(cx, "vector", 0)
        cx.wait("vector", "T")
        ln_stats_combine(cx, pb[0][:], pb[1][:], mu_s[:], rstd_s[:],
                         tmp1[:], tmp2[:], 1.0 / D)
        r_banks[0].record(0, cx.mark("vector"))
        r_banks[1].record(0, cx.mark("vector"))
        rstd_done = cx.val("S")
        cx.wait("vector", "S", rstd_done)
        r_xn.gate(cx, "vector", 0)
        for c in range(DC):
            cx.op("vector", lambda e, c=c: e.tensor_tensor(
                out=xn[:, c * 512:(c + 1) * 512], in0=xh[:, c * 512:(c + 1) * 512],
                in1=mu_s[:], op=ALU.subtract))
            cx.wait("vector", "V")
            cx.op("vector", lambda e, c=c: e.tensor_tensor(
                out=xn[:, c * 512:(c + 1) * 512], in0=xn[:, c * 512:(c + 1) * 512],
                in1=rstd_s[:], op=ALU.mult))
        r_xh.record(0, {"V": cx.val("V"), "T": cx.val("T")})
        r_stat.record(0, {"V": cx.val("V"), "S": cx.val("S")})
        return cx.val("V")

    def load_w_slab(src2d):
        nonlocal wload
        w = src2d.shape[1]
        par = wload % 4
        q = sw.queue(wload)
        r_wb.gate(cx, q, wload)
        sw.dma(wload, wb[:, par * 2048:par * 2048 + w], src2d)
        sw.wait("tensor", wload)
        wload += 1
        return par, 0

    for tc in range(NTC):
        ts = tc * 512
        xn_done = stats_and_xn(ts)

        r_qkvg.gate(cx, "vector", 0)
        r_qkvg.gate(cx, "scalar", 0)
        r_ob.gate(cx, "vector", 0)
        for which, wmat, dstT in (("q", Wq, qT), ("k", Wk, kT)):
            for f in range(8):
                if f % 2 == 0:
                    par, wcast = load_w_slab(wmat[:, f * 1024:(f + 2) * 1024])
                    pw = (par, wload - 1)
                    cx.wait("tensor", "S", wcast)
                    cx.wait("tensor", "V", xn_done)
                par = pw[0]
                off = par * 2048 + (f % 2) * 1024
                bank = pb[2 + f % 2]
                r_banks[2 + f % 2].gate(cx, "tensor", 0)
                for k in range(DC):
                    cx.op("tensor", lambda e, off=off, k=k, bank=bank: e.matmul(
                        out=bank[:], lhsT=wb[:, off + k * 128:off + (k + 1) * 128],
                        rhs=xn[:, k * 512:(k + 1) * 512],
                        start=(k == 0), stop=(k == DC - 1)), inc=(k == DC - 1))
                if f % 2 == 1:
                    r_wb.record(pw[1], cx.mark("tensor"))
                h, c = f // 2, f % 2
                cx.wait("vector", "T")
                if c == 0:
                    cx.op("vector", lambda e, bank=bank: e.tensor_copy(out=rt1[:], in_=bank[:]))
                    r_banks[2 + f % 2].record(0, cx.mark("vector"))
                else:
                    cx.op("vector", lambda e, bank=bank: e.tensor_copy(out=rt2[:], in_=bank[:]))
                    r_banks[2 + f % 2].record(0, cx.mark("vector"))
                    cx.wait("vector", "V")
                    gq = consts["gq"] if which == "q" else consts["gk"]
                    cs = consts["cos1"][:, ts:ts + 512]
                    sn = consts["sin1"][:, ts:ts + 512]
                    ghead = gq[:, h * 512:(h + 1) * 512]
                    d0 = (h * 2 + 0) * 512
                    d1 = (h * 2 + 1) * 512
                    def chain(fn):
                        cx.op("vector", fn)
                        cx.wait("vector", "V")
                    chain(lambda e, cs=cs: e.tensor_tensor(
                        out=tmp1[:], in0=rt1[:], in1=cs, op=ALU.mult))
                    chain(lambda e, sn=sn: e.tensor_tensor(
                        out=tmp2[:], in0=rt2[:], in1=sn, op=ALU.mult))
                    chain(lambda e: e.tensor_tensor(
                        out=tmp1[:], in0=tmp1[:], in1=tmp2[:], op=ALU.subtract))
                    chain(lambda e, dstT=dstT, d0=d0, ghead=ghead: e.tensor_tensor(
                        out=dstT[:, d0:d0 + 512], in0=tmp1[:], in1=ghead, op=ALU.mult))
                    chain(lambda e, cs=cs: e.tensor_tensor(
                        out=tmp1[:], in0=rt2[:], in1=cs, op=ALU.mult))
                    chain(lambda e, sn=sn: e.tensor_tensor(
                        out=tmp2[:], in0=rt1[:], in1=sn, op=ALU.mult))
                    chain(lambda e: e.tensor_tensor(
                        out=tmp1[:], in0=tmp1[:], in1=tmp2[:], op=ALU.add))
                    chain(lambda e, dstT=dstT, d1=d1, ghead=ghead: e.tensor_tensor(
                        out=dstT[:, d1:d1 + 512], in0=tmp1[:], in1=ghead, op=ALU.mult))
        for which, wmat, dstT in (("v", Wv, vT), ("g", Wg, gT)):
            for f in range(16):
                if f % 2 == 0:
                    par, wcast = load_w_slab(wmat[:, f * 1024:(f + 2) * 1024])
                    pw = (par, wload - 1)
                    cx.wait("tensor", "S", wcast)
                    cx.wait("tensor", "V", xn_done)
                par = pw[0]
                off = par * 2048 + (f % 2) * 1024
                bank = pb[2 + f % 2]
                r_banks[2 + f % 2].gate(cx, "tensor", 0)
                for k in range(DC):
                    cx.op("tensor", lambda e, off=off, k=k, bank=bank: e.matmul(
                        out=bank[:], lhsT=wb[:, off + k * 128:off + (k + 1) * 128],
                        rhs=xn[:, k * 512:(k + 1) * 512],
                        start=(k == 0), stop=(k == DC - 1)), inc=(k == DC - 1))
                if f % 2 == 1:
                    r_wb.record(pw[1], cx.mark("tensor"))
                cx.wait("scalar", "T")
                cx.op("scalar", lambda e, dstT=dstT, f=f, bank=bank: e.activation(
                    out=dstT[:, f * 512:(f + 1) * 512], in_=bank[:], func=AF.Identity))
                r_banks[2 + f % 2].record(0, cx.mark("scalar"))
        qkvg_done_v = cx.val("V")
        qkvg_done_s = cx.val("S")

        for ml in range(4):
            tl = ml * 128
            for h in range(RET_H):
                r_pbh.gate(cx, "tensor", 0)
                cx.wait("tensor", "V", qkvg_done_v)
                cx.wait("tensor", "S", qkvg_done_s)
                kpar = (ml * RET_H + h) % 2
                for c in range(2):
                    cx.op("tensor", lambda e, h=h, c=c, tl=tl: e.transpose(
                        out=pbh[:, c * 128:(c + 1) * 128],
                        in_=kT[:, (h * 2 + c) * 512 + tl:(h * 2 + c) * 512 + tl + 128],
                        identity=identb[:]), inc=True)
                for d in range(4):
                    cx.op("tensor", lambda e, h=h, d=d, tl=tl: e.transpose(
                        out=pbh[:, 256 + d * 128:256 + (d + 1) * 128],
                        in_=vT[:, (h * 4 + d) * 512 + tl:(h * 4 + d) * 512 + tl + 128],
                        identity=identb[:]), inc=True)
                tdone = cx.val("T")
                r_ktok.gate(cx, "vector", kpar)
                r_vtok.gate(cx, "vector", kpar)
                cx.wait("vector", "T", tdone)
                cx.op("vector", lambda e, kpar=kpar: e.tensor_copy(
                    out=ktok[:, kpar * 256:kpar * 256 + 256], in_=pbh[:, 0:256]))
                cx.op("vector", lambda e, kpar=kpar: e.tensor_copy(
                    out=vtok[:, kpar * 512:kpar * 512 + 512], in_=pbh[:, 256:768]))
                r_pbh.record(0, cx.mark("vector"))
                ktok_v = cx.val("V")

                r_banks[3].gate(cx, "tensor", 0)
                for c in range(2):
                    cx.op("tensor", lambda e, h=h, c=c, tl=tl: e.matmul(
                        out=pb[3][:, 0:128],
                        lhsT=kT[:, (h * 2 + c) * 512 + tl:(h * 2 + c) * 512 + tl + 128],
                        rhs=qT[:, (h * 2 + c) * 512 + tl:(h * 2 + c) * 512 + tl + 128],
                        start=(c == 0), stop=(c == 1)), inc=(c == 1))
                cx.wait("vector", "T")
                r_pbuf.gate(cx, "vector", 0)
                cx.op("vector", lambda e: e.tensor_tensor(
                    out=pbuf[:], in0=pb[3][:, 0:128], in1=consts["maskT"],
                    op=ALU.mult))
                r_banks[3].record(0, cx.mark("vector"))
                p_done = cx.val("V")

                r_banks[4].gate(cx, "tensor", 0)
                cx.wait("tensor", "V", max(a_ready, ktok_v, p_done))
                for d in range(4):
                    for c in range(2):
                        cx.op("tensor", lambda e, h=h, c=c, d=d, tl=tl: e.matmul(
                            out=pb[4][:, d * 128:(d + 1) * 128],
                            lhsT=Ab[:, (h * 2 + c) * 512 + d * 128:(h * 2 + c) * 512 + (d + 1) * 128],
                            rhs=qT[:, (h * 2 + c) * 512 + tl:(h * 2 + c) * 512 + tl + 128],
                            start=(c == 0), stop=False), inc=False)
                    cx.op("tensor", lambda e, d=d, kpar=kpar: e.matmul(
                        out=pb[4][:, d * 128:(d + 1) * 128],
                        lhsT=vtok[:, kpar * 512 + d * 128:kpar * 512 + (d + 1) * 128],
                        rhs=pbuf[:],
                        start=False, stop=True), inc=True)
                o_mm = cx.val("T")
                r_pbuf.record(0, {"T": o_mm})
                r_Ab.record(0, {"T": o_mm})
                cx.wait("vector", "T", o_mm)
                for d in range(4):
                    cx.op("vector", lambda e, h=h, d=d, tl=tl: e.tensor_copy(
                        out=ob[:, (h * 4 + d) * 512 + tl:(h * 4 + d) * 512 + tl + 128],
                        in_=pb[4][:, d * 128:(d + 1) * 128]))
                r_banks[4].record(0, cx.mark("vector"))

                r_banks[5].gate(cx, "tensor", 0)
                r_banks[6].gate(cx, "tensor", 0)
                cx.wait("tensor", "V", ktok_v)
                for c in range(2):
                    cx.op("tensor", lambda e, c=c, kpar=kpar: e.matmul(
                        out=pb[5 + c][:],
                        lhsT=ktok[:, kpar * 256 + c * 128:kpar * 256 + (c + 1) * 128],
                        rhs=vtok[:, kpar * 512:kpar * 512 + 512],
                        start=True, stop=True), inc=True)
                da_mm = cx.val("T")
                r_ktok.record(kpar, cx.mark("tensor"))
                r_vtok.record(kpar, cx.mark("tensor"))
                cx.wait("vector", "T", da_mm)
                r_Ab.gate(cx, "vector", 0)
                for c in range(2):
                    acol = (h * 2 + c) * 512
                    cx.op("vector", lambda e, acol=acol, c=c: e.tensor_tensor(
                        out=At[:, acol:acol + 512], in0=At[:, acol:acol + 512],
                        in1=pb[5 + c][:], op=ALU.add))
                    cx.wait("vector", "V")
                    cx.op("vector", lambda e, acol=acol, h=h: e.tensor_scalar_mul(
                        out=At[:, acol:acol + 512], in0=At[:, acol:acol + 512],
                        scalar1=float(GAMMAS[h] ** 128)))
                    cx.wait("vector", "V")
                    cx.op("vector", lambda e, acol=acol: e.tensor_copy(
                        out=Ab[:, acol:acol + 512], in_=At[:, acol:acol + 512]))
                    r_banks[5 + c].record(0, cx.mark("vector"))
                a_ready = cx.val("V")
        ret_done_v = cx.val("V")

        for h in range(RET_H):
            for d in range(4):
                col = (h * 4 + d) * 512
                cx.wait("scalar", "V", ret_done_v)
                if d == 0:
                    r_sq.gate(cx, "scalar", 0)
                cx.op("scalar", lambda e, col=col, d=d: e.activation(
                    out=sq[:, d * 512:(d + 1) * 512], in_=ob[:, col:col + 512],
                    func=AF.Square))
                scast = cx.val("S")
                cx.wait("tensor", "S", scast)
                if d == 0:
                    r_banks[0].gate(cx, "tensor", 0)
                cx.op("tensor", lambda e, d=d: e.matmul(
                    out=pb[0][:], lhsT=ones[:], rhs=sq[:, d * 512:(d + 1) * 512],
                    start=(d == 0), stop=(d == 3)), inc=(d == 3))
            r_sq.record(0, cx.mark("tensor"))
            r_stat.gate(cx, "vector", 0)
            cx.wait("vector", "T")
            cx.op("vector", lambda e: e.tensor_scalar_mul(
                out=tmp2[:], in0=pb[0][:], scalar1=1.0 / RET_DV))
            r_banks[0].record(0, cx.mark("vector"))
            cx.wait("scalar", "V")
            cx.op("scalar", lambda e: e.activation(out=tmp1[:], in_=tmp2[:],
                                                   func=AF.Ln, bias=cx.eps_col))
            cx.wait("scalar", "S")
            cx.op("scalar", lambda e: e.activation(out=rstd_s[:], in_=tmp1[:],
                                                   func=AF.Exp, scale=-0.5))
            rstd_done = cx.val("S")
            cx.wait("vector", "S", rstd_done)
            for d in range(4):
                col = (h * 4 + d) * 512
                cx.op("vector", lambda e, col=col: e.tensor_tensor(
                    out=ob[:, col:col + 512], in0=ob[:, col:col + 512],
                    in1=rstd_s[:], op=ALU.mult))
                cx.wait("vector", "V")
                cx.op("vector", lambda e, col=col, d=d: e.tensor_scalar_mul(
                    out=ob[:, col:col + 512], in0=ob[:, col:col + 512],
                    scalar1=consts["gnw"][:, d:d + 1]))
                cx.wait("vector", "V")
                cx.op("vector", lambda e, col=col: e.tensor_tensor(
                    out=gT[:, col:col + 512], in0=gT[:, col:col + 512],
                    in1=ob[:, col:col + 512], op=ALU.mult))
            r_stat.record(0, {"V": cx.val("V"), "S": cx.val("S")})
        og_done = cx.val("V")

        r_zt.gate(cx, "scalar", 0)
        for f in range(8):
            par, wcast = load_w_slab(Wo[:, f * 2048:(f + 1) * 2048])
            cx.wait("tensor", "S", wcast)
            cx.wait("tensor", "V", og_done)
            bank = pb[2 + f % 2]
            r_banks[2 + f % 2].gate(cx, "tensor", 0)
            for kc in range(16):
                cx.op("tensor", lambda e, par=par, kc=kc, bank=bank: e.matmul(
                    out=bank[:],
                    lhsT=wb[:, par * 2048 + kc * 128:par * 2048 + (kc + 1) * 128],
                    rhs=gT[:, kc * 512:(kc + 1) * 512],
                    start=(kc == 0), stop=(kc == 15)), inc=(kc == 15))
            r_wb.record(wload - 1, cx.mark("tensor"))
            cx.wait("scalar", "T")
            cx.op("scalar", lambda e, f=f, bank=bank: e.activation(
                out=zt[:, f * 512:(f + 1) * 512], in_=bank[:], func=AF.Identity))
            r_banks[2 + f % 2].record(0, cx.mark("scalar"))
        zt_done = cx.val("S")

        for e_ in range(8):
            col = e_ * 512
            if e_ == 0:
                r_sq.gate(cx, "scalar", 0)
                cx.wait("scalar", "S", zt_done)
            cx.op("scalar", lambda e, col=col, e_=e_: e.activation(
                out=sq[:, e_ * 512:(e_ + 1) * 512], in_=zt[:, col:col + 512],
                func=AF.Square))
            scast = cx.val("S")
            if e_ == 0:
                cx.wait("tensor", "S", zt_done)
                r_banks[0].gate(cx, "tensor", 0)
                r_banks[1].gate(cx, "tensor", 0)
            st, sp = (e_ == 0), (e_ == 7)
            cx.op("tensor", lambda e, col=col, st=st, sp=sp: e.matmul(
                out=pb[0][:], lhsT=ones[:], rhs=zt[:, col:col + 512],
                start=st, stop=sp), inc=sp)
            cx.wait("tensor", "S", scast)
            cx.op("tensor", lambda e, e_=e_, st=st, sp=sp: e.matmul(
                out=pb[1][:], lhsT=ones[:], rhs=sq[:, e_ * 512:(e_ + 1) * 512],
                start=st, stop=sp), inc=sp)
            if sp:
                r_sq.record(0, cx.mark("tensor"))
        r_stat.gate(cx, "vector", 0)
        cx.wait("vector", "T")
        ln_stats_combine(cx, pb[0][:], pb[1][:], mu_s[:], rstd_s[:],
                         tmp1[:], tmp2[:], 1.0 / D)
        r_banks[0].record(0, cx.mark("vector"))
        r_banks[1].record(0, cx.mark("vector"))
        rstd2_done = cx.val("S")
        cx.wait("vector", "S", rstd2_done)
        for c in range(DC):
            par = xload % 2
            r_xf.gate(cx, "sync", xload)
            sx.dma(xload, xf[:, par * 512:(par + 1) * 512],
                   xT[c * 128:(c + 1) * 128, ts:ts + 512])
            cx.op("vector", lambda e, c=c: e.tensor_tensor(
                out=ftmp[:], in0=zt[:, c * 512:(c + 1) * 512],
                in1=mu_s[:], op=ALU.subtract))
            cx.wait("vector", "V")
            cx.op("vector", lambda e: e.tensor_tensor(
                out=ftmp2[:], in0=ftmp[:], in1=rstd_s[:], op=ALU.mult))
            sx.wait("vector", xload)
            opar = ostore % 2
            r_outf.gate(cx, "vector", ostore)
            cx.wait("vector", "V")
            cx.op("vector", lambda e, par=par, opar=opar: e.tensor_tensor(
                out=outf[:, opar * 512:(opar + 1) * 512], in0=ftmp2[:],
                in1=xf[:, par * 512:(par + 1) * 512], op=ALU.add))
            r_xf.record(xload, cx.mark("vector"))
            vadd = cx.val("V")
            cx.wait("sync", "V", vadd)
            so.dma(ostore, yT[c * 128:(c + 1) * 128, ts:ts + 512],
                   outf[:, opar * 512:(opar + 1) * 512])
            r_outf.record(ostore, so.mark(ostore))
            xload += 1
            ostore += 1
        r_zt.record(0, {"V": cx.val("V"), "T": cx.val("T"), "S": cx.val("S")})
        r_stat.record(0, {"V": cx.val("V"), "S": cx.val("S")})
        allm = {"V": cx.val("V"), "T": cx.val("T"), "S": cx.val("S")}
        r_xn.record(0, allm)
        r_qkvg.record(0, allm)
        r_ob.record(0, allm)
    so.drain()


def build_l1_attn(T=4096):
    nc = bass.Bass("TRN2", target_bir_lowering=False)
    xT = nc.dram_tensor("xT", [D, T], F32, kind="ExternalInput").ap()
    Wq = nc.dram_tensor("Wq", [128, 8 * 1024], F32, kind="ExternalInput").ap()
    Wk = nc.dram_tensor("Wk", [128, 8 * 1024], F32, kind="ExternalInput").ap()
    Wv = nc.dram_tensor("Wv", [128, 16 * 1024], F32, kind="ExternalInput").ap()
    Wg = nc.dram_tensor("Wg", [128, 16 * 1024], F32, kind="ExternalInput").ap()
    Wo = nc.dram_tensor("Wo", [128, 8 * 2048], F32, kind="ExternalInput").ap()
    cos1 = nc.dram_tensor("cos1", [128, T], BF16, kind="ExternalInput").ap()
    sin1 = nc.dram_tensor("sin1", [128, T], BF16, kind="ExternalInput").ap()
    gq = nc.dram_tensor("gq", [128, RET_H * 512], BF16, kind="ExternalInput").ap()
    gk = nc.dram_tensor("gk", [128, RET_H * 512], BF16, kind="ExternalInput").ap()
    maskT = nc.dram_tensor("maskT", [128, 128], BF16, kind="ExternalInput").ap()
    gnw = nc.dram_tensor("gnw", [128, 4], F32, kind="ExternalInput").ap()
    ident_in = nc.dram_tensor("ident_in", [128, 128], BF16, kind="ExternalInput").ap()
    yT = nc.dram_tensor("yT", [D, T], F32, kind="ExternalOutput").ap()

    es = ExitStack()
    cx = Ctx(nc, es)
    ones = cx.sb("ones", [128, 128], BF16)
    identb = cx.sb("identb", [128, 128], BF16)
    epsc = cx.sb("epsc", [128, 1], F32)
    cx.eps_col = epsc[:, 0:1]
    c_cos = cx.sb("c_cos", [128, T], BF16)
    c_sin = cx.sb("c_sin", [128, T], BF16)
    c_gq = cx.sb("c_gq", [128, RET_H * 512], BF16)
    c_gk = cx.sb("c_gk", [128, RET_H * 512], BF16)
    c_maskT = cx.sb("c_maskT", [128, 128], BF16)
    c_gnw = cx.sb("c_gnw", [128, 4], F32)
    pbig = cx.ps("pbig", [128, 2048], F32)
    cx.pb = [cx.ps("pb0", [128, 512]), cx.ps("pb1", [128, 512]),
             pbig[:, 0:512], pbig[:, 512:1024],
             pbig[:, 1024:1536], pbig[:, 1536:2048],
             cx.ps("pb6", [128, 512])]
    cx.pbB = pbig[:, 0:1024]
    cx.pbB2 = pbig[:, 1024:2048]
    cx.pbh = cx.ps("pbh", [128, 1024], BF16)[:]

    sc = Stream(cx, "const", nbuf=8)
    for i, (dst, srcap) in enumerate([(c_cos, cos1), (c_sin, sin1), (c_gq, gq),
                                      (c_gk, gk), (c_maskT, maskT), (c_gnw, gnw),
                                      (identb, ident_in)]):
        sc.dma(i, dst[:], srcap)
    cx.op("vector", lambda e: e.memset(ones[:], 1.0))
    cx.op("vector", lambda e: e.memset(epsc[:], LN_EPS))
    consts_v = cx.val("V")
    sc.drain("vector")
    sc.drain("scalar")
    sc.drain("tensor")
    cx.wait("tensor", "V", consts_v)
    cx.wait("scalar", "V", consts_v)
    cx.wait("vector", "V", consts_v)

    consts = {"ones": ones[:], "identb": identb[:], "cos1": c_cos[:],
              "sin1": c_sin[:], "gq": c_gq[:], "gk": c_gk[:],
              "maskT": c_maskT[:], "gnw": c_gnw[:]}
    emit_l1_attn(cx, T, xT, Wq, Wk, Wv, Wg, Wo, yT, consts)

    with es, nc.Block() as block:
        cx.finish(block)
    return nc


def emit_l0_attn(cx, T, xT, wqkvT, wprojT, bqkvc, bprojc, qnc, knc, yT, consts):
    ones = consts["ones"]
    identb = consts["identb"]
    eps6 = consts["eps6"]
    NTC = T // 512
    nc = cx.nc
    OT = nc.dram_tensor("ga_oscr", [D, T], BF16, kind="Internal").ap()

    xf = cx.sb("ga_xf", [128, 2 * 512], F32)
    xh = cx.sb("ga_xh", [128, 2 * 512], BF16)
    xn = cx.sb("ga_xn", [128, DC * 512], BF16)
    sqb = cx.sb("ga_sq", [128, 8 * 512], BF16)
    muA = cx.sb("ga_muA", [128, NTC * 512], BF16)
    rsA = cx.sb("ga_rsA", [128, NTC * 512], BF16)
    kA = cx.sb("ga_kA", [65, 8 * T], BF16)
    vh = cx.sb("ga_vh", [128, (T // 128) * 520], BF16)
    qa = cx.sb("ga_qa", [65, 8 * 512], BF16)
    wres = cx.sb("ga_wres", [128, 4 * 2048], BF16)
    qkt = cx.sb("ga_qkt", [128, 2 * 512], BF16)
    pt = cx.sb("ga_pt", [128, 2 * 1024], BF16)
    dbc = cx.sb("ga_dbc", [128, 512], BF16)
    mng = cx.sb("ga_mng", [128, 8], BF16)
    oc = cx.sb("ga_oc", [128, 8 * 512], BF16)
    zt = cx.sb("ga_zt", [128, DC * 512], BF16)
    mu_s = cx.sb("ga_mu", [128, 512], F32)
    rstd_s = cx.sb("ga_rstd", [128, 512], F32)
    tmp1 = cx.sb("ga_tmp1", [128, 512], F32)
    tmp2 = cx.sb("ga_tmp2", [128, 512], F32)
    ftmp = cx.sb("ga_ftmp", [128, 512], F32)
    ftmp2 = cx.sb("ga_ftmp2", [128, 512], F32)
    outf = cx.sb("ga_outf", [128, 2 * 512], F32)
    mnf = cx.sb("ga_mnf", [128, 1], F32)
    mnf8 = cx.sb("ga_mnf8", [128, 8], F32)

    pb = cx.pb
    pbh = cx.pbh

    sx = Stream(cx, "gax")
    sw = Stream(cx, "gaw", nbuf=4, alt=False)
    sk = Stream(cx, "gak")
    sov = Stream(cx, "gao")
    so = Stream(cx, "gas")

    r_xf = RBuf(2)
    r_xh = RBuf(2)
    r_xn = RBuf(1)
    r_sq = RBuf(1)
    r_wf = RBuf(2)
    r_wslab = RBuf(2)
    r_wres = RBuf(1)
    r_banks = [RBuf(1) for _ in range(8)]
    r_pbh = RBuf(1)
    r_kA = RBuf(1)
    r_vh = RBuf(1)
    r_qa = RBuf(1)
    r_qkt = RBuf(1)
    r_qraw2 = RBuf(1)
    r_qfin2 = RBuf(1)
    r_pt = RBuf(2)
    r_oc = RBuf(1)
    r_zt = RBuf(1)
    r_stat = RBuf(1)
    r_outf = RBuf(2)
    r_mng = RBuf(4)
    r_dbc = RBuf(1)

    xload = wload = kcopy = ostore = ovio = 0

    for j in range(NTC):
        ts = j * 512
        for c in range(DC):
            par = xload % 2
            r_xf.gate(cx, "sync", xload)
            sx.dma(xload, xf[:, par * 512:(par + 1) * 512],
                   xT[c * 128:(c + 1) * 128, ts:ts + 512])
            sx.wait("vector", xload)
            r_xh.gate(cx, "vector", xload)
            cx.op("vector", lambda e, par=par:
                  e.tensor_copy(out=xh[:, par * 512:(par + 1) * 512],
                                in_=xf[:, par * 512:(par + 1) * 512]))
            r_xf.record(xload, cx.mark("vector"))
            vcast = cx.val("V")
            cx.wait("scalar", "V", vcast)
            if c == 0:
                r_sq.gate(cx, "scalar", 0)
            cx.op("scalar", lambda e, c=c, par=par:
                  e.activation(out=sqb[:, c * 512:(c + 1) * 512],
                               in_=xh[:, par * 512:(par + 1) * 512], func=AF.Square))
            scast = cx.val("S")
            r_xh.record(xload, cx.mark("scalar"))
            cx.wait("tensor", "V", vcast)
            if c == 0:
                r_banks[0].gate(cx, "tensor", 0)
                r_banks[1].gate(cx, "tensor", 0)
            st, sp = (c == 0), (c == DC - 1)
            cx.op("tensor", lambda e, par=par, st=st, sp=sp: e.matmul(
                out=pb[0][:], lhsT=ones[:], rhs=xh[:, par * 512:(par + 1) * 512],
                start=st, stop=sp), inc=True)
            r_xh.record(xload, cx.mark("tensor"))
            cx.wait("tensor", "S", scast)
            cx.op("tensor", lambda e, c=c, st=st, sp=sp: e.matmul(
                out=pb[1][:], lhsT=ones[:], rhs=sqb[:, c * 512:(c + 1) * 512],
                start=st, stop=sp), inc=sp)
            if sp:
                r_sq.record(0, cx.mark("tensor"))
            xload += 1
        r_stat.gate(cx, "vector", 0)
        cx.wait("vector", "T")
        ln_stats_combine(cx, pb[0][:], pb[1][:], mu_s[:], rstd_s[:],
                         tmp1[:], tmp2[:], 1.0 / D)
        r_banks[0].record(0, cx.mark("vector"))
        r_banks[1].record(0, cx.mark("vector"))
        rstd_done = cx.val("S")
        cx.wait("vector", "S", rstd_done)
        cx.op("vector", lambda e, ts=ts: e.tensor_copy(
            out=muA[:, ts:ts + 512], in_=mu_s[:]))
        cx.op("vector", lambda e, ts=ts: e.tensor_copy(
            out=rsA[:, ts:ts + 512], in_=rstd_s[:]))
        r_stat.record(0, {"V": cx.val("V"), "S": cx.val("S")})
    mu_ready = cx.val("V")

    def make_xn(ts):
        nonlocal xload
        r_xn.gate(cx, "vector", 0)
        for c in range(DC):
            par = xload % 2
            r_xf.gate(cx, "sync", xload)
            sx.dma(xload, xf[:, par * 512:(par + 1) * 512],
                   xT[c * 128:(c + 1) * 128, ts:ts + 512])
            sx.wait("vector", xload)
            if c == 0:
                cx.wait("vector", "V", mu_ready)
            cx.op("vector", lambda e, c=c, par=par, ts=ts: e.tensor_tensor(
                out=xn[:, c * 512:(c + 1) * 512],
                in0=xf[:, par * 512:(par + 1) * 512],
                in1=muA[:, ts:ts + 512], op=ALU.subtract))
            cx.wait("vector", "V")
            cx.op("vector", lambda e, c=c, ts=ts: e.tensor_tensor(
                out=xn[:, c * 512:(c + 1) * 512], in0=xn[:, c * 512:(c + 1) * 512],
                in1=rsA[:, ts:ts + 512], op=ALU.mult))
            r_xf.record(xload, cx.mark("vector"))
            xload += 1
        return cx.val("V")

    def load_slab(src2d):
        nonlocal wload
        par = wload % 4
        slot = par * 2048
        w = src2d.shape[1]
        q = sw.queue(wload)
        cx.wait_marks(q, cx.convm_l0a)
        r_wslab.gate(cx, q, wload)
        sw.dma(wload, wres[:, slot:slot + w], src2d)
        sw.wait("tensor", wload)
        idx = wload
        wload += 1
        return slot, idx, 0

    def rms_norm_evac(bank, fglob, nw_col, extra_scale, dst_cb):
        cx.wait("scalar", "T")
        r_qraw2.gate(cx, "scalar", 0)
        cx.op("scalar", lambda e, bank=bank, fglob=fglob: e.activation(
            out=qkt[:, 0:512], in_=bank[:], func=AF.Identity,
            bias=bqkvc[:, fglob:fglob + 1]))
        r_qfin2.gate(cx, "scalar", 0)
        cx.wait("scalar", "S")
        cx.op("scalar", lambda e: e.activation(
            out=qkt[:, 512:1024], in_=qkt[:, 0:512], func=AF.Square))
        sq_done = cx.val("S")
        cx.wait("tensor", "S", sq_done)
        r_banks[4].gate(cx, "tensor", 0)
        cx.op("tensor", lambda e: e.matmul(
            out=pb[4][0:2, :], lhsT=consts["i2"], rhs=qkt[:, 512:1024],
            start=True, stop=True), inc=True)
        hsum = cx.val("T")
        r_stat.gate(cx, "vector", 0)
        cx.wait("vector", "T", hsum)
        cx.op("vector", lambda e: e.tensor_scalar_mul(
            out=tmp1[0:2, :], in0=pb[4][0:2, :], scalar1=1.0 / 64.0))
        r_banks[4].record(0, cx.mark("vector"))
        cx.wait("scalar", "V")
        cx.op("scalar", lambda e: e.activation(
            out=tmp2[0:2, :], in_=tmp1[0:2, :], func=AF.Ln, bias=eps6[0:2, 0:1]))
        cx.wait("scalar", "S")
        cx.op("scalar", lambda e: e.activation(
            out=tmp1[0:2, :], in_=tmp2[0:2, :], func=AF.Exp, scale=-0.5))
        rstd_done = cx.val("S")
        r_dbc.gate(cx, "vector", 0)
        cx.wait("vector", "S", rstd_done)
        cx.op("vector", lambda e: e.tensor_copy(out=dbc[0:2, :], in_=tmp1[0:2, :]))
        r_stat.record(0, {"V": cx.val("V"), "S": cx.val("S")})
        bc_v = cx.val("V")
        cx.wait("tensor", "V", bc_v)
        r_banks[5].gate(cx, "tensor", 0)
        cx.op("tensor", lambda e: e.matmul(
            out=pb[5][:], lhsT=consts["i2t"], rhs=dbc[0:2, :],
            start=True, stop=True), inc=True)
        r_dbc.record(0, cx.mark("tensor"))
        bc_t = cx.val("T")
        cx.wait("vector", "T", bc_t)
        cx.op("vector", lambda e: e.tensor_tensor(
            out=qkt[:, 512:1024], in0=qkt[:, 0:512], in1=pb[5][:], op=ALU.mult))
        r_banks[5].record(0, cx.mark("vector"))
        r_qraw2.record(0, cx.mark("vector"))
        cx.wait("vector", "V")
        cx.op("vector", lambda e, nw_col=nw_col, extra_scale=extra_scale:
              e.tensor_scalar(out=qkt[:, 512:1024], in0=qkt[:, 512:1024],
                              scalar1=nw_col, scalar2=extra_scale,
                              op0=ALU.mult, op1=ALU.mult))
        dst_cb(cx.val("V"))

    for g in range(2):
        r_kA.gate(cx, "vector", 0)
        r_vh.gate(cx, "vector", 0)
        cx.op("vector", lambda e: e.memset(kA[64:65, :], 1.0))
        cx.op("vector", lambda e: e.memset(vh[:], 1.0))
        prep_v = cx.val("V")
        cx.wait("tensor", "V", prep_v)
        for j in range(NTC):
            ts = j * 512
            xn_done = make_xn(ts)
            for sec, base in (("k", 1024 + g * 512), ("v", 2048 + g * 512)):
                for fc in range(4):
                    fglob = (base + fc * 128) // 128
                    if fc % 2 == 0:
                        pslot, pwidx, wcast = load_slab(
                            wqkvT[:, ((base // 128) + fc) * 1024:((base // 128) + fc + 2) * 1024])
                        cx.wait("tensor", "S", wcast)
                        cx.wait("tensor", "V", xn_done)
                    slot = pslot + (fc % 2) * 1024
                    widx = pwidx
                    bank = pb[2 + fc % 2]
                    r_banks[2 + fc % 2].gate(cx, "tensor", 0)
                    for k in range(DC):
                        cx.op("tensor", lambda e, k=k, bank=bank, slot=slot: e.matmul(
                            out=bank[:], lhsT=wres[:, slot + k * 128:slot + (k + 1) * 128],
                            rhs=xn[:, k * 512:(k + 1) * 512],
                            start=(k == 0), stop=(k == DC - 1)), inc=(k == DC - 1))
                    r_wslab.record(widx, cx.mark("tensor"))
                    r_xn.record(0, cx.mark("tensor"))
                    if sec == "k":
                        def store_k(vdone, fc=fc, ts=ts):
                            nonlocal kcopy
                            h0col = (2 * fc) * T + ts
                            h1col = (2 * fc + 1) * T + ts
                            cx.wait("vector", "V", vdone)
                            cx.op("vector", lambda e, h0col=h0col: e.tensor_copy(
                                out=kA[0:64, h0col:h0col + 512],
                                in_=qkt[0:64, 512:1024]))
                            r_qfin2.record(0, cx.mark("vector"))
                            cx.wait("sync", "V", cx.val("V"))
                            sk.dma(kcopy, kA[0:64, h1col:h1col + 512],
                                   qkt[64:128, 512:1024])
                            r_qfin2.record(0, sk.mark(kcopy))
                            kcopy += 1
                        rms_norm_evac(bank, fglob, knc, 1.0, store_k)
                        r_banks[2 + fc % 2].record(0, cx.mark("scalar"))
                    else:
                        cx.wait("scalar", "T")
                        r_qraw2.gate(cx, "scalar", 0)
                        cx.op("scalar", lambda e, bank=bank, fglob=fglob: e.activation(
                            out=qkt[:, 0:512], in_=bank[:], func=AF.Identity,
                            bias=bqkvc[:, fglob:fglob + 1]))
                        r_banks[2 + fc % 2].record(0, cx.mark("scalar"))
                        ev_done = cx.val("S")
                        cx.wait("tensor", "S", ev_done)
                        r_pbh.gate(cx, "tensor", 0)
                        for tt in range(4):
                            cx.op("tensor", lambda e, tt=tt: e.transpose(
                                out=pbh[:, tt * 128:(tt + 1) * 128],
                                in_=qkt[:, tt * 128:tt * 128 + 128],
                                identity=identb[:]), inc=(tt == 3))
                        r_qraw2.record(0, cx.mark("tensor"))
                        tp_done = cx.val("T")
                        cx.wait("vector", "T", tp_done)
                        for tt in range(4):
                            tile = (ts // 128) + tt
                            c0 = tile * 520 + (2 * fc) * 65
                            c1 = tile * 520 + (2 * fc + 1) * 65
                            cx.op("vector", lambda e, tt=tt, c0=c0: e.tensor_copy(
                                out=vh[:, c0:c0 + 64],
                                in_=pbh[:, tt * 128:tt * 128 + 64]))
                            cx.op("vector", lambda e, tt=tt, c1=c1: e.tensor_copy(
                                out=vh[:, c1:c1 + 64],
                                in_=pbh[:, tt * 128 + 64:tt * 128 + 128]))
                        r_pbh.record(0, cx.mark("vector"))
        kv_ready = {"V": cx.val("V"), "T": cx.val("T"), "S": cx.val("S")}
        for i, n in enumerate(sk.names):
            kv_ready[n] = 16 * sk.counts[i]

        for j in range(NTC):
            ts = j * 512
            xn_done = make_xn(ts)
            r_qa.gate(cx, "vector", 0)
            cx.op("vector", lambda e: e.memset(qa[64:65, :], 0.0))
            for fc in range(4):
                base = g * 512
                fglob = (base + fc * 128) // 128
                if fc % 2 == 0:
                    pslot, pwidx, wcast = load_slab(
                        wqkvT[:, ((base // 128) + fc) * 1024:((base // 128) + fc + 2) * 1024])
                    cx.wait("tensor", "S", wcast)
                    cx.wait("tensor", "V", xn_done)
                slot = pslot + (fc % 2) * 1024
                widx = pwidx
                bank = pb[2 + fc % 2]
                r_banks[2 + fc % 2].gate(cx, "tensor", 0)
                for k in range(DC):
                    cx.op("tensor", lambda e, k=k, bank=bank, slot=slot: e.matmul(
                        out=bank[:], lhsT=wres[:, slot + k * 128:slot + (k + 1) * 128],
                        rhs=xn[:, k * 512:(k + 1) * 512],
                        start=(k == 0), stop=(k == DC - 1)), inc=(k == DC - 1))
                r_wslab.record(widx, cx.mark("tensor"))
                r_xn.record(0, cx.mark("tensor"))

                def store_q(vdone, fc=fc):
                    nonlocal kcopy
                    c0 = (2 * fc) * 512
                    c1 = (2 * fc + 1) * 512
                    cx.wait("vector", "V", vdone)
                    cx.op("vector", lambda e, c0=c0: e.tensor_copy(
                        out=qa[0:64, c0:c0 + 512], in_=qkt[0:64, 512:1024]))
                    r_qfin2.record(0, cx.mark("vector"))
                    cx.wait("sync", "V", cx.val("V"))
                    sk.dma(kcopy, qa[0:64, c1:c1 + 512], qkt[64:128, 512:1024])
                    r_qfin2.record(0, sk.mark(kcopy))
                    kcopy += 1
                rms_norm_evac(bank, fglob, qnc, 0.125, store_q)
                r_banks[2 + fc % 2].record(0, cx.mark("scalar"))
            qa_marks = {"V": cx.val("V")}
            for i, n in enumerate(sk.names):
                qa_marks[n] = 16 * sk.counts[i]
            cx.wait_marks("tensor", qa_marks)
            cx.wait_marks("tensor", kv_ready)
            r_oc.gate(cx, "vector", 0)

            for hl in range(8):
                hcol = hl * T
                pend = []

                def pa_tail():
                    hlp, ttp, mng_v = pend.pop(0)
                    cx.wait("tensor", "V", mng_v)
                    r_pbh.gate(cx, "tensor", 0)
                    cx.op("tensor", lambda e, ttp=ttp: e.transpose(
                        out=pbh[0:1, 0:128], in_=mng[:, ttp:ttp + 1],
                        identity=identb[:]), inc=True)
                    tr_done = cx.val("T")
                    cx.wait("vector", "T", tr_done)
                    cx.op("vector", lambda e, hlp=hlp, ttp=ttp: e.tensor_copy(
                        out=qa[64:65, hlp * 512 + ttp * 128:hlp * 512 + (ttp + 1) * 128],
                        in_=pbh[0:1, 0:128]))
                    r_pbh.record(0, cx.mark("vector"))
                    r_mng.record(ttp, cx.mark("vector"))

                for tt in range(4):
                    for sc in range(T // 512):
                        bank = pb[2 + sc % 2]
                        r_banks[2 + sc % 2].gate(cx, "tensor", 0)
                        cx.op("tensor", lambda e, hl=hl, tt=tt, sc=sc, hcol=hcol,
                              bank=bank: e.matmul(
                            out=bank[:],
                            lhsT=qa[0:64, hl * 512 + tt * 128:hl * 512 + (tt + 1) * 128],
                            rhs=kA[0:64, hcol + sc * 512:hcol + (sc + 1) * 512],
                            start=True, stop=True), inc=True)
                        smm = cx.val("T")
                        if sc == 1 and pend:
                            pa_tail()
                        cx.wait("vector", "T", smm)
                        if sc == 0:
                            cx.wait("vector", "V")
                        cx.op("vector", lambda e, bank=bank, sc=sc: e.reduce_max(
                            out=mnf8[:, sc:sc + 1], in_=bank[:], axis=AX.X,
                            negate=True))
                        cx.wait("vector", "V")
                        r_banks[2 + sc % 2].record(0, cx.mark("vector"))
                    r_mng.gate(cx, "vector", tt)
                    cx.op("vector", lambda e, tt=tt: e.tensor_reduce(
                        out=mng[:, tt:tt + 1], in_=mnf8[:, 0:T // 512], axis=AX.X,
                        op=ALU.min))
                    pend.append((hl, tt, cx.val("V")))
                while pend:
                    pa_tail()
                qa_row_v = cx.val("V")
                cx.wait("tensor", "V", qa_row_v)
                r_banks[6].gate(cx, "tensor", 0)
                NP = T // 256
                exp_of = {}

                def emit_exp_av(kp, hl=hl):
                    ppar = kp % 2
                    buf = cx.pbB if ppar == 0 else cx.pbB2
                    cx.wait("scalar", "T", exp_of[kp])
                    r_pt.gate(cx, "scalar", kp)
                    cx.op("scalar", lambda e, ppar=ppar, buf=buf: e.activation(
                        out=pt[:, ppar * 1024:(ppar + 1) * 1024], in_=buf,
                        func=AF.Exp))
                    for b in ((2, 3) if ppar == 0 else (4, 5)):
                        r_banks[b].record(0, cx.mark("scalar"))
                    exp_done = cx.val("S")
                    cx.wait("tensor", "S", exp_done)
                    for half in range(2):
                        sc = kp * 2 + half
                        cx.op("tensor", lambda e, sc=sc, ppar=ppar, half=half,
                              hl=hl: e.matmul(
                            out=pb[6][0:65, :],
                            lhsT=vh[:, sc * 520 + hl * 65:sc * 520 + (hl + 1) * 65],
                            rhs=pt[:, ppar * 1024 + half * 512:ppar * 1024 + (half + 1) * 512],
                            start=(sc == 0), stop=(sc == 2 * NP - 1)),
                            inc=(sc == 2 * NP - 1))
                    r_pt.record(kp, cx.mark("tensor"))

                for kp in range(NP):
                    ppar = kp % 2
                    for b in ((2, 3) if ppar == 0 else (4, 5)):
                        r_banks[b].gate(cx, "tensor", 0)
                    for half in range(2):
                        sc = kp * 2 + half
                        cx.op("tensor", lambda e, sc=sc, hcol=hcol, hl=hl,
                              ppar=ppar, half=half: e.matmul(
                            out=(cx.pbB if ppar == 0 else cx.pbB2)[:, half * 512:(half + 1) * 512],
                            lhsT=kA[:, hcol + sc * 128:hcol + (sc + 1) * 128],
                            rhs=qa[:, hl * 512:(hl + 1) * 512],
                            start=True, stop=True), inc=(half == 1))
                    exp_of[kp] = cx.val("T")
                    if kp >= 1:
                        emit_exp_av(kp - 1)
                emit_exp_av(NP - 1)
                av_mm = cx.val("T")
                cx.wait("vector", "T", av_mm)
                r_dbc.gate(cx, "vector", 0)
                cx.op("vector", lambda e: e.reciprocal(
                    out=tmp1[64:65, :], in_=pb[6][64:65, :]))
                cx.wait("vector", "V")
                cx.op("vector", lambda e: e.tensor_copy(
                    out=dbc[64:65, :], in_=tmp1[64:65, :]))
                rec_v = cx.val("V")
                cx.wait("tensor", "V", rec_v)
                r_banks[5].gate(cx, "tensor", 0)
                cx.op("tensor", lambda e: e.matmul(
                    out=pb[5][:], lhsT=consts["ones"][64:65, :], rhs=dbc[64:65, :],
                    start=True, stop=True), inc=True)
                r_dbc.record(0, cx.mark("tensor"))
                bc_t = cx.val("T")
                cx.wait("scalar", "T", bc_t)
                r_dbc.gate(cx, "scalar", 0)
                cx.op("scalar", lambda e: e.activation(
                    out=dbc[:], in_=pb[5][:], func=AF.Identity))
                r_banks[5].record(0, cx.mark("scalar"))
                dbc_s = cx.val("S")
                cx.wait("vector", "S", dbc_s)
                pbase = (hl % 2) * 64
                ocol = (hl // 2) * 512
                cx.op("vector", lambda e, pbase=pbase, ocol=ocol: e.tensor_tensor(
                    out=oc[pbase:pbase + 64, ocol:ocol + 512], in0=pb[6][0:64, :],
                    in1=dbc[pbase:pbase + 64, :], op=ALU.mult))
                r_banks[6].record(0, cx.mark("vector"))
                r_dbc.record(0, cx.mark("vector"))
            oc_v = cx.val("V")
            cx.wait("sync", "V", oc_v)
            for cc in range(4):
                sov.dma(ovio, OT[g * 512 + cc * 128:g * 512 + (cc + 1) * 128,
                                 ts:ts + 512],
                        oc[:, cc * 512:(cc + 1) * 512])
                ovio += 1
            for i, n in enumerate(sov.names):
                r_oc.record(0, {n: 16 * sov.counts[i]})
            r_qa.record(0, {"T": cx.val("T")})
        r_kA.record(0, {"T": cx.val("T")})
        r_vh.record(0, {"T": cx.val("T")})

    for j in range(NTC):
        ts = j * 512
        r_oc.gate(cx, "sync", 0)
        for cc in range(DC):
            sov.dma(ovio, oc[:, cc * 512:(cc + 1) * 512],
                    OT[cc * 128:(cc + 1) * 128, ts:ts + 512])
            ovio += 1
        oload_marks = {}
        for i, n in enumerate(sov.names):
            oload_marks[n] = 16 * sov.counts[i]
        cx.wait_marks("tensor", oload_marks)
        r_zt.gate(cx, "scalar", 0)
        for f in range(8):
            if f % 2 == 0:
                pslot, pwidx, wcast = load_slab(wprojT[:, f * 1024:(f + 2) * 1024])
                cx.wait("tensor", "S", wcast)
            slot = pslot + (f % 2) * 1024
            widx = pwidx
            bank = pb[2 + f % 2]
            r_banks[2 + f % 2].gate(cx, "tensor", 0)
            for k in range(DC):
                cx.op("tensor", lambda e, k=k, bank=bank, slot=slot: e.matmul(
                    out=bank[:], lhsT=wres[:, slot + k * 128:slot + (k + 1) * 128],
                    rhs=oc[:, k * 512:(k + 1) * 512],
                    start=(k == 0), stop=(k == DC - 1)), inc=(k == DC - 1))
            r_wslab.record(widx, cx.mark("tensor"))
            cx.wait("scalar", "T")
            cx.op("scalar", lambda e, f=f, bank=bank: e.activation(
                out=zt[:, f * 512:(f + 1) * 512], in_=bank[:], func=AF.Identity,
                bias=bprojc[:, f:f + 1]))
            r_banks[2 + f % 2].record(0, cx.mark("scalar"))
        zt_done = cx.val("S")
        r_oc.record(0, {"T": cx.val("T")})
        for e_ in range(DC):
            col = e_ * 512
            if e_ == 0:
                r_sq.gate(cx, "scalar", 0)
                cx.wait("scalar", "S", zt_done)
            cx.op("scalar", lambda e, col=col, e_=e_: e.activation(
                out=sqb[:, e_ * 512:(e_ + 1) * 512], in_=zt[:, col:col + 512],
                func=AF.Square))
            scast = cx.val("S")
            if e_ == 0:
                cx.wait("tensor", "S", zt_done)
                r_banks[0].gate(cx, "tensor", 0)
                r_banks[1].gate(cx, "tensor", 0)
            st, sp = (e_ == 0), (e_ == 7)
            cx.op("tensor", lambda e, col=col, st=st, sp=sp: e.matmul(
                out=pb[0][:], lhsT=ones[:], rhs=zt[:, col:col + 512],
                start=st, stop=sp), inc=sp)
            cx.wait("tensor", "S", scast)
            cx.op("tensor", lambda e, e_=e_, st=st, sp=sp: e.matmul(
                out=pb[1][:], lhsT=ones[:], rhs=sqb[:, e_ * 512:(e_ + 1) * 512],
                start=st, stop=sp), inc=sp)
            if sp:
                r_sq.record(0, cx.mark("tensor"))
        r_stat.gate(cx, "vector", 0)
        cx.wait("vector", "T")
        ln_stats_combine(cx, pb[0][:], pb[1][:], mu_s[:], rstd_s[:],
                         tmp1[:], tmp2[:], 1.0 / D)
        r_banks[0].record(0, cx.mark("vector"))
        r_banks[1].record(0, cx.mark("vector"))
        rstd2_done = cx.val("S")
        cx.wait("vector", "S", rstd2_done)
        for c in range(DC):
            par = xload % 2
            r_xf.gate(cx, "sync", xload)
            sx.dma(xload, xf[:, par * 512:(par + 1) * 512],
                   xT[c * 128:(c + 1) * 128, ts:ts + 512])
            cx.op("vector", lambda e, c=c: e.tensor_tensor(
                out=ftmp[:], in0=zt[:, c * 512:(c + 1) * 512],
                in1=mu_s[:], op=ALU.subtract))
            cx.wait("vector", "V")
            cx.op("vector", lambda e: e.tensor_tensor(
                out=ftmp2[:], in0=ftmp[:], in1=rstd_s[:], op=ALU.mult))
            sx.wait("vector", xload)
            opar = ostore % 2
            r_outf.gate(cx, "vector", ostore)
            cx.wait("vector", "V")
            cx.op("vector", lambda e, par=par, opar=opar: e.tensor_tensor(
                out=outf[:, opar * 512:(opar + 1) * 512], in0=ftmp2[:],
                in1=xf[:, par * 512:(par + 1) * 512], op=ALU.add))
            r_xf.record(xload, cx.mark("vector"))
            vadd = cx.val("V")
            cx.wait("sync", "V", vadd)
            so.dma(ostore, yT[c * 128:(c + 1) * 128, ts:ts + 512],
                   outf[:, opar * 512:(opar + 1) * 512])
            r_outf.record(ostore, so.mark(ostore))
            xload += 1
            ostore += 1
        r_zt.record(0, {"V": cx.val("V"), "S": cx.val("S"), "T": cx.val("T")})
        r_stat.record(0, {"V": cx.val("V"), "S": cx.val("S")})
    so.drain()


def build_l0_attn(T=4096):
    nc = bass.Bass("TRN2", target_bir_lowering=False)
    xT = nc.dram_tensor("xT", [D, T], F32, kind="ExternalInput").ap()
    wqkvT = nc.dram_tensor("wqkvT", [128, 24 * 1024], F32, kind="ExternalInput").ap()
    wprojT = nc.dram_tensor("wprojT", [128, 8 * 1024], F32, kind="ExternalInput").ap()
    bqkv = nc.dram_tensor("bqkv", [128, 24], F32, kind="ExternalInput").ap()
    bproj = nc.dram_tensor("bproj", [128, 8], F32, kind="ExternalInput").ap()
    qn = nc.dram_tensor("qn", [128, 1], F32, kind="ExternalInput").ap()
    kn = nc.dram_tensor("kn", [128, 1], F32, kind="ExternalInput").ap()
    i2_in = nc.dram_tensor("i2_in", [128, 2], BF16, kind="ExternalInput").ap()
    i2t_in = nc.dram_tensor("i2t_in", [128, 128], BF16, kind="ExternalInput").ap()
    ident_in = nc.dram_tensor("ident_in", [128, 128], BF16, kind="ExternalInput").ap()
    yT = nc.dram_tensor("yT", [D, T], F32, kind="ExternalOutput").ap()

    es = ExitStack()
    cx = Ctx(nc, es)
    ones = cx.sb("ones", [128, 128], BF16)
    identb = cx.sb("identb", [128, 128], BF16)
    i2 = cx.sb("i2", [128, 2], BF16)
    i2t = cx.sb("i2t", [2, 128], BF16)
    epsc = cx.sb("epsc", [128, 1], F32)
    eps6c = cx.sb("eps6c", [128, 1], F32)
    bqkvc = cx.sb("bqkvc", [128, 24], F32)
    bprojc = cx.sb("bprojc", [128, 8], F32)
    qnc = cx.sb("qnc", [128, 1], F32)
    knc = cx.sb("knc", [128, 1], F32)
    cx.eps_col = epsc[:, 0:1]
    pbig = cx.ps("pbig", [128, 2048], F32)
    cx.pb = [cx.ps("pb0", [128, 512]), cx.ps("pb1", [128, 512]),
             pbig[:, 0:512], pbig[:, 512:1024],
             pbig[:, 1024:1536], pbig[:, 1536:2048],
             cx.ps("pb6", [128, 512])]
    cx.pbB = pbig[:, 0:1024]
    cx.pbB2 = pbig[:, 1024:2048]
    cx.pbh = cx.ps("pbh", [128, 1024], BF16)[:]

    sc = Stream(cx, "const", nbuf=8)
    for i, (dst, srcap) in enumerate([(bqkvc, bqkv), (bprojc, bproj), (qnc, qn),
                                      (knc, kn), (identb, ident_in),
                                      (i2, i2_in)]):
        sc.dma(i, dst[:], srcap)
    sc.dma(6, i2t[:], i2t_in[0:2, :])
    cx.op("vector", lambda e: e.memset(ones[:], 1.0))
    cx.op("vector", lambda e: e.memset(epsc[:], LN_EPS))
    cx.op("vector", lambda e: e.memset(eps6c[:], 1e-6))
    consts_v = cx.val("V")
    sc.drain("vector")
    sc.drain("scalar")
    sc.drain("tensor")
    sc.drain("sync")
    cx.wait("tensor", "V", consts_v)
    cx.wait("scalar", "V", consts_v)
    cx.wait("vector", "V", consts_v)

    consts = {"ones": ones[:], "identb": identb[:], "i2": i2[:], "i2t": i2t[:],
              "eps6": eps6c[:], "ones1": ones[0:1, :]}
    emit_l0_attn(cx, T, xT, wqkvT, wprojT, bqkvc[:], bprojc[:],
                 qnc[:, 0:1], knc[:, 0:1], yT, consts)

    with es, nc.Block() as block:
        cx.finish(block)
    return nc



LF_DEPTH = 6
LF_FI = 2730
LF_NA = 22


def emit_l0_ffn(cx, T, xT, Wqkv6, Wo6, W16, W26, yT, consts, depth=LF_DEPTH):
    ones = consts["ones"]
    identb = consts["identb"]
    P64 = consts["p64"]
    NTC = T // 512
    NW = T // 256
    TP = T + 1

    xr = cx.sb("lf_xr", [128, DC * TP], BF16)
    share = cx.sb("lf_share", [128, 26720], BF16)
    kaug = share[0:65, 0:16 * 768]
    qaug = share[0:65, 16 * 768:16 * 768 + 16 * 512]
    vb = share[:, 20480:20480 + 6 * 1040]
    SLOTW = 22 * 128
    wslab = cx.sb("lf_wslab", [128, 4 * SLOTW], BF16)
    un = cx.sb("lf_un", [128, 8192], BF16)
    rotb = un[:, 0:2048]
    qkt = un[:, 2048:4096]
    pt = cx.sb("lf_pt", [128, 1024], BF16)
    sqb = cx.sb("lf_sqb", [128, 512], BF16)
    stt2 = cx.sb("lf_stt", [128, 512], F32)
    r_sqb = RBuf(1)
    oT = un[:, 4096:8192]
    xn = un[:, 0:8192]
    dbc = cx.sb("lf_dbc", [128, 512], BF16)
    mng = cx.sb("lf_mng", [128, 8], BF16)
    mnf = cx.sb("lf_mnf", [128, 1], F32)
    ztmp = cx.sb("lf_ztmp", [128, 2 * 512], BF16)
    sqb = cx.sb("lf_sq", [128, 4 * 512], BF16)
    xf = cx.sb("lf_xf", [128, 2 * 512], F32)
    mu_s = cx.sb("lf_mu", [128, 512], BF16)
    rstd_s = cx.sb("lf_rstd", [128, 512], BF16)
    tmp1 = cx.sb("lf_tmp1", [128, 512], F32)
    tmp2 = cx.sb("lf_tmp2", [128, 512], F32)
    ftmp = tmp1
    ftmp2 = tmp2
    outf = cx.sb("lf_outf", [128, 512], F32)
    ha = cx.sb("lf_ha", [128, 4 * 512], BF16)

    pb = cx.pb
    pbB = cx.pbB
    pbh = cx.pbh

    sx = Stream(cx, "lfx")
    sw = Stream(cx, "lfw", nbuf=4, alt=True)
    sk = Stream(cx, "lfk")
    srt = Stream(cx, "lfr")
    so = Stream(cx, "lfo")

    r_xf = RBuf(2)
    r_wf = RBuf(2)
    r_wslab = RBuf(2)
    r_wo = RBuf(1)
    r_banks = [RBuf(1) for _ in range(8)]
    r_pbB = RBuf(1)
    r_banks[2] = r_pbB
    r_banks[3] = r_pbB
    r_pbh = RBuf(1)
    r_share = RBuf(1)
    r_qkt = RBuf(1)
    r_qraw = RBuf(2)
    r_qfin = RBuf(2)
    r_pt = RBuf(1)
    r_oT = RBuf(1)
    r_rot = RBuf(1)
    r_mng = RBuf(4)
    r_dbc = RBuf(2)
    r_stat = RBuf(1)
    r_xn = RBuf(1)
    r_sq = RBuf(4)
    r_ha = RBuf(1)
    r_un = RBuf(1)
    r_ztmp = RBuf(2)
    r_outf = RBuf(1)
    r_xr = RBuf(1)

    xload = wload = kcopy = ostore = rtload = 0
    eps_col = cx.eps_col
    epsl2 = consts["epsl2"]

    def xr_view(c, ts, w, shifted):
        off = 0 if (shifted and c >= 4) else 1
        base = c * TP + off + ts
        return xr[:, base:base + w]

    def load_slab(src2d):
        nonlocal wload
        par = wload % 4
        slot = par * SLOTW
        w = src2d.shape[1]
        q = sw.queue(wload)
        r_wslab.gate(cx, q, wload)
        sw.dma(wload, wslab[:, slot:slot + w], src2d)
        sw.wait("tensor", wload)
        idx = wload
        wload += 1
        return slot, idx, 0

    for j in range(NTC):
        ts = j * 512
        for c in range(DC):
            par = xload % 2
            r_xf.gate(cx, "sync", xload)
            sx.dma(xload, xf[:, par * 512:(par + 1) * 512],
                   xT[c * 128:(c + 1) * 128, ts:ts + 512])
            sx.wait("vector", xload)
            cx.op("vector", lambda e, c=c, par=par, ts=ts: e.tensor_copy(
                out=xr_view(c, ts, 512, False), in_=xf[:, par * 512:(par + 1) * 512]))
            r_xf.record(xload, cx.mark("vector"))
            vcast = cx.val("V")
            cx.wait("scalar", "V", vcast)
            r_sq.gate(cx, "scalar", c)
            cx.op("scalar", lambda e, c=c, ts=ts: e.activation(
                out=sqb[:, (c % 4) * 512:(c % 4 + 1) * 512],
                in_=xr_view(c, ts, 512, False), func=AF.Square))
            scast = cx.val("S")
            cx.wait("tensor", "V", vcast)
            if c == 0:
                r_banks[0].gate(cx, "tensor", 0)
                r_banks[1].gate(cx, "tensor", 0)
            st, sp = (c == 0), (c == DC - 1)
            cx.op("tensor", lambda e, c=c, ts=ts, st=st, sp=sp: e.matmul(
                out=pb[0][:], lhsT=ones[:], rhs=xr_view(c, ts, 512, False),
                start=st, stop=sp), inc=sp)
            cx.wait("tensor", "S", scast)
            cx.op("tensor", lambda e, c=c, st=st, sp=sp: e.matmul(
                out=pb[1][:], lhsT=ones[:], rhs=sqb[:, (c % 4) * 512:(c % 4 + 1) * 512],
                start=st, stop=sp), inc=True)
            r_sq.record(c, cx.mark("tensor"))
            xload += 1
        r_stat.gate(cx, "vector", 0)
        cx.wait("vector", "T")
        ln_stats_combine(cx, pb[0][:], pb[1][:], mu_s[:], rstd_s[:],
                         tmp1[:], tmp2[:], 1.0 / D)
        r_banks[0].record(0, cx.mark("vector"))
        r_banks[1].record(0, cx.mark("vector"))
        rstd_done = cx.val("S")
        cx.wait("vector", "S", rstd_done)
        for c in range(DC):
            cx.op("vector", lambda e, c=c, ts=ts: e.tensor_tensor(
                out=xr_view(c, ts, 512, False), in0=xr_view(c, ts, 512, False),
                in1=mu_s[:], op=ALU.subtract))
            cx.wait("vector", "V")
            cx.op("vector", lambda e, c=c, ts=ts: e.tensor_tensor(
                out=xr_view(c, ts, 512, False), in0=xr_view(c, ts, 512, False),
                in1=rstd_s[:], op=ALU.mult))
        r_stat.record(0, {"V": cx.val("V"), "S": cx.val("S")})
    for c in range(DC):
        cx.op("vector", lambda e, c=c: e.memset(xr[:, c * TP:c * TP + 1], 0.0))
    xr_ready = cx.val("V")
    cx.wait("tensor", "V", xr_ready)
    cx.wait("scalar", "V", xr_ready)

    for L in range(depth):
        r_share.gate(cx, "vector", 0)
        cx.op("vector", lambda e: e.memset(share[:, 0:16 * 768], 0.0))
        cx.wait("vector", "V")
        cx.op("vector", lambda e: e.memset(kaug[64:65, :], 1.0))
        cx.wait("vector", "V")
        cx.op("vector", lambda e: e.memset(vb, 1.0))
        prep_v = cx.val("V")
        cx.wait("tensor", "V", prep_v)

        for jt in range(NTC):
            ts = jt * 512
            if jt == 0:
                r_un.gate(cx, "vector", 0)
                r_un.gate(cx, "scalar", 0)
                r_un.gate(cx, "sync", 0)
            r_rot.gate(cx, "sync", 0)
            for i, tab in enumerate((consts["csq"], consts["ssq"],
                                     consts["csk"], consts["ssk"])):
                srt.dma(rtload + i, rotb[:, i * 512:(i + 1) * 512],
                        tab[:, ts:ts + 512])
            rot_marks = {}
            for i, n in enumerate(srt.names):
                rot_marks[n] = 16 * srt.counts[i]
            rtload += 4
            cx.wait_marks("vector", rot_marks)

            pair_info = {}
            c1 = c2 = c3 = None

            for sec in range(3):
                for fc in range(8):
                    gidx = sec * 8 + fc
                    if gidx % 2 == 0:
                        pslot, pwidx, pwcast = load_slab(
                            Wqkv6[L][:, gidx * 1024:(gidx + 2) * 1024])
                        pair_info = {"slot": pslot, "widx": pwidx}
                        cx.wait("tensor", "V", xr_ready)
                    slot = pair_info["slot"] + (gidx % 2) * 1024
                    bank = pb[2 + fc % 2]
                    r_banks[2 + fc % 2].gate(cx, "tensor", 0)
                    for k in range(DC):
                        cx.op("tensor", lambda e, k=k, bank=bank, slot=slot,
                              ts=ts: e.matmul(
                            out=bank[:], lhsT=wslab[:, slot + k * 128:slot + (k + 1) * 128],
                            rhs=xr_view(k, ts, 512, True),
                            start=(k == 0), stop=(k == DC - 1)), inc=(k == DC - 1))
                    if gidx % 2 == 1:
                        r_wslab.record(pair_info["widx"], cx.mark("tensor"))
                    prod_t = cx.val("T")
                    h0, h1 = 2 * fc, 2 * fc + 1
                    qoff = (gidx % 2) * 1024
                    ent = {"hsum": None, "l2v": None, "ev": None}
                    if sec < 2:
                        def p1(sec=sec, fc=fc, bank=bank, prod_t=prod_t,
                               qoff=qoff, ent=ent):
                            cx.wait("scalar", "T", prod_t)
                            r_qraw.gate(cx, "scalar", qoff // 1024)
                            cx.op("scalar", lambda e: e.activation(
                                out=qkt[:, qoff:qoff + 512], in_=bank[:],
                                func=AF.Identity))
                            r_banks[2 + fc % 2].record(0, cx.mark("scalar"))
                            r_sqb.gate(cx, "scalar", 0)
                            cx.wait("scalar", "S")
                            cx.op("scalar", lambda e: e.activation(
                                out=sqb[:, 0:512], in_=qkt[:, qoff:qoff + 512],
                                func=AF.Square))
                            sq_done = cx.val("S")
                            cx.wait("tensor", "S", sq_done)
                            r_banks[4].gate(cx, "tensor", 0)
                            cx.op("tensor", lambda e: e.matmul(
                                out=pb[4][0:2, :], lhsT=consts["i2"], rhs=sqb[:, 0:512],
                                start=True, stop=True), inc=True)
                            r_sqb.record(0, cx.mark("tensor"))
                            ent["hsum"] = cx.val("T")

                        def p2(sec=sec, qoff=qoff, ent=ent):
                            cx.wait("scalar", "T", ent["hsum"])
                            cx.op("scalar", lambda e: e.activation(
                                out=stt2[0:2, :], in_=pb[4][0:2, :], func=AF.Ln,
                                bias=epsl2[0:2, 0:1]))
                            r_banks[4].record(0, cx.mark("scalar"))
                            r_dbc.gate(cx, "scalar", 0)
                            cx.wait("scalar", "S")
                            cx.op("scalar", lambda e: e.activation(
                                out=dbc[0:2, 0:512], in_=stt2[0:2, :], func=AF.Exp,
                                scale=-0.5))
                            bc_s = cx.val("S")
                            cx.wait("tensor", "S", bc_s)
                            r_banks[5].gate(cx, "tensor", 0)
                            cx.op("tensor", lambda e: e.matmul(
                                out=pb[5][:], lhsT=consts["i2t"], rhs=dbc[0:2, 0:512],
                                start=True, stop=True), inc=True)
                            r_dbc.record(0, cx.mark("tensor"))
                            bc_t = cx.val("T")
                            r_qfin.gate(cx, "vector", qoff // 1024)
                            cx.wait("vector", "T", bc_t)
                            cx.op("vector", lambda e: e.tensor_tensor(
                                out=qkt[:, qoff + 512:qoff + 1024],
                                in0=qkt[:, qoff:qoff + 512], in1=pb[5][:],
                                op=ALU.mult))
                            r_banks[5].record(0, cx.mark("vector"))
                            r_qraw.record(qoff // 1024, cx.mark("vector"))
                            cx.wait("vector", "V")
                            scol = consts["qs6"][:, L:L + 1] if sec == 0 else consts["ks6"][:, L:L + 1]
                            xsc = 8.0 if sec == 0 else 1.0
                            cx.op("vector", lambda e, scol=scol, xsc=xsc: e.tensor_scalar(
                                out=qkt[:, qoff + 512:qoff + 1024],
                                in0=qkt[:, qoff + 512:qoff + 1024],
                                scalar1=scol, scalar2=xsc, op0=ALU.mult, op1=ALU.mult))
                            ent["l2v"] = cx.val("V")

                        def p3(sec=sec, h0=h0, h1=h1, qoff=qoff, ent=ent):
                            cx.wait("tensor", "V", ent["l2v"])
                            r_banks[6].gate(cx, "tensor", 0)
                            cx.op("tensor", lambda e: e.matmul(
                                out=pb[6][:], lhsT=P64, rhs=qkt[:, qoff + 512:qoff + 1024],
                                start=True, stop=True), inc=True)
                            perm_t = cx.val("T")
                            cs_off = 0 if sec == 0 else 1024
                            cx.wait("vector", "T", perm_t)
                            cx.op("vector", lambda e, cs_off=cs_off: e.tensor_tensor(
                                out=tmp1[:], in0=qkt[:, qoff + 512:qoff + 1024],
                                in1=rotb[:, cs_off:cs_off + 512], op=ALU.mult))
                            cx.wait("vector", "V")
                            cx.op("vector", lambda e, cs_off=cs_off: e.tensor_tensor(
                                out=tmp2[:], in0=pb[6][:],
                                in1=rotb[:, cs_off + 512:cs_off + 1024], op=ALU.mult))
                            r_banks[6].record(0, cx.mark("vector"))
                            cx.wait("vector", "V")
                            cx.op("vector", lambda e: e.tensor_tensor(
                                out=qkt[:, qoff + 512:qoff + 1024], in0=tmp1[:],
                                in1=tmp2[:], op=ALU.add))
                            rot_v = cx.val("V")
                            cx.wait("vector", "V", rot_v)
                            nonlocal kcopy
                            if sec == 0:
                                c0 = h0 * 512
                                cc1 = h1 * 512
                                cx.op("vector", lambda e, c0=c0: e.tensor_copy(
                                    out=qaug[0:64, c0:c0 + 512],
                                    in_=qkt[0:64, qoff + 512:qoff + 1024]))
                                cx.wait("sync", "V", cx.val("V"))
                                sk.dma(kcopy, qaug[0:64, cc1:cc1 + 512],
                                       qkt[64:128, qoff + 512:qoff + 1024])
                                r_qfin.record(qoff // 1024, sk.mark(kcopy))
                                kcopy += 1
                            else:
                                c0 = h0 * 768 + 256
                                cc1 = h1 * 768 + 256
                                cx.op("vector", lambda e, c0=c0: e.tensor_copy(
                                    out=kaug[0:64, c0:c0 + 512],
                                    in_=qkt[0:64, qoff + 512:qoff + 1024]))
                                cx.wait("sync", "V", cx.val("V"))
                                sk.dma(kcopy, kaug[0:64, cc1:cc1 + 512],
                                       qkt[64:128, qoff + 512:qoff + 1024])
                                r_qfin.record(qoff // 1024, sk.mark(kcopy))
                                kcopy += 1
                        newchain = {"p1": p1, "p2": p2, "p3": p3}
                    else:
                        def vp1(fc=fc, bank=bank, prod_t=prod_t, qoff=qoff, ent=ent):
                            cx.wait("scalar", "T", prod_t)
                            r_qraw.gate(cx, "scalar", qoff // 1024)
                            cx.op("scalar", lambda e: e.activation(
                                out=qkt[:, qoff:qoff + 512], in_=bank[:],
                                func=AF.Identity))
                            r_banks[2 + fc % 2].record(0, cx.mark("scalar"))
                            ent["ev"] = cx.val("S")

                        def vp2(h0=h0, h1=h1, qoff=qoff, ent=ent):
                            cx.wait("tensor", "S", ent["ev"])
                            r_pbh.gate(cx, "tensor", 0)
                            for tt in range(4):
                                cx.op("tensor", lambda e, tt=tt: e.transpose(
                                    out=pbh[:, tt * 128:(tt + 1) * 128],
                                    in_=qkt[:, qoff + tt * 128:qoff + tt * 128 + 128],
                                    identity=identb[:]), inc=(tt == 3))
                            r_qraw.record(qoff // 1024, cx.mark("tensor"))
                            tp_done = cx.val("T")
                            cx.wait("vector", "T", tp_done)
                            for tt in range(4):
                                tile = 2 + tt
                                c0 = tile * 1040 + h0 * 65
                                cc1 = tile * 1040 + h1 * 65
                                cx.op("vector", lambda e, tt=tt, c0=c0: e.tensor_copy(
                                    out=vb[:, c0:c0 + 64],
                                    in_=pbh[:, tt * 128:tt * 128 + 64]))
                                cx.op("vector", lambda e, tt=tt, cc1=cc1: e.tensor_copy(
                                    out=vb[:, cc1:cc1 + 64],
                                    in_=pbh[:, tt * 128 + 64:tt * 128 + 128]))
                            r_pbh.record(0, cx.mark("vector"))
                        newchain = {"p1": vp1, "p2": vp2, "p3": None}
                    if c3 is not None:
                        if c3["p3"] is not None:
                            c3["p3"]()
                        c3 = None
                    if c2 is not None:
                        c2["p2"]()
                        c3 = c2
                        c2 = None
                    if c1 is not None:
                        c1["p1"]()
                        c2 = c1
                        c1 = None
                    c1 = newchain
            for _ in range(3):
                if c3 is not None and c3["p3"] is not None:
                    c3["p3"]()
                c3 = None
                if c2 is not None:
                    c2["p2"]()
                    c3 = c2
                    c2 = None
                if c1 is not None:
                    c1["p1"]()
                    c2 = c1
                    c1 = None
            qkv_marks = {"V": cx.val("V"), "S": cx.val("S")}
            for i, n in enumerate(sk.names):
                qkv_marks[n] = 16 * sk.counts[i]
            cx.wait_marks("tensor", qkv_marks)

            pend = []

            def pa_tail():
                hlp, ttp, mng_v = pend.pop(0)
                cx.wait("tensor", "V", mng_v)
                r_pbh.gate(cx, "tensor", 0)
                cx.op("tensor", lambda e, ttp=ttp: e.transpose(
                    out=pbh[0:1, 0:128], in_=mng[:, ttp:ttp + 1],
                    identity=identb[:]), inc=True)
                tr_done = cx.val("T")
                cx.wait("vector", "T", tr_done)
                cx.op("vector", lambda e, hlp=hlp, ttp=ttp: e.tensor_copy(
                    out=qaug[64:65, hlp * 512 + ttp * 128:hlp * 512 + (ttp + 1) * 128],
                    in_=pbh[0:1, 0:128]))
                r_pbh.record(0, cx.mark("vector"))
                r_mng.record(ttp, cx.mark("vector"))

            for hl in range(16):
                for tt in range(4):
                    w = tt // 2
                    bank = pb[2 + tt % 2]
                    r_banks[2 + tt % 2].gate(cx, "tensor", 0)
                    cx.op("tensor", lambda e, hl=hl, tt=tt, w=w, bank=bank: e.matmul(
                        out=bank[:],
                        lhsT=qaug[0:64, hl * 512 + tt * 128:hl * 512 + (tt + 1) * 128],
                        rhs=kaug[0:64, hl * 768 + w * 256:hl * 768 + w * 256 + 512],
                        start=True, stop=False), inc=False)
                    blk0 = (jt == 0 and w == 0)
                    mvar = consts["maskA0"] if blk0 else consts["maskA"]
                    cx.op("tensor", lambda e, tt=tt, bank=bank, mvar=mvar: e.matmul(
                        out=bank[:], lhsT=identb[:],
                        rhs=mvar[:, (tt % 2) * 512:(tt % 2) * 512 + 512],
                        start=False, stop=True), inc=True)
                    smm = cx.val("T")
                    if pend:
                        pa_tail()
                    cx.wait("vector", "T", smm)
                    r_mng.gate(cx, "vector", tt)
                    cx.op("vector", lambda e, bank=bank, tt=tt: e.reduce_max(
                        out=mng[:, tt:tt + 1], in_=bank[:], axis=AX.X, negate=True))
                    cx.wait("vector", "V")
                    r_banks[2 + tt % 2].record(0, cx.mark("vector"))
                    pend.append((hl, tt, cx.val("V")))
            while pend:
                pa_tail()
            qrow_v = cx.val("V")
            cx.wait("tensor", "V", qrow_v)

            units = [(hl, w) for hl in range(16) for w in range(2)]
            uinfo = {}

            def unit_head(i):
                hl, w = units[i]
                blk0 = (jt == 0 and w == 0)
                scs = (2, 3) if blk0 else (0, 1, 2, 3)
                e_off = 512 if blk0 else 0
                buf = cx.pbB if i % 2 == 0 else cx.pbB2
                bks = (2, 3) if i % 2 == 0 else (4, 5)
                for bk in bks:
                    r_banks[bk].gate(cx, "tensor", 0)
                for sc in scs:
                    st = (sc == scs[0]) or (sc == 2)
                    cx.op("tensor", lambda e, hl=hl, w=w, sc=sc, st=st, buf=buf:
                          e.matmul(
                        out=buf[:, sc * 256:(sc + 1) * 256],
                        lhsT=kaug[:, hl * 768 + w * 256 + sc * 128:hl * 768 + w * 256 + (sc + 1) * 128],
                        rhs=qaug[:, hl * 512 + w * 256:hl * 512 + w * 256 + 256],
                        start=st, stop=False), inc=False)
                for mh in range(e_off // 512, 2):
                    cx.op("tensor", lambda e, mh=mh, buf=buf: e.matmul(
                        out=buf[:, mh * 512:(mh + 1) * 512], lhsT=identb[:],
                        rhs=consts["maskB"][:, mh * 512:(mh + 1) * 512],
                        start=False, stop=True), inc=(mh == 1))
                uinfo[i] = (cx.val("T"), scs, e_off, buf, bks)

            def unit_expav(i):
                hl, w = units[i]
                st_mm, scs, e_off, buf, bks = uinfo[i]
                avb = pb[6] if i % 2 == 0 else pb[0]
                cx.wait("scalar", "T", st_mm)
                r_pt.gate(cx, "scalar", 0)
                cx.op("scalar", lambda e, e_off=e_off, buf=buf, i=i: e.activation(
                    out=pt[:, e_off:1024], in_=buf[:, e_off:1024], func=AF.Exp))
                for bk in bks:
                    r_banks[bk].record(0, cx.mark("scalar"))
                exp_s = cx.val("S")
                cx.wait("tensor", "S", exp_s)
                r_banks[6 if i % 2 == 0 else 0].gate(cx, "tensor", 0)
                for sc in scs:
                    tile = w * 2 + sc
                    st, sp = (sc == scs[0]), (sc == scs[-1])
                    cx.op("tensor", lambda e, sc=sc, tile=tile, hl=hl, st=st,
                          sp=sp, avb=avb: e.matmul(
                        out=avb[0:65, 0:256],
                        lhsT=vb[:, tile * 1040 + hl * 65:tile * 1040 + (hl + 1) * 65],
                        rhs=pt[:, sc * 256:(sc + 1) * 256],
                        start=st, stop=sp), inc=sp)
                av_mm = cx.val("T")
                r_pt.record(0, {"T": av_mm})
                uinfo[i] = (av_mm, avb)
                cx.wait("vector", "T", av_mm)
                r_dbc.gate(cx, "vector", i)
                cx.op("vector", lambda e, avb=avb, i=i: e.reciprocal(
                    out=tmp1[64:65, (i % 2) * 256:(i % 2) * 256 + 256],
                    in_=avb[64:65, 0:256]))
                cx.wait("vector", "V")
                cx.op("vector", lambda e, i=i: e.tensor_copy(
                    out=dbc[64:65, (i % 2) * 256:(i % 2) * 256 + 256],
                    in_=tmp1[64:65, (i % 2) * 256:(i % 2) * 256 + 256]))
                uinfo[i] = (av_mm, avb, cx.val("V"))

            def unit_norm(i):
                hl, w = units[i]
                av_mm, avb, rec_v = uinfo[i]
                dof = (i % 2) * 256
                cx.wait("tensor", "V", rec_v)
                r_banks[1].gate(cx, "tensor", 0)
                cx.op("tensor", lambda e, dof=dof: e.matmul(
                    out=pb[1][:, 0:256], lhsT=ones[64:65, :],
                    rhs=dbc[64:65, dof:dof + 256],
                    start=True, stop=True), inc=True)
                r_dbc.record(i, cx.mark("tensor"))
                bc_t = cx.val("T")
                cx.wait("scalar", "T", bc_t)
                r_dbc.gate(cx, "scalar", i)
                cx.op("scalar", lambda e, dof=dof: e.activation(
                    out=dbc[:, dof:dof + 256], in_=pb[1][:, 0:256], func=AF.Identity))
                r_banks[1].record(0, cx.mark("scalar"))
                dbc_s = cx.val("S")
                cx.wait("vector", "S", dbc_s)
                pbase = (hl % 2) * 64
                ocol = (hl // 2) * 512 + w * 256
                if i == 0:
                    r_oT.gate(cx, "vector", 0)
                cx.op("vector", lambda e, pbase=pbase, ocol=ocol, avb=avb,
                      dof=dof: e.tensor_tensor(
                    out=oT[pbase:pbase + 64, ocol:ocol + 256],
                    in0=avb[0:64, 0:256], in1=dbc[pbase:pbase + 64, dof:dof + 256],
                    op=ALU.mult))
                r_banks[6 if i % 2 == 0 else 0].record(0, cx.mark("vector"))
                r_dbc.record(i, cx.mark("vector"))

            for i in range(len(units)):
                unit_head(i)
                if i >= 1:
                    unit_expav(i - 1)
                if i >= 2:
                    unit_norm(i - 2)
            unit_expav(len(units) - 1)
            unit_norm(len(units) - 2)
            unit_norm(len(units) - 1)
            oT_v = cx.val("V")
            cx.wait("tensor", "V", oT_v)

            for fp in range(4):
                slot, widx, wcast = load_slab(Wo6[L][:, fp * 2048:(fp + 1) * 2048])
                cx.wait("tensor", "S", wcast)
                for fh in range(2):
                    f = fp * 2 + fh
                    bank = pb[2 + f % 2]
                    r_banks[2 + f % 2].gate(cx, "tensor", 0)
                    for k in range(DC):
                        cx.op("tensor", lambda e, fh=fh, k=k, bank=bank, slot=slot:
                              e.matmul(
                            out=bank[:],
                            lhsT=wslab[:, slot + fh * 1024 + k * 128:slot + fh * 1024 + (k + 1) * 128],
                            rhs=oT[:, k * 512:(k + 1) * 512],
                            start=(k == 0), stop=(k == DC - 1)), inc=(k == DC - 1))
                    cx.wait("vector", "T")
                    r_xr.gate(cx, "vector", 0)
                    cx.op("vector", lambda e, f=f, ts=ts, bank=bank: e.tensor_tensor(
                        out=xr_view(f, ts, 512, False), in0=xr_view(f, ts, 512, False),
                        in1=bank[:], op=ALU.add))
                    r_banks[2 + f % 2].record(0, cx.mark("vector"))
                r_wslab.record(widx, cx.mark("tensor"))
            r_oT.record(0, cx.mark("tensor"))
            xr_ready = cx.val("V")
            cx.wait("tensor", "V", xr_ready)
            cx.wait("scalar", "V", xr_ready)

            if jt < NTC - 1:
                tail_t = cx.val("T")
                cx.wait("vector", "T", tail_t)
                for hl in range(16):
                    cx.op("vector", lambda e, hl=hl: e.tensor_copy(
                        out=kaug[0:64, hl * 768:hl * 768 + 256],
                        in_=kaug[0:64, hl * 768 + 512:hl * 768 + 768]))
                cx.op("vector", lambda e: e.tensor_copy(
                    out=vb[:, 0:2 * 1040], in_=vb[:, 4 * 1040:6 * 1040]))
                tails_v = cx.val("V")
                cx.wait("tensor", "V", tails_v)
        r_share.record(0, {"T": cx.val("T"), "V": cx.val("V"), "S": cx.val("S")})
        r_un.record(0, {"T": cx.val("T"), "V": cx.val("V"), "S": cx.val("S")})

        hbuf = share[:, 0:LF_NA * 1024]
        r_share.gate(cx, "scalar", 0)
        r_share.gate(cx, "vector", 0)
        cx.op("vector", lambda e: e.memset(hbuf, 0.0))
        cx.wait("vector", "V")
        for tb in range(T // 1024):
            t0 = tb * 1024
            r_un.gate(cx, "vector", 0)
            for sub in range(2):
                ts = t0 + sub * 512
                for c in range(DC):
                    r_sq.gate(cx, "scalar", c)
                    cx.op("scalar", lambda e, c=c, ts=ts: e.activation(
                        out=sqb[:, (c % 4) * 512:(c % 4 + 1) * 512],
                        in_=xr_view(c, ts, 512, True), func=AF.Square))
                    scast = cx.val("S")
                    if c == 0:
                        r_banks[0].gate(cx, "tensor", 0)
                        r_banks[1].gate(cx, "tensor", 0)
                    st, sp = (c == 0), (c == DC - 1)
                    cx.op("tensor", lambda e, c=c, ts=ts, st=st, sp=sp: e.matmul(
                        out=pb[0][:], lhsT=ones[:], rhs=xr_view(c, ts, 512, True),
                        start=st, stop=sp), inc=sp)
                    cx.wait("tensor", "S", scast)
                    cx.op("tensor", lambda e, c=c, st=st, sp=sp: e.matmul(
                        out=pb[1][:], lhsT=ones[:], rhs=sqb[:, (c % 4) * 512:(c % 4 + 1) * 512],
                        start=st, stop=sp), inc=True)
                    r_sq.record(c, cx.mark("tensor"))
                r_stat.gate(cx, "vector", 0)
                cx.wait("vector", "T")
                ln_stats_combine(cx, pb[0][:], pb[1][:], mu_s[:], rstd_s[:],
                                 tmp1[:], tmp2[:], 1.0 / D)
                r_banks[0].record(0, cx.mark("vector"))
                r_banks[1].record(0, cx.mark("vector"))
                rstd_done = cx.val("S")
                cx.wait("vector", "S", rstd_done)
                for c in range(DC):
                    col = c * 1024 + sub * 512
                    cx.op("vector", lambda e, c=c, ts=ts, col=col: e.tensor_tensor(
                        out=xn[:, col:col + 512], in0=xr_view(c, ts, 512, True),
                        in1=mu_s[:], op=ALU.subtract))
                    cx.wait("vector", "V")
                    cx.op("vector", lambda e, col=col: e.tensor_tensor(
                        out=xn[:, col:col + 512], in0=xn[:, col:col + 512],
                        in1=rstd_s[:], op=ALU.mult))
                    cx.wait("vector", "V")
                    cx.op("vector", lambda e, col=col, c=c, L=L: e.tensor_scalar(
                        out=xn[:, col:col + 512], in0=xn[:, col:col + 512],
                        scalar1=consts["lng6"][:, L * 8 + c:L * 8 + c + 1],
                        scalar2=consts["lnb6"][:, L * 8 + c:L * 8 + c + 1],
                        op0=ALU.mult, op1=ALU.add))
                r_stat.record(0, {"V": cx.val("V"), "S": cx.val("S")})
            xn_done = cx.val("V")

            for j2 in range(LF_NA):
                aw = 128 if j2 < LF_NA - 1 else LF_FI - 128 * (LF_NA - 1)
                slot, widx, wcast = load_slab(
                    W16[L][:, (2 * j2) * 1024:(2 * j2 + 2) * 1024])
                cx.wait("tensor", "S", wcast)
                cx.wait("tensor", "V", xn_done)
                r_banks[2].gate(cx, "tensor", 0)
                r_banks[3].gate(cx, "tensor", 0)
                for half in range(2):
                    for k in range(DC):
                        cx.op("tensor", lambda e, k=k, slot=slot, half=half: e.matmul(
                            out=pb[2 + half][:],
                            lhsT=wslab[:, slot + k * 128:slot + (k + 1) * 128],
                            rhs=xn[:, k * 1024 + half * 512:k * 1024 + half * 512 + 512],
                            start=(k == 0), stop=(k == DC - 1)), inc=(k == DC - 1))
                cx.wait("scalar", "T")
                r_ha.gate(cx, "scalar", 0)
                for half in range(2):
                    cx.op("scalar", lambda e, aw=aw, j2=j2, L=L, half=half: e.activation(
                        out=ha[0:aw, half * 512:half * 512 + 512],
                        in_=pb[2 + half][0:aw, :], func=AF.Identity,
                        bias=consts["b16"][:, L * 44 + j2:L * 44 + j2 + 1][0:aw, :]))
                    r_banks[2 + half].record(0, cx.mark("scalar"))
                a_ev = cx.val("S")
                gslot = slot + 1024
                r_banks[2].gate(cx, "tensor", 0)
                r_banks[3].gate(cx, "tensor", 0)
                for half in range(2):
                    for k in range(DC):
                        cx.op("tensor", lambda e, k=k, gslot=gslot, half=half: e.matmul(
                            out=pb[2 + half][:],
                            lhsT=wslab[:, gslot + k * 128:gslot + (k + 1) * 128],
                            rhs=xn[:, k * 1024 + half * 512:k * 1024 + half * 512 + 512],
                            start=(k == 0), stop=(k == DC - 1)), inc=(k == DC - 1))
                r_wslab.record(widx, cx.mark("tensor"))
                cx.wait("scalar", "T")
                for half in range(2):
                    cx.op("scalar", lambda e, aw=aw, j2=j2, L=L, half=half: e.activation(
                        out=ha[0:aw, 1024 + half * 512:1024 + half * 512 + 512],
                        in_=pb[2 + half][0:aw, :], func=AF.Gelu,
                        bias=consts["b16"][:, L * 44 + 22 + j2:L * 44 + 22 + j2 + 1][0:aw, :]))
                    r_banks[2 + half].record(0, cx.mark("scalar"))
                hga = cx.val("S")
                cx.wait("vector", "S", hga)
                for half in range(2):
                    cx.op("vector", lambda e, aw=aw, j2=j2, half=half: e.tensor_tensor(
                        out=hbuf[0:aw, j2 * 1024 + half * 512:j2 * 1024 + half * 512 + 512],
                        in0=ha[0:aw, half * 512:half * 512 + 512],
                        in1=ha[0:aw, 1024 + half * 512:1024 + half * 512 + 512],
                        op=ALU.mult))
                r_ha.record(0, cx.mark("vector"))
            h_done = cx.val("V")

            for e_ in range(DC):
                slotA, widxA, wcastA = load_slab(
                    W26[L][:, (e_ * 2) * 1408:(e_ * 2 + 2) * 1408])
                slotB, widxB = slotA + 1408, widxA
                cx.wait("tensor", "S", wcastA)
                cx.wait("tensor", "V", h_done)
                bp = (2, 3) if e_ % 2 == 0 else (4, 5)
                r_banks[bp[0]].gate(cx, "tensor", 0)
                r_banks[bp[1]].gate(cx, "tensor", 0)
                for sub in range(2):
                    for j2 in range(LF_NA):
                        slot = slotA if j2 < 11 else slotB
                        jo = j2 if j2 < 11 else j2 - 11
                        cx.op("tensor", lambda e, jo=jo, j2=j2, slot=slot,
                              sub=sub, bk=bp[sub]: e.matmul(
                            out=pb[bk][:],
                            lhsT=wslab[:, slot + jo * 128:slot + jo * 128 + 128],
                            rhs=hbuf[:, j2 * 1024 + sub * 512:j2 * 1024 + sub * 512 + 512],
                            start=(j2 == 0), stop=(j2 == LF_NA - 1)),
                            inc=(j2 == LF_NA - 1))
                r_wslab.record(widxA, cx.mark("tensor"))
                r_wslab.record(widxB, cx.mark("tensor"))
                cx.wait("scalar", "T")
                for sub in range(2):
                    ts = t0 + sub * 512
                    r_ztmp.gate(cx, "scalar", sub)
                    cx.op("scalar", lambda e, e_=e_, L=L, bk=bp[sub], sub=sub:
                          e.activation(
                        out=ztmp[:, sub * 512:(sub + 1) * 512], in_=pb[bk][:],
                        func=AF.Identity, bias=consts["b26"][:, L * 8 + e_:L * 8 + e_ + 1]))
                    zs = cx.val("S")
                    cx.wait("vector", "S", zs)
                    r_xr.gate(cx, "vector", 0)
                    cx.op("vector", lambda e, e_=e_, ts=ts, sub=sub: e.tensor_tensor(
                        out=xr_view(e_, ts, 512, False), in0=xr_view(e_, ts, 512, False),
                        in1=ztmp[:, sub * 512:(sub + 1) * 512], op=ALU.add))
                    r_ztmp.record(sub, cx.mark("vector"))
                    r_banks[bp[sub]].record(0, cx.mark("vector"))
            xr_ready = cx.val("V")
            cx.wait("tensor", "V", xr_ready)
            cx.wait("scalar", "V", xr_ready)
            r_un.record(0, {"T": cx.val("T"), "V": cx.val("V"), "S": cx.val("S")})
        r_share.record(0, {"T": cx.val("T"), "V": cx.val("V"), "S": cx.val("S")})

    for j in range(NTC):
        ts = j * 512
        for c in range(DC):
            if c == 0:
                cx.wait("scalar", "V", xr_ready)
            r_sq.gate(cx, "scalar", c)
            cx.op("scalar", lambda e, c=c, ts=ts: e.activation(
                out=sqb[:, (c % 4) * 512:(c % 4 + 1) * 512],
                in_=xr_view(c, ts, 512, False), func=AF.Square))
            scast = cx.val("S")
            if c == 0:
                cx.wait("tensor", "V", xr_ready)
                r_banks[0].gate(cx, "tensor", 0)
                r_banks[1].gate(cx, "tensor", 0)
            st, sp = (c == 0), (c == DC - 1)
            cx.op("tensor", lambda e, c=c, ts=ts, st=st, sp=sp: e.matmul(
                out=pb[0][:], lhsT=ones[:], rhs=xr_view(c, ts, 512, False),
                start=st, stop=sp), inc=sp)
            cx.wait("tensor", "S", scast)
            cx.op("tensor", lambda e, c=c, st=st, sp=sp: e.matmul(
                out=pb[1][:], lhsT=ones[:], rhs=sqb[:, (c % 4) * 512:(c % 4 + 1) * 512],
                start=st, stop=sp), inc=True)
            r_sq.record(c, cx.mark("tensor"))
        r_stat.gate(cx, "vector", 0)
        cx.wait("vector", "T")
        ln_stats_combine(cx, pb[0][:], pb[1][:], mu_s[:], rstd_s[:],
                         tmp1[:], tmp2[:], 1.0 / D)
        r_banks[0].record(0, cx.mark("vector"))
        r_banks[1].record(0, cx.mark("vector"))
        rstd2_done = cx.val("S")
        cx.wait("vector", "S", rstd2_done)
        for c in range(DC):
            par = xload % 2
            r_xf.gate(cx, "sync", xload)
            sx.dma(xload, xf[:, par * 512:(par + 1) * 512],
                   xT[c * 128:(c + 1) * 128, ts:ts + 512])
            cx.op("vector", lambda e, c=c, ts=ts: e.tensor_tensor(
                out=ftmp[:], in0=xr_view(c, ts, 512, False), in1=mu_s[:],
                op=ALU.subtract))
            cx.wait("vector", "V")
            cx.op("vector", lambda e: e.tensor_tensor(
                out=ftmp2[:], in0=ftmp[:], in1=rstd_s[:], op=ALU.mult))
            sx.wait("vector", xload)
            r_outf.gate(cx, "vector", 0)
            cx.wait("vector", "V")
            cx.op("vector", lambda e, par=par: e.tensor_tensor(
                out=outf[:], in0=ftmp2[:],
                in1=xf[:, par * 512:(par + 1) * 512], op=ALU.add))
            r_xf.record(xload, cx.mark("vector"))
            vadd = cx.val("V")
            cx.wait("sync", "V", vadd)
            so.dma(ostore, yT[c * 128:(c + 1) * 128, ts:ts + 512], outf[:])
            r_outf.record(0, so.mark(ostore))
            xload += 1
            ostore += 1
        r_stat.record(0, {"V": cx.val("V"), "S": cx.val("S")})
    so.drain()


def build_l0_ffn(T=4096, depth=LF_DEPTH):
    nc = bass.Bass("TRN2", target_bir_lowering=False)
    xT = nc.dram_tensor("xT", [D, T], F32, kind="ExternalInput").ap()
    Wqkv = nc.dram_tensor("Wqkv", [128, LF_DEPTH * 24 * 1024], F32, kind="ExternalInput").ap()
    Wo = nc.dram_tensor("Wo", [128, LF_DEPTH * 8 * 1024], F32, kind="ExternalInput").ap()
    W1 = nc.dram_tensor("W1", [128, LF_DEPTH * 44 * 1024], F32, kind="ExternalInput").ap()
    W2 = nc.dram_tensor("W2", [128, LF_DEPTH * 16 * 1408], F32, kind="ExternalInput").ap()
    qs6 = nc.dram_tensor("qs6", [128, LF_DEPTH], F32, kind="ExternalInput").ap()
    ks6 = nc.dram_tensor("ks6", [128, LF_DEPTH], F32, kind="ExternalInput").ap()
    lng6 = nc.dram_tensor("lng6", [128, LF_DEPTH * 8], F32, kind="ExternalInput").ap()
    lnb6 = nc.dram_tensor("lnb6", [128, LF_DEPTH * 8], F32, kind="ExternalInput").ap()
    b16 = nc.dram_tensor("b16", [128, LF_DEPTH * 44], F32, kind="ExternalInput").ap()
    b26 = nc.dram_tensor("b26", [128, LF_DEPTH * 8], F32, kind="ExternalInput").ap()
    csq = nc.dram_tensor("csq", [128, T], BF16, kind="ExternalInput").ap()
    ssq = nc.dram_tensor("ssq", [128, T], BF16, kind="ExternalInput").ap()
    csk = nc.dram_tensor("csk", [128, T], BF16, kind="ExternalInput").ap()
    ssk = nc.dram_tensor("ssk", [128, T], BF16, kind="ExternalInput").ap()
    maskA = nc.dram_tensor("maskA", [128, 2 * 512], BF16, kind="ExternalInput").ap()
    maskA0 = nc.dram_tensor("maskA0", [128, 2 * 512], BF16, kind="ExternalInput").ap()
    maskB = nc.dram_tensor("maskB", [128, 4 * 256], BF16, kind="ExternalInput").ap()
    p64 = nc.dram_tensor("p64", [128, 128], BF16, kind="ExternalInput").ap()
    i2_in = nc.dram_tensor("i2_in", [128, 2], BF16, kind="ExternalInput").ap()
    i2t_in = nc.dram_tensor("i2t_in", [128, 128], BF16, kind="ExternalInput").ap()
    ident_in = nc.dram_tensor("ident_in", [128, 128], BF16, kind="ExternalInput").ap()
    yT = nc.dram_tensor("yT", [D, T], F32, kind="ExternalOutput").ap()

    es = ExitStack()
    cx = Ctx(nc, es)
    ones = cx.sb("ones", [128, 128], BF16)
    identb = cx.sb("identb", [128, 128], BF16)
    p64b = cx.sb("p64b", [128, 128], BF16)
    i2 = cx.sb("i2", [128, 2], BF16)
    i2t = cx.sb("i2t", [2, 128], BF16)
    epsc = cx.sb("epsc", [128, 1], F32)
    epsl2 = cx.sb("epsl2", [128, 1], F32)
    cqs = cx.sb("cqs", [128, LF_DEPTH], F32)
    cks = cx.sb("cks", [128, LF_DEPTH], F32)
    clng = cx.sb("clng", [128, LF_DEPTH * 8], F32)
    clnb = cx.sb("clnb", [128, LF_DEPTH * 8], F32)
    cb1 = cx.sb("cb1", [128, LF_DEPTH * 44], F32)
    cb2 = cx.sb("cb2", [128, LF_DEPTH * 8], F32)
    cmaskA = cx.sb("cmaskA", [128, 2 * 512], BF16)
    cmaskA0 = cx.sb("cmaskA0", [128, 2 * 512], BF16)
    cmaskB = cx.sb("cmaskB", [128, 4 * 256], BF16)
    cx.eps_col = epsc[:, 0:1]
    pbig = cx.ps("pbig", [128, 2048], F32)
    cx.pb = [cx.ps("pb0", [128, 512]), cx.ps("pb1", [128, 512]),
             pbig[:, 0:512], pbig[:, 512:1024],
             pbig[:, 1024:1536], pbig[:, 1536:2048],
             cx.ps("pb6", [128, 512])]
    cx.pbB = pbig[:, 0:1024]
    cx.pbB2 = pbig[:, 1024:2048]
    cx.pbh = cx.ps("pbh", [128, 1024], BF16)[:]

    sc = Stream(cx, "const", nbuf=4)
    loads = [(i2, i2_in), (identb, ident_in), (p64b, p64), (cqs, qs6),
             (cks, ks6), (clng, lng6), (clnb, lnb6), (cb1, b16), (cb2, b26),
             (cmaskA, maskA), (cmaskA0, maskA0), (cmaskB, maskB)]
    for i, (dst, srcap) in enumerate(loads):
        sc.dma(i, dst[:], srcap)
    sc.dma(len(loads), i2t[:], i2t_in[0:2, :])
    cx.op("vector", lambda e: e.memset(ones[:], 1.0))
    cx.op("vector", lambda e: e.memset(epsc[:], LN_EPS))
    cx.op("vector", lambda e: e.memset(epsl2[:], 1e-24))
    consts_v = cx.val("V")
    for eng in ("vector", "scalar", "tensor", "sync"):
        sc.drain(eng)
    cx.wait("tensor", "V", consts_v)
    cx.wait("scalar", "V", consts_v)
    cx.wait("vector", "V", consts_v)

    consts = {"ones": ones[:], "identb": identb[:], "p64": p64b[:],
              "i2": i2[:], "i2t": i2t[:], "epsl2": epsl2[:],
              "qs6": cqs[:], "ks6": cks[:], "lng6": clng[:], "lnb6": clnb[:],
              "b16": cb1[:], "b26": cb2[:], "maskA": cmaskA[:],
              "maskA0": cmaskA0[:], "maskB": cmaskB[:],
              "csq": csq, "ssq": ssq, "csk": csk, "ssk": ssk}
    Wqkv6 = [Wqkv[:, L * 24576:(L + 1) * 24576] for L in range(LF_DEPTH)]
    Wo6 = [Wo[:, L * 8192:(L + 1) * 8192] for L in range(LF_DEPTH)]
    W16 = [W1[:, L * 45056:(L + 1) * 45056] for L in range(LF_DEPTH)]
    W26 = [W2[:, L * 22528:(L + 1) * 22528] for L in range(LF_DEPTH)]
    emit_l0_ffn(cx, T, xT, Wqkv6, Wo6, W16, W26, yT, consts, depth=depth)

    with es, nc.Block() as block:
        cx.finish(block)
    return nc



def _pre_slab(W, cw=128):
    K, F = W.shape
    nk = K // 128
    nf = F // cw
    t = W.reshape(nk, 128, nf, cw).transpose(1, 2, 0, 3)
    return np.ascontiguousarray(t.reshape(128, nf * nk * cw)).astype(np.float32)


def _pre_w1_l0ffn(W1):
    FI = LF_FI
    pad = np.zeros((1024, 44 * 128), np.float32)
    for j2 in range(22):
        aw = 128 if j2 < 21 else FI - 21 * 128
        pad[:, (2 * j2) * 128:(2 * j2) * 128 + aw] = W1[:, j2 * 128:j2 * 128 + aw]
        pad[:, (2 * j2 + 1) * 128:(2 * j2 + 1) * 128 + aw] = W1[:, FI + j2 * 128:FI + j2 * 128 + aw]
    return _pre_slab(pad)


def _pre_w2_l0ffn(W2):
    pad = np.zeros((2816, 1024), np.float32)
    pad[:2730] = W2
    blocks = []
    for e in range(8):
        for hh in range(2):
            blocks.append(_pre_slab(pad[hh * 1408:(hh + 1) * 1408,
                                        e * 128:(e + 1) * 128]))
    return np.ascontiguousarray(np.concatenate(blocks, 1))


def _pre_o1w(Wo):
    return _pre_slab(Wo)


def _pre_w2_l1ffn(w2):
    blocks = []
    for e in range(8):
        for hh in range(2):
            blocks.append(_pre_slab(w2[hh * 2048:(hh + 1) * 2048,
                                       e * 128:(e + 1) * 128]))
    return np.ascontiguousarray(np.concatenate(blocks, 1))


def emit_weight_convert(cx, wscr, plan, cvb):
    cvf, cvh = cvb
    scv_in = Stream(cx, "wcvi", nbuf=2)
    scv_out = Stream(cx, "wcvo", nbuf=2)
    marks = []
    off = 0
    i = 0
    CH = 1024
    r_cvf = RBuf(1)
    r_cvh = RBuf(1)
    for src_ap, ncols in plan:
        for c0 in range(0, ncols, CH):
            w = min(CH, ncols - c0)
            par = 0
            r_cvf.gate(cx, "gpsimd", i)
            scv_in.dma(i, cvf[:, par * CH:par * CH + w], src_ap[:, c0:c0 + w],
                       engine="gpsimd")
            scv_in.wait("gpsimd", i)
            r_cvh.gate(cx, "gpsimd", i)
            cx.op("gpsimd", lambda e, par=par, w=w: e.tensor_copy(
                out=cvh[:, par * CH:par * CH + w], in_=cvf[:, par * CH:par * CH + w]))
            r_cvf.record(i, cx.mark("gpsimd"))
            cx.wait("gpsimd", "P")
            scv_out.dma(i, wscr[:, off + c0:off + c0 + w],
                        cvh[:, par * CH:par * CH + w], engine="gpsimd")
            r_cvh.record(i, scv_out.mark(i))
            i += 1
        m = {}
        for j, n in enumerate(scv_out.names):
            m[n] = 16 * scv_out.counts[j]
        marks.append((off, ncols, m))
        off += ncols
    return marks


def _layout_rows(b):
    k = b.shape[0] // 128
    return np.ascontiguousarray(b.reshape(k, 128).T).astype(np.float32)


def _l0attn_consts():
    import ml_dtypes
    i2 = np.zeros((128, 2), np.float32); i2[:64, 0] = 1; i2[64:, 1] = 1
    i2t = np.zeros((128, 128), np.float32); i2t[0, :64] = 1; i2t[1, 64:] = 1
    bf = lambda a: a.astype(ml_dtypes.bfloat16)
    return dict(i2_in=bf(i2), i2t_in=bf(i2t), ident_in=bf(np.eye(128)))


def _l0ffn_consts(T):
    import ml_dtypes
    DH, WIN = 64, 256
    half = np.arange(0, DH, 2, dtype=np.float64)
    inv_freq = 1.0 / 10000.0 ** (half / DH)
    t = np.arange(T, dtype=np.float64)
    freqs = t[:, None] * inv_freq[None, :]
    freqs = np.concatenate([freqs, freqs], -1)
    base_scale = (half + 0.4 * DH) / (1.4 * DH)
    power = (t - T // 2) / (WIN // 2)
    scale = base_scale[None, :] ** power[:, None]
    scale = np.concatenate([scale, scale], -1)
    cos, sin = np.cos(freqs), np.sin(freqs)
    def tab(m):
        m64 = np.ascontiguousarray(m.T)
        return np.vstack([m64, m64]).astype(ml_dtypes.bfloat16)
    p64 = np.zeros((128, 128))
    for m in range(32):
        p64[m + 32, m] = -1
        p64[m, m + 32] = 1
    p64[64:, 64:] = p64[:64, :64]
    mA = np.zeros((128, 1024)); mA0 = np.zeros((128, 1024))
    for p in range(2):
        i_rel = p * 128 + np.arange(128)[:, None]
        j_rel = np.arange(512)[None, :] - 256
        bad = (i_rel - j_rel < 0) | (i_rel - j_rel > 256)
        mA[:, p * 512:(p + 1) * 512] = np.where(bad, -1e30, 0.0)
        bad0 = bad | (np.arange(512)[None, :] < 256)
        mA0[:, p * 512:(p + 1) * 512] = np.where(bad0, -1e30, 0.0)
    mB = np.zeros((128, 1024))
    for sc in range(4):
        j_rel = sc * 128 + np.arange(128)[:, None] - 256
        i_rel = np.arange(256)[None, :]
        bad = (i_rel - j_rel < 0) | (i_rel - j_rel > 256)
        mB[:, sc * 256:(sc + 1) * 256] = np.where(bad, -1e30, 0.0)
    i2 = np.zeros((128, 2)); i2[:64, 0] = 1; i2[64:, 1] = 1
    i2t = np.zeros((128, 128)); i2t[0, :64] = 1; i2t[1, 64:] = 1
    bf = lambda a: a.astype(ml_dtypes.bfloat16)
    return dict(csq=tab(cos * scale), ssq=tab(sin * scale),
                csk=tab(cos / scale), ssk=tab(sin / scale), p64=bf(p64),
                maskA=bf(mA), maskA0=bf(mA0), maskB=bf(mB),
                i2_in=bf(i2), i2t_in=bf(i2t), ident_in=bf(np.eye(128)))


def _pack_b1(b1):
    FI = LF_FI
    out = np.zeros((128, 44), np.float32)
    for j2 in range(22):
        aw = 128 if j2 < 21 else FI - 21 * 128
        out[:aw, j2] = b1[j2 * 128:j2 * 128 + aw]
        out[:aw, 22 + j2] = b1[FI + j2 * 128:FI + j2 * 128 + aw]
    return out


def _l1attn_consts(T):
    import ml_dtypes
    inv_freq = 1.0 / 10000.0 ** (np.arange(0, 256, 2) / 256)
    fr = np.arange(T)[:, None] * inv_freq[None, :]
    gam = GAMMAS
    i_loc = np.arange(512) % 128
    gq = np.zeros((128, RET_H * 512), np.float64)
    gk = np.zeros((128, RET_H * 512), np.float64)
    for h in range(RET_H):
        gq[:, h * 512:(h + 1) * 512] = (gam[h] ** (i_loc + 1.0) / 16.0)[None, :]
        gk[:, h * 512:(h + 1) * 512] = (gam[h] ** (-(i_loc + 1.0)))[None, :]
    jj, ii = np.meshgrid(np.arange(128), np.arange(128), indexing="ij")
    bf = lambda a: a.astype(ml_dtypes.bfloat16)
    return dict(cos1=bf(np.ascontiguousarray(np.cos(fr).T)),
                sin1=bf(np.ascontiguousarray(np.sin(fr).T)),
                gq=bf(gq), gk=bf(gk), maskT=bf((ii >= jj).astype(np.float64)),
                ident_in=bf(np.eye(128)))


_BUILT = {}


def _get(name, builder):
    if name not in _BUILT:
        _BUILT[name] = builder()
    return _BUILT[name]


def kernel_unfused(L0_attn_x, L0_attn_w_qkv, L0_attn_b_qkv, L0_attn_qn_w, L0_attn_kn_w,
           L0_attn_w_proj, L0_attn_b_proj, L0_ffn_Wqkv, L0_ffn_q_scale,
           L0_ffn_k_scale, L0_ffn_Wo, L0_ffn_ln_g, L0_ffn_ln_b, L0_ffn_W1,
           L0_ffn_b1, L0_ffn_W2, L0_ffn_b2, L1_attn_Wq, L1_attn_Wk, L1_attn_Wv,
           L1_attn_Wg, L1_attn_Wo, L1_attn_gn_weight, L1_ffn_w1, L1_ffn_b1,
           L1_ffn_w2, L1_ffn_b2):
    T = 4096
    B = 8
    f32 = lambda a: np.ascontiguousarray(np.asarray(a), dtype=np.float32)
    xTs = [np.ascontiguousarray(f32(L0_attn_x[b]).T) for b in range(B)]

    nc = _get("l0attn", lambda: build_l0_attn(T))
    common = dict(
        wqkvT=_pre_slab(np.ascontiguousarray(f32(L0_attn_w_qkv).T)),
        wprojT=_pre_slab(np.ascontiguousarray(f32(L0_attn_w_proj).T)),
        bqkv=_layout_rows(f32(L0_attn_b_qkv)),
        bproj=_layout_rows(f32(L0_attn_b_proj)),
        qn=np.tile(f32(L0_attn_qn_w), 2)[:, None],
        kn=np.tile(f32(L0_attn_kn_w), 2)[:, None],
        **_l0attn_consts())
    res = run_bass_kernel_spmd(nc, [dict(xT=xTs[b], **common) for b in range(B)],
                               core_ids=list(range(B)))
    xTs = [res.results[b]["yT"] for b in range(B)]

    nc = _get("l0ffn", lambda: build_l0_ffn(T))
    Wqkv6 = f32(L0_ffn_Wqkv).reshape(LF_DEPTH * D, 3072)
    Wo6 = f32(L0_ffn_Wo).reshape(LF_DEPTH * D, D)
    W16 = f32(L0_ffn_W1).reshape(LF_DEPTH * D, 5460)
    W26 = f32(L0_ffn_W2).reshape(LF_DEPTH * LF_FI, D)
    qs = f32(L0_ffn_q_scale); ks = f32(L0_ffn_k_scale)
    lng = f32(L0_ffn_ln_g); lnb = f32(L0_ffn_ln_b)
    b1 = f32(L0_ffn_b1); b2 = f32(L0_ffn_b2)
    common = dict(
        Wqkv=Wqkv6, Wo=Wo6, W1=W16, W2=W26,
        qs6=np.stack([np.tile(qs[i], 2) for i in range(LF_DEPTH)], 1),
        ks6=np.stack([np.tile(ks[i], 2) for i in range(LF_DEPTH)], 1),
        lng6=np.concatenate([_layout_rows(lng[i]) for i in range(LF_DEPTH)], 1),
        lnb6=np.concatenate([_layout_rows(lnb[i]) for i in range(LF_DEPTH)], 1),
        b16=np.concatenate([_pack_b1(b1[i]) for i in range(LF_DEPTH)], 1),
        b26=np.concatenate([_layout_rows(b2[i]) for i in range(LF_DEPTH)], 1),
        **_l0ffn_consts(T))
    res = run_bass_kernel_spmd(nc, [dict(xT=xTs[b], **common) for b in range(B)],
                               core_ids=list(range(B)))
    xTs = [res.results[b]["yT"] for b in range(B)]

    nc = _get("l1attn", lambda: build_l1_attn(T))
    common = dict(Wq=f32(L1_attn_Wq), Wk=f32(L1_attn_Wk), Wv=f32(L1_attn_Wv),
                  Wg=f32(L1_attn_Wg), Wo=f32(L1_attn_Wo),
                  gnw=_layout_rows(f32(L1_attn_gn_weight)),
                  **_l1attn_consts(T))
    res = run_bass_kernel_spmd(nc, [dict(xT=xTs[b], **common) for b in range(B)],
                               core_ids=list(range(B)))
    xTs = [res.results[b]["yT"] for b in range(B)]

    nc = _get("l1ffn", lambda: build_l1_ffn(T))
    common = dict(w1=f32(L1_ffn_w1), w2=f32(L1_ffn_w2),
                  b1l=_layout_rows(f32(L1_ffn_b1)),
                  b2l=_layout_rows(f32(L1_ffn_b2)))
    res = run_bass_kernel_spmd(nc, [dict(xT=xTs[b], **common) for b in range(B)],
                               core_ids=list(range(B)))
    out = np.stack([np.ascontiguousarray(res.results[b]["yT"].T) for b in range(B)])
    return out



def build_fused(T=4096):
    nc = bass.Bass("TRN2", target_bir_lowering=False)
    dt = lambda n, s, d=F32, k="ExternalInput": nc.dram_tensor(n, s, d, kind=k).ap()
    xT = dt("xT", [D, T])
    wqkvT = dt("wqkvT", [128, 24 * 1024])
    wprojT = dt("wprojT", [128, 8 * 1024])
    bqkv = dt("bqkv", [128, 24])
    bproj = dt("bproj", [128, 8])
    qn = dt("qn", [128, 1])
    kn = dt("kn", [128, 1])
    Wqkv = dt("Wqkv", [128, LF_DEPTH * 24 * 1024])
    Wo = dt("Wo", [128, LF_DEPTH * 8 * 1024])
    W1 = dt("W1", [128, LF_DEPTH * 44 * 1024])
    W2 = dt("W2", [128, LF_DEPTH * 16 * 1408])
    qs6 = dt("qs6", [128, LF_DEPTH])
    ks6 = dt("ks6", [128, LF_DEPTH])
    lng6 = dt("lng6", [128, LF_DEPTH * 8])
    lnb6 = dt("lnb6", [128, LF_DEPTH * 8])
    b16 = dt("b16", [128, LF_DEPTH * 44])
    b26 = dt("b26", [128, LF_DEPTH * 8])
    csq = dt("csq", [128, T], BF16)
    ssq = dt("ssq", [128, T], BF16)
    csk = dt("csk", [128, T], BF16)
    ssk = dt("ssk", [128, T], BF16)
    maskA = dt("maskA", [128, 2 * 512], BF16)
    maskA0 = dt("maskA0", [128, 2 * 512], BF16)
    maskB = dt("maskB", [128, 4 * 256], BF16)
    p64 = dt("p64", [128, 128], BF16)
    q1w = dt("q1w", [128, 8 * 1024])
    k1w = dt("k1w", [128, 8 * 1024])
    v1w = dt("v1w", [128, 16 * 1024])
    g1w = dt("g1w", [128, 16 * 1024])
    o1w = dt("o1w", [128, 8 * 2048])
    cos1 = dt("cos1", [128, T], BF16)
    sin1 = dt("sin1", [128, T], BF16)
    gq = dt("gq", [128, RET_H * 512], BF16)
    gk = dt("gk", [128, RET_H * 512], BF16)
    maskT = dt("maskT", [128, 128], BF16)
    gnw = dt("gnw", [128, 4])
    fw1 = dt("fw1", [128, 32 * 1024])
    fw2 = dt("fw2", [128, 16 * 2048])
    fb1 = dt("fb1", [128, 32])
    fb2 = dt("fb2", [128, 8])
    i2_in = dt("i2_in", [128, 2], BF16)
    i2t_in = dt("i2t_in", [128, 128], BF16)
    ident_in = dt("ident_in", [128, 128], BF16)
    wscr = dt("wscr", [128, 765952], BF16, "Internal")
    s1 = dt("s1", [D, T], F32, "Internal")
    s2 = dt("s2", [D, T], F32, "Internal")
    s3 = dt("s3", [D, T], F32, "Internal")
    yT = dt("yT", [D, T], F32, "ExternalOutput")

    es = ExitStack()
    cx = Ctx(nc, es)
    ones = cx.sb("ones", [128, 128], BF16)
    identb = cx.sb("identb", [128, 128], BF16)
    p64b = cx.sb("p64b", [128, 128], BF16)
    i2 = cx.sb("i2", [128, 2], BF16)
    i2t = cx.sb("i2t", [2, 128], BF16)
    epsc = cx.sb("epsc", [128, 1], F32)
    eps6c = cx.sb("eps6c", [128, 1], F32)
    epsl2 = cx.sb("epsl2", [128, 1], F32)
    cx.eps_col = epsc[:, 0:1]
    pbig = cx.ps("pbig", [128, 2048], F32)
    cx.pb = [cx.ps("pb0", [128, 512]), cx.ps("pb1", [128, 512]),
             pbig[:, 0:512], pbig[:, 512:1024],
             pbig[:, 1024:1536], pbig[:, 1536:2048],
             cx.ps("pb6", [128, 512])]
    cx.pbB = pbig[:, 0:1024]
    cx.pbB2 = pbig[:, 1024:2048]
    cx.pbh = cx.ps("pbh", [128, 1024], BF16)[:]

    sc = Stream(cx, "c0", nbuf=4)
    for i, (dst, srcap) in enumerate([(identb, i2_in), (i2, i2_in),
                                      (p64b, p64)]):
        pass
    sc.dma(0, identb[:], ident_in)
    sc.dma(1, i2[:], i2_in)
    sc.dma(2, p64b[:], p64)
    sc.dma(3, i2t[:], i2t_in[0:2, :])
    cx.op("vector", lambda e: e.memset(ones[:], 1.0))
    cx.op("vector", lambda e: e.memset(epsc[:], LN_EPS))
    cx.op("vector", lambda e: e.memset(eps6c[:], 1e-6))
    cx.op("vector", lambda e: e.memset(epsl2[:], 1e-24))
    consts_v = cx.val("V")
    for eng in ("vector", "scalar", "tensor", "sync"):
        sc.drain(eng)
    cx.wait("tensor", "V", consts_v)
    cx.wait("scalar", "V", consts_v)
    cx.wait("vector", "V", consts_v)

    cx.push_scope()
    cvf = cx.sb("cvf", [128, 1024], F32)
    cvh = cx.sb("cvh", [128, 1024], BF16)
    plan = [(wqkvT, 24576), (wprojT, 8192), (Wqkv, 6 * 24576), (Wo, 6 * 8192),
            (W1, 6 * 45056), (W2, 6 * 22528), (q1w, 8192), (k1w, 8192),
            (v1w, 16384), (g1w, 16384), (o1w, 16384), (fw1, 32768),
            (fw2, 32768)]
    cmarks = emit_weight_convert(cx, wscr, plan, (cvf, cvh))
    offs = {}
    names = ["wqkvT", "wprojT", "Wqkv", "Wo", "W1", "W2", "q1w", "k1w",
             "v1w", "g1w", "o1w", "fw1", "fw2"]
    for nm, (off, ncols, m) in zip(names, cmarks):
        offs[nm] = (off, ncols)
    cx.convm_l0a = cmarks[1][2]
    convm_all = cmarks[-1][2]

    def wsl(nm):
        off, ncols = offs[nm]
        return wscr[:, off:off + ncols]
    bqkvc = cx.sb("bqkvc", [128, 24], F32)
    bprojc = cx.sb("bprojc", [128, 8], F32)
    qnc = cx.sb("qnc", [128, 1], F32)
    knc = cx.sb("knc", [128, 1], F32)
    s1c = Stream(cx, "c1", nbuf=4)
    s1c.dma(0, bqkvc[:], bqkv)
    s1c.dma(1, bprojc[:], bproj)
    s1c.dma(2, qnc[:], qn)
    s1c.dma(3, knc[:], kn)
    for eng in ("vector", "scalar", "tensor", "sync"):
        s1c.drain(eng)
    consts = {"ones": ones[:], "identb": identb[:], "i2": i2[:], "i2t": i2t[:],
              "eps6": eps6c[:], "ones1": ones[0:1, :]}
    emit_l0_attn(cx, T, xT, wsl("wqkvT"), wsl("wprojT"), bqkvc[:], bprojc[:],
                 qnc[:, 0:1], knc[:, 0:1], s1, consts)
    cx.barrier()
    cx.pop_scope()

    cx.push_scope()
    for eng in ("sync", "gpsimd", "tensor"):
        cx.wait_marks(eng, convm_all)
    cqs = cx.sb("cqs", [128, LF_DEPTH], F32)
    cks = cx.sb("cks", [128, LF_DEPTH], F32)
    clng = cx.sb("clng", [128, LF_DEPTH * 8], F32)
    clnb = cx.sb("clnb", [128, LF_DEPTH * 8], F32)
    cb1 = cx.sb("cb1", [128, LF_DEPTH * 44], F32)
    cb2 = cx.sb("cb2", [128, LF_DEPTH * 8], F32)
    cmaskA = cx.sb("cmaskA", [128, 2 * 512], BF16)
    cmaskA0 = cx.sb("cmaskA0", [128, 2 * 512], BF16)
    cmaskB = cx.sb("cmaskB", [128, 4 * 256], BF16)
    s2c = Stream(cx, "c2", nbuf=4)
    for i, (d2, s_) in enumerate([(cqs, qs6), (cks, ks6), (clng, lng6),
                                  (clnb, lnb6), (cb1, b16), (cb2, b26),
                                  (cmaskA, maskA), (cmaskA0, maskA0),
                                  (cmaskB, maskB)]):
        s2c.dma(i, d2[:], s_)
    for eng in ("vector", "scalar", "tensor", "sync"):
        s2c.drain(eng)
    consts = {"ones": ones[:], "identb": identb[:], "p64": p64b[:],
              "i2": i2[:], "i2t": i2t[:], "epsl2": epsl2[:],
              "qs6": cqs[:], "ks6": cks[:], "lng6": clng[:], "lnb6": clnb[:],
              "b16": cb1[:], "b26": cb2[:], "maskA": cmaskA[:],
              "maskA0": cmaskA0[:], "maskB": cmaskB[:],
              "csq": csq, "ssq": ssq, "csk": csk, "ssk": ssk}
    Wq_s, Wo_s, W1_s, W2_s = wsl("Wqkv"), wsl("Wo"), wsl("W1"), wsl("W2")
    Wqkv6 = [Wq_s[:, L * 24576:(L + 1) * 24576] for L in range(LF_DEPTH)]
    Wo6 = [Wo_s[:, L * 8192:(L + 1) * 8192] for L in range(LF_DEPTH)]
    W16 = [W1_s[:, L * 45056:(L + 1) * 45056] for L in range(LF_DEPTH)]
    W26 = [W2_s[:, L * 22528:(L + 1) * 22528] for L in range(LF_DEPTH)]
    emit_l0_ffn(cx, T, s1, Wqkv6, Wo6, W16, W26, s2, consts)
    cx.barrier()
    cx.pop_scope()

    cx.push_scope()
    c_cos = cx.sb("c_cos", [128, T], BF16)
    c_sin = cx.sb("c_sin", [128, T], BF16)
    c_gq = cx.sb("c_gq", [128, RET_H * 512], BF16)
    c_gk = cx.sb("c_gk", [128, RET_H * 512], BF16)
    c_maskT = cx.sb("c_maskT", [128, 128], BF16)
    c_gnw = cx.sb("c_gnw", [128, 4], F32)
    s3c = Stream(cx, "c3", nbuf=4)
    for i, (d3, s_) in enumerate([(c_cos, cos1), (c_sin, sin1), (c_gq, gq),
                                  (c_gk, gk), (c_maskT, maskT), (c_gnw, gnw)]):
        s3c.dma(i, d3[:], s_)
    for eng in ("vector", "scalar", "tensor", "sync"):
        s3c.drain(eng)
    consts = {"ones": ones[:], "identb": identb[:], "cos1": c_cos[:],
              "sin1": c_sin[:], "gq": c_gq[:], "gk": c_gk[:],
              "maskT": c_maskT[:], "gnw": c_gnw[:]}
    emit_l1_attn(cx, T, s2, wsl("q1w"), wsl("k1w"), wsl("v1w"), wsl("g1w"),
                 wsl("o1w"), s3, consts)
    cx.barrier()
    cx.pop_scope()

    cx.push_scope()
    b1c = cx.sb("b1c", [128, 32], F32)
    b2c = cx.sb("b2c", [128, 8], F32)
    s4c = Stream(cx, "c4", nbuf=2)
    s4c.dma(0, b1c[:], fb1)
    s4c.dma(1, b2c[:], fb2)
    for eng in ("vector", "scalar", "tensor", "sync"):
        s4c.drain(eng)
    emit_l1_ffn(cx, T, s3, wsl("fw1"), wsl("fw2"), b1c[:], b2c[:], yT,
                ones[:], cx.pb)
    cx.barrier()
    cx.pop_scope()

    with es, nc.Block() as block:
        cx.finish(block)
    return nc


def kernel(L0_attn_x, L0_attn_w_qkv, L0_attn_b_qkv, L0_attn_qn_w, L0_attn_kn_w,
           L0_attn_w_proj, L0_attn_b_proj, L0_ffn_Wqkv, L0_ffn_q_scale,
           L0_ffn_k_scale, L0_ffn_Wo, L0_ffn_ln_g, L0_ffn_ln_b, L0_ffn_W1,
           L0_ffn_b1, L0_ffn_W2, L0_ffn_b2, L1_attn_Wq, L1_attn_Wk, L1_attn_Wv,
           L1_attn_Wg, L1_attn_Wo, L1_attn_gn_weight, L1_ffn_w1, L1_ffn_b1,
           L1_ffn_w2, L1_ffn_b2):
    T = 4096
    B = 8
    f32 = lambda a: np.ascontiguousarray(np.asarray(a), dtype=np.float32)
    nc = _get("fused", lambda: build_fused(T))
    qs = f32(L0_ffn_q_scale); ks = f32(L0_ffn_k_scale)
    lng = f32(L0_ffn_ln_g); lnb = f32(L0_ffn_ln_b)
    b1 = f32(L0_ffn_b1); b2 = f32(L0_ffn_b2)
    l0f = _l0ffn_consts(T)
    l1a = _l1attn_consts(T)
    common = dict(
        wqkvT=_pre_slab(np.ascontiguousarray(f32(L0_attn_w_qkv).T)),
        wprojT=_pre_slab(np.ascontiguousarray(f32(L0_attn_w_proj).T)),
        bqkv=_layout_rows(f32(L0_attn_b_qkv)),
        bproj=_layout_rows(f32(L0_attn_b_proj)),
        qn=np.tile(f32(L0_attn_qn_w), 2)[:, None],
        kn=np.tile(f32(L0_attn_kn_w), 2)[:, None],
        Wqkv=np.concatenate([_pre_slab(f32(L0_ffn_Wqkv[i])) for i in range(LF_DEPTH)], 1),
        Wo=np.concatenate([_pre_slab(f32(L0_ffn_Wo[i])) for i in range(LF_DEPTH)], 1),
        W1=np.concatenate([_pre_w1_l0ffn(f32(L0_ffn_W1[i])) for i in range(LF_DEPTH)], 1),
        W2=np.concatenate([_pre_w2_l0ffn(f32(L0_ffn_W2[i])) for i in range(LF_DEPTH)], 1),
        qs6=np.stack([np.tile(qs[i], 2) for i in range(LF_DEPTH)], 1),
        ks6=np.stack([np.tile(ks[i], 2) for i in range(LF_DEPTH)], 1),
        lng6=np.concatenate([_layout_rows(lng[i]) for i in range(LF_DEPTH)], 1),
        lnb6=np.concatenate([_layout_rows(lnb[i]) for i in range(LF_DEPTH)], 1),
        b16=np.concatenate([_pack_b1(b1[i]) for i in range(LF_DEPTH)], 1),
        b26=np.concatenate([_layout_rows(b2[i]) for i in range(LF_DEPTH)], 1),
        csq=l0f["csq"], ssq=l0f["ssq"], csk=l0f["csk"], ssk=l0f["ssk"],
        maskA=l0f["maskA"], maskA0=l0f["maskA0"], maskB=l0f["maskB"],
        p64=l0f["p64"],
        q1w=_pre_slab(f32(L1_attn_Wq)), k1w=_pre_slab(f32(L1_attn_Wk)),
        v1w=_pre_slab(f32(L1_attn_Wv)), g1w=_pre_slab(f32(L1_attn_Wg)),
        o1w=_pre_slab(f32(L1_attn_Wo), cw=2048) if False else _pre_o1w(f32(L1_attn_Wo)),
        gnw=_layout_rows(f32(L1_attn_gn_weight)),
        cos1=l1a["cos1"], sin1=l1a["sin1"], gq=l1a["gq"], gk=l1a["gk"],
        maskT=l1a["maskT"],
        fw1=_pre_slab(f32(L1_ffn_w1)), fw2=_pre_w2_l1ffn(f32(L1_ffn_w2)),
        fb1=_layout_rows(f32(L1_ffn_b1)), fb2=_layout_rows(f32(L1_ffn_b2)),
        i2_in=l0f["i2_in"], i2t_in=l0f["i2t_in"], ident_in=l0f["ident_in"])
    in_maps = [dict(xT=np.ascontiguousarray(f32(L0_attn_x[b]).T), **common)
               for b in range(B)]
    res = run_bass_kernel_spmd(nc, in_maps, core_ids=list(range(B)))
    return np.stack([np.ascontiguousarray(res.results[b]["yT"].T)
                     for b in range(B)])
